# version 14
# baseline (speedup 1.0000x reference)
import numpy as np
import ml_dtypes
from contextlib import ExitStack

import concourse.bass as bass
import concourse.mybir as mybir
from concourse.bass_utils import run_bass_kernel_spmd

F32 = mybir.dt.float32
BF16 = mybir.dt.bfloat16
AF = mybir.ActivationFunctionType
ALU = mybir.AluOpType
AX = mybir.AxisListType

P = 128
DM = 1024
DI = 2048
NST = 16
DTR = 64
T = 2048
NE = 8
FF = 4096
NDC = DM // P
NIC = DI // P
TC = 256
NTCH = T // TC
TT = 512
NTT = T // TT
CAP = 768
CRT = 384
NCORES = 8

bfdt = ml_dtypes.bfloat16


class Sem:
    def __init__(self, es, nc, name):
        self.h = es.enter_context(nc.semaphore(name))
        self.c = 0


class Prog:
    def __init__(self, nc):
        self.nc = nc
        self.reset()

    def reset(self):
        self.ops = {k: [] for k in "tvspq"}

    def _add(self, k, fn):
        self.ops[k].append(fn)

    def t(self, fn): self._add("t", fn)
    def v(self, fn): self._add("v", fn)
    def s(self, fn): self._add("s", fn)
    def p(self, fn): self._add("p", fn)
    def q(self, fn): self._add("q", fn)
    def vd(self): self._add("v", lambda e: e.drain())
    def sd(self): self._add("s", lambda e: e.drain())
    def pd(self): self._add("p", lambda e: e.drain())

    def wait(self, k, sem, val=None):
        v = sem.c if val is None else val
        if v > 0:
            self._add(k, lambda e, _h=sem.h, _v=v: e.wait_ge(_h, _v))

    def run_block(self, name=None):
        ops = self.ops
        with self.nc.Block(name) as blk:
            if ops["q"]:
                blk.sync(lambda e, L=list(ops["q"]): [f(e) for f in L])
            if ops["t"]:
                blk.tensor(lambda e, L=list(ops["t"]): [f(e) for f in L])
            if ops["v"]:
                blk.vector(lambda e, L=list(ops["v"]): [f(e) for f in L])
            if ops["s"]:
                blk.scalar(lambda e, L=list(ops["s"]): [f(e) for f in L])
            if ops["p"]:
                blk.gpsimd(lambda e, L=list(ops["p"]): [f(e) for f in L])
        self.reset()


def build(debug=False, phases=3):
    nc = bass.Bass("TRN2", target_bir_lowering=False)
    es = ExitStack()

    EI, EO = "ExternalInput", "ExternalOutput"
    d = {}
    def din(name, shape, dt=F32):
        d[name] = nc.dram_tensor(name, list(shape), dt, kind=EI)
    def dint(name, shape, dt=F32):
        d[name] = nc.dram_tensor(name, list(shape), dt,
                                 kind=(EO if debug else "Internal"))

    din("xT", (DM, T))
    din("W_in", (DM, 2 * DI))
    din("W_x", (DI, DTR + 2 * NST))
    din("W_dt", (DTR, DI))
    din("W_out", (DI, DM))
    din("router_w", (DM, NE))
    din("wv", (NE, DM, FF)); din("wmat", (NE, DM, FF)); din("w1", (NE, FF, DM))
    din("svT", (NE, P, FF // P)); din("sT", (NE, P, FF // P))
    din("s1T", (NE, P, DM // P))
    din("gamma_c", (P, NDC)); din("beta_c", (P, NDC))
    din("b_in_c", (P, 2 * DI // P)); din("b_dt_c", (P, NIC))
    din("conv_b_c", (P, NIC)); din("D_c", (P, NIC)); din("b_out_c", (P, NDC))
    din("convw_c", (NIC, P, 4)); din("A_log_c", (NIC, P, NST))
    din("ones_sc", (P, P)); din("ident", (P, P))

    out_fm = nc.dram_tensor("out_fm", [DM, T], F32, kind=EO)

    dint("xnb_d", (DM, T), BF16)
    dint("xnb_tm_d", (T, DM), BF16)
    dint("xn2b_d", (DM, T), BF16)
    dint("gate_d", (NE, T))
    dint("bc_d", (2 * NST, TC))
    dint("oint_d", (2 * T,), mybir.dt.int32)
    dint("slotlist_d", (NE * CAP,), mybir.dt.int32)
    dint("pairs_d", (2 * T, DM))
    din("ident_bf", (P, P), BF16)
    din("sel8", (NE, 2))
    din("slot_ids", (P, 2 * T // P), mybir.dt.int32)
    din("oob_init", (NE * CAP,), mybir.dt.int32)

    def sbuf(stack, name, shape, dt=F32):
        return stack.enter_context(nc.sbuf_tensor(name, list(shape), dt))

    ones_sc = sbuf(es, "c_ones", (P, P))
    ident = sbuf(es, "c_id", (P, P))
    gamma = sbuf(es, "c_gamma", (P, NDC)); beta = sbuf(es, "c_beta", (P, NDC))
    b_in = sbuf(es, "c_bin", (P, 2 * DI // P))
    b_dt = sbuf(es, "c_bdt", (P, NIC)); conv_b = sbuf(es, "c_convb", (P, NIC))
    Dvec = sbuf(es, "c_D", (P, NIC)); b_out = sbuf(es, "c_bout", (P, NDC))
    convw = sbuf(es, "c_convw", (P, NIC * 4))
    A_col = sbuf(es, "c_Acol", (P, NIC * NST))
    wx_sb = sbuf(es, "c_wx", (P, NIC * 96), BF16)
    wdt_sb = sbuf(es, "c_wdt", (DTR, DI), BF16)
    rw_sb = sbuf(es, "c_rw", (P, NDC * NE))
    u_tails = sbuf(es, "c_utails", (P, NIC * 4))
    hstate = sbuf(es, "c_hstate", (P, NIC * NST))
    ccol = sbuf(es, "c_ccol", (P, 2))
    bsum = sbuf(es, "c_bsum", (P, NIC))
    g01_all = sbuf(es, "c_g01", (P, 32))
    ident_bf = sbuf(es, "c_idbf", (P, P), BF16)
    sel8 = sbuf(es, "c_sel8", (NE, 2))
    slotids = sbuf(es, "c_slotids", (P, 2 * T // P), mybir.dt.int32)

    psum = [es.enter_context(nc.psum_tensor(f"ps{i}", [P, 512], F32))
            for i in range(8)]

    ld = Sem(es, nc, "ld")
    ldp = Sem(es, nc, "ldp")
    lds = Sem(es, nc, "lds")
    st = Sem(es, nc, "st")
    lw = [Sem(es, nc, "lw0"), Sem(es, nc, "lw1")]
    lwp = Sem(es, nc, "lwp")
    pe = Sem(es, nc, "pe")
    dv = Sem(es, nc, "dv")
    sc = Sem(es, nc, "sc")
    pl = Sem(es, nc, "pl")

    pr = Prog(nc)
    bnd_tok = nc.gpsimd.alloc_register("bnd_tok")
    bnd_slot = nc.gpsimd.alloc_register("bnd_slot")
    bnd_list = nc.gpsimd.alloc_register("bnd_list")

    def qdma(dst, src, sem=ld, pin=False):
        pr.q(lambda e, a=dst, b=src, s=sem: e.dma_start(a, b).then_inc(s.h, 16))
        sem.c += 16
        if pin:
            pr.wait("q", sem)

    def pdma(dst, src, sem=ldp, accum=None, pin=False):
        if accum is None:
            pr.p(lambda e, a=dst, b=src, s=sem: e.dma_start(a, b)
                 .then_inc(s.h, 16))
        else:
            pr.p(lambda e, a=dst, b=src, s=sem, op=accum: e.dma_start(
                a, b, accum_op=op).then_inc(s.h, 16))
        sem.c += 16
        if pin:
            pr.wait("p", sem)

    for dst, src in [(ones_sc, d["ones_sc"]), (ident, d["ident"]),
                     (gamma, d["gamma_c"]), (beta, d["beta_c"]),
                     (b_in, d["b_in_c"]), (b_dt, d["b_dt_c"]),
                     (conv_b, d["conv_b_c"]), (Dvec, d["D_c"]),
                     (b_out, d["b_out_c"])]:
        qdma(dst[:, :], src[:, :])
    qdma(convw[:, :].rearrange("p (c f) -> p c f", c=NIC),
         d["convw_c"][:, :, :].transpose([1, 0, 2]))
    qdma(A_col[:, :].rearrange("p (c f) -> p c f", c=NIC),
         d["A_log_c"][:, :, :].transpose([1, 0, 2]))
    pdma(wx_sb[:, :].rearrange("p (c f) -> p c f", c=NIC),
         d["W_x"][:, :].rearrange("(c p) f -> p c f", p=P))
    pdma(wdt_sb[:, :], d["W_dt"][:, :])
    qdma(ident_bf[:, :], d["ident_bf"][:, :])
    qdma(sel8[:, :], d["sel8"][:, :])
    qdma(slotids[:, :], d["slot_ids"][:, :])
    qdma(rw_sb[:, :].rearrange("p (c f) -> p c f", c=NDC),
         d["router_w"][:, :].rearrange("(c p) f -> p c f", p=P), pin=True)
    qdma(d["slotlist_d"][:], d["oob_init"][:])
    pr.p(lambda e: e.reg_mov(bnd_tok, T - 1))
    pr.p(lambda e: e.reg_mov(bnd_slot, 2 * T - 1))
    pr.p(lambda e: e.reg_mov(bnd_list, NE * CAP - 1))
    pr.wait("s", ld)
    pr.s(lambda e: e.activation(A_col[:, :], A_col[:, :], AF.Exp)
         .then_inc(sc.h, 1)); sc.c += 1
    pr.wait("v", sc)
    pr.v(lambda e: e.tensor_scalar_mul(A_col[:, :], A_col[:, :], -1.0))
    pr.vd()
    pr.v(lambda e: e.memset(u_tails[:, :], 0.0))
    pr.v(lambda e: e.memset(hstate[:, :], 0.0))
    pr.v(lambda e: e.memset(ccol[:, 0:1], 1e-5))
    pr.v(lambda e: e.memset(ccol[:, 1:2], -0.5))
    pr.vd()
    for j in range(NIC):
        pr.v(lambda e, j=j: e.tensor_single_scalar(
            u_tails[:, j * 4:j * 4 + 3], u_tails[:, j * 4:j * 4 + 3],
            b_in[:, j:j + 1], ALU.subtract))
        pr.v(lambda e, j=j: e.reduce_sum(
            bsum[:, j:j + 1], convw[:, j * 4:(j + 1) * 4], axis=AX.X))
    pr.vd()
    for j in range(NIC):
        pr.v(lambda e, j=j: e.scalar_tensor_tensor(
            bsum[:, j:j + 1], bsum[:, j:j + 1], b_in[:, j:j + 1],
            conv_b[:, j:j + 1], ALU.mult, ALU.add))
    pr.vd()
    pr.run_block("consts")

    ohss = ExitStack()
    ohs_fm = sbuf(ohss, "c_ohs", (NE, 2 * T))
    lns = ExitStack()
    x_all = sbuf(lns, "x_all", (P, NDC * T))
    xn_all = sbuf(lns, "xn_all", (P, NDC * T))
    xsq = [sbuf(lns, f"xsq{i}", (P, T)) for i in range(2)]
    mean_bc = sbuf(lns, "mean_bc", (P, T))
    s2_bc = sbuf(lns, "s2_bc", (P, T))
    rstd_bc = s2_bc
    xtile_b = sbuf(lns, "xtile_b", (P, T), BF16)
    probs_t = sbuf(lns, "probs_t", (P, NE))
    row_t = sbuf(lns, "row_t", (P, 4))
    top_t = sbuf(lns, "top_t", (P, NE))
    mask_t = sbuf(lns, "mask_t", (P, NE))
    gate_t = sbuf(lns, "gate_t", (P, NE))
    gate_fm = sbuf(lns, "gate_fm", (NE, T))
    oh0_t = sbuf(lns, "oh0_t", (P, NE))
    oh1_t = sbuf(lns, "oh1_t", (P, NE))
    xtm_stage = sbuf(lns, "xtm_stage", (P, DM), BF16)
    xcast = sbuf(lns, "xcast", (P, DM), BF16)

    def xc(c): return x_all[:, c * T:(c + 1) * T]
    def xnc(c): return xn_all[:, c * T:(c + 1) * T]

    for c in range(NDC):
        qdma(xc(c), d["xT"][c * P:(c + 1) * P, :], pin=(c == NDC - 1))

    def ln_stats(src_of):
        sq_pe = [0, 0]
        for c in range(NDC):
            cp = c % 2
            if c >= 2:
                pr.wait("s", pe, sq_pe[cp])
            if c == 0:
                pr.wait("s", ld)
            pr.s(lambda e, c=c, cp=cp: e.activation(xsq[cp][:, :], src_of(c),
                                                    AF.Square)
                 .then_inc(sc.h, 1)); sc.c += 1
            pr.wait("t", sc)
            for tt4 in range(4):
                st_, sp_ = (c == 0), (c == NDC - 1)
                pr.t(lambda e, c=c, tt4=tt4, st_=st_, sp_=sp_: e.matmul(
                    psum[tt4][:, :], ones_sc[:, :],
                    src_of(c)[:, tt4 * 512:(tt4 + 1) * 512],
                    start=st_, stop=sp_))
                if tt4 < 3:
                    pr.t(lambda e, c=c, tt4=tt4, st_=st_, sp_=sp_, cp=cp: e.matmul(
                        psum[4 + tt4][:, :], ones_sc[:, :],
                        xsq[cp][:, tt4 * 512:(tt4 + 1) * 512],
                        start=st_, stop=sp_))
                else:
                    pr.t(lambda e, c=c, tt4=tt4, st_=st_, sp_=sp_, cp=cp: e.matmul(
                        psum[4 + tt4][:, :], ones_sc[:, :],
                        xsq[cp][:, tt4 * 512:(tt4 + 1) * 512],
                        start=st_, stop=sp_).then_inc(pe.h, 1))
            pe.c += 1
            sq_pe[cp] = pe.c
        pr.wait("v", pe)
        for tt4 in range(4):
            pr.v(lambda e, tt4=tt4: e.tensor_copy(
                mean_bc[:, tt4 * 512:(tt4 + 1) * 512], psum[tt4][:, :]))
            pr.v(lambda e, tt4=tt4: e.tensor_copy(
                s2_bc[:, tt4 * 512:(tt4 + 1) * 512], psum[4 + tt4][:, :]))
        pr.vd()
        pr.v(lambda e: e.tensor_tensor(xsq[0][:, :], mean_bc[:, :],
                                       mean_bc[:, :], ALU.mult))
        pr.vd()
        pr.v(lambda e: e.tensor_tensor(s2_bc[:, :], s2_bc[:, :], xsq[0][:, :],
                                       ALU.subtract).then_inc(dv.h, 1))
        dv.c += 1
        pr.wait("s", dv)
        pr.s(lambda e: e.activation(rstd_bc[:, :], s2_bc[:, :], AF.Ln,
                                    bias=ccol[:, 0:1]))
        pr.sd()
        pr.s(lambda e: e.activation(rstd_bc[:, :], rstd_bc[:, :], AF.Exp,
                                    scale=ccol[:, 1:2]).then_inc(sc.h, 1)); sc.c += 1

    ln_stats(xc)
    pr.wait("v", sc)
    for c in range(NDC):
        pr.v(lambda e, c=c: e.tensor_tensor(xnc(c), xc(c), mean_bc[:, :],
                                            ALU.subtract))
        pr.vd()
        if c < NDC - 1:
            pr.v(lambda e, c=c: e.tensor_tensor(xnc(c), xnc(c), rstd_bc[:, :],
                                                ALU.mult))
        else:
            pr.v(lambda e, c=c: e.tensor_tensor(xnc(c), xnc(c), rstd_bc[:, :],
                                                ALU.mult).then_inc(dv.h, 1))
        pr.vd()
    dv.c += 1
    ln1_done_dv = dv.c

    for c in range(NDC):
        pr.wait("v", st)
        pr.v(lambda e, c=c: e.tensor_copy(xtile_b[:, :], xnc(c))
             .then_inc(dv.h, 1)); dv.c += 1
        pr.vd()
        pr.wait("q", dv)
        qdma(d["xnb_d"][c * P:(c + 1) * P, :], xtile_b[:, :], st)

    pr.wait("t", dv, ln1_done_dv)
    for ti in range(16):
        for c in range(NDC):
            if ti == 15 and c == NDC - 1:
                pr.t(lambda e, ti=ti, c=c: e.matmul(
                    psum[0][:, ti * 8:(ti + 1) * 8],
                    xnc(c)[:, ti * P:(ti + 1) * P],
                    rw_sb[:, c * NE:(c + 1) * NE],
                    start=(c == 0), stop=(c == NDC - 1)).then_inc(pe.h, 1))
            else:
                pr.t(lambda e, ti=ti, c=c: e.matmul(
                    psum[0][:, ti * 8:(ti + 1) * 8],
                    xnc(c)[:, ti * P:(ti + 1) * P],
                    rw_sb[:, c * NE:(c + 1) * NE],
                    start=(c == 0), stop=(c == NDC - 1)))
    pe.c += 1
    logits_pe = pe.c

    for c in range(NDC):
        pr.v(lambda e, c=c: e.tensor_tensor(xc(c), xc(c), xnc(c), ALU.add)
             .then_inc(dv.h, 1)); dv.c += 1
        pr.vd()
        pr.wait("q", dv)
        qdma(out_fm[c * P:(c + 1) * P, :], xc(c), st)
    r0_store_st = st.c

    pr.wait("v", pe, logits_pe)
    pr.wait("s", pe, logits_pe)
    evac_dv_of_bank = {1: 0, 2: 0}
    for ti in range(16):
        lg = psum[0][:, ti * 8:(ti + 1) * 8]
        pr.v(lambda e, lg=lg: e.reduce_max(row_t[:, 0:1], lg, axis=AX.X,
                                           negate=True).then_inc(dv.h, 1))
        dv.c += 1
        pr.vd()
        pr.wait("s", dv)
        pr.s(lambda e, lg=lg: e.activation(probs_t[:, :], lg, AF.Exp,
                                           bias=row_t[:, 0:1])
             .then_inc(sc.h, 1)); sc.c += 1
        pr.wait("v", sc)
        pr.v(lambda e: e.reduce_sum(row_t[:, 1:2], probs_t[:, :], axis=AX.X))
        pr.vd()
        pr.v(lambda e: e.reciprocal(row_t[:, 2:3], row_t[:, 1:2]))
        pr.vd()
        pr.v(lambda e: e.tensor_scalar_mul(probs_t[:, :], probs_t[:, :],
                                           row_t[:, 2:3]))
        pr.vd()
        pr.v(lambda e: e.max(top_t[:, :], probs_t[:, :]))
        pr.vd()
        pr.v(lambda e: e.tensor_single_scalar(mask_t[:, :], probs_t[:, :],
                                              top_t[:, 1:2], ALU.is_ge))
        pr.v(lambda e, ti=ti: e.tensor_copy(g01_all[:, ti * 2:ti * 2 + 2],
                                            top_t[:, 0:2]))
        pr.v(lambda e: e.tensor_single_scalar(oh0_t[:, :], probs_t[:, :],
                                              top_t[:, 0:1], ALU.is_equal))
        pr.v(lambda e: e.tensor_single_scalar(oh1_t[:, :], probs_t[:, :],
                                              top_t[:, 1:2], ALU.is_equal))
        pr.vd()
        pr.v(lambda e: e.tensor_tensor(gate_t[:, :], probs_t[:, :],
                                       mask_t[:, :], ALU.mult)
             .then_inc(dv.h, 1)); dv.c += 1
        pr.vd()
        bank = 1 + (ti % 2)
        pr.wait("t", dv, evac_dv_of_bank[bank])
        pr.wait("t", dv)
        pr.t(lambda e, bank=bank: e.transpose(
            psum[bank][0:NE, 0:P], gate_t[:, :], ident[:, :]))
        pr.t(lambda e, bank=bank: e.transpose(
            psum[bank][0:NE, 128:256], oh0_t[:, :], ident[:, :]))
        pr.t(lambda e, bank=bank: e.transpose(
            psum[bank][0:NE, 256:384], oh1_t[:, :], ident[:, :])
            .then_inc(pe.h, 1)); pe.c += 1
        pr.wait("v", pe)
        pr.v(lambda e, ti=ti, bank=bank: e.tensor_copy(
            gate_fm[:, ti * P:(ti + 1) * P], psum[bank][0:NE, 0:P]))
        pr.v(lambda e, ti=ti, bank=bank: e.tensor_copy(
            ohs_fm[:, 2 * ti * P:2 * (ti + 1) * P]
            .rearrange("e (t two) -> e two t", two=2)[:, 0, :],
            psum[bank][0:NE, 128:256]))
        pr.v(lambda e, ti=ti, bank=bank: e.tensor_copy(
            ohs_fm[:, 2 * ti * P:2 * (ti + 1) * P]
            .rearrange("e (t two) -> e two t", two=2)[:, 1, :],
            psum[bank][0:NE, 256:384])
            .then_inc(dv.h, 1)); dv.c += 1
        pr.vd()
        evac_dv_of_bank[bank] = dv.c
    pr.wait("q", dv)
    qdma(d["gate_d"][:, :], gate_fm[:, :], st)

    xtm_banks = {5: 0, 6: 0}
    for ti in range(16):
        bank = 5 + (ti % 2)
        pr.wait("v", pe)
        for c in range(NDC):
            if c < NDC - 1:
                pr.v(lambda e, ti=ti, c=c: e.tensor_copy(
                    xcast[:, c * P:(c + 1) * P],
                    xnc(c)[:, ti * P:(ti + 1) * P]))
            else:
                pr.v(lambda e, ti=ti, c=c: e.tensor_copy(
                    xcast[:, c * P:(c + 1) * P],
                    xnc(c)[:, ti * P:(ti + 1) * P]).then_inc(dv.h, 1))
        dv.c += 1
        pr.vd()
        pr.wait("t", dv)
        pr.wait("t", dv, xtm_banks[bank])
        for c in range(NDC):
            if c < NDC - 1:
                pr.t(lambda e, c=c, bank=bank: e.transpose(
                    psum[bank][:, :].bitcast(BF16)[:, c * P:(c + 1) * P],
                    xcast[:, c * P:(c + 1) * P], ident_bf[:, :]))
            else:
                pr.t(lambda e, c=c, bank=bank: e.transpose(
                    psum[bank][:, :].bitcast(BF16)[:, c * P:(c + 1) * P],
                    xcast[:, c * P:(c + 1) * P], ident_bf[:, :])
                    .then_inc(pe.h, 1))
        pe.c += 1
        pr.wait("v", pe)
        pr.wait("v", st)
        pr.v(lambda e, bank=bank: e.tensor_copy(
            xtm_stage[:, :], psum[bank][:, :].bitcast(BF16))
            .then_inc(dv.h, 1)); dv.c += 1
        pr.vd()
        xtm_banks[bank] = dv.c
        pr.wait("q", dv)
        qdma(d["xnb_tm_d"][ti * P:(ti + 1) * P, :], xtm_stage[:, :], st,
             pin=(ti == 15))

    routing_done_dv = dv.c

    pr.wait("t", dv, routing_done_dv)
    ln_stats(xnc)
    pr.wait("v", sc)
    for c in range(NDC):
        pr.v(lambda e, c=c: e.tensor_tensor(xnc(c), xnc(c), mean_bc[:, :],
                                            ALU.subtract))
        pr.vd()
        pr.v(lambda e, c=c: e.tensor_tensor(xnc(c), xnc(c), rstd_bc[:, :],
                                            ALU.mult))
        pr.vd()
        pr.wait("v", st)
        pr.v(lambda e, c=c: e.tensor_scalar(
            xtile_b[:, :], xnc(c), gamma[:, c:c + 1], beta[:, c:c + 1],
            ALU.mult, ALU.add).then_inc(dv.h, 1)); dv.c += 1
        pr.vd()
        pr.wait("q", dv)
        qdma(d["xn2b_d"][c * P:(c + 1) * P, :], xtile_b[:, :], st)

    pr.wait("q", st)
    pr.run_block("ln_routing")
    lns.close()

    cps = ExitStack()
    pos_fm = sbuf(cps, "pos_fm", (NE, 2 * T))
    otile = sbuf(cps, "otile", (1, 512))
    ot_int = sbuf(cps, "ot_int", (1, 512), mybir.dt.int32)
    pr.wait("q", st); pr.wait("q", ld); pr.wait("q", ldp); pr.wait("q", lds)
    pr.v(lambda e: e.tensor_tensor_scan(pos_fm[:, :], ohs_fm[:, :],
                                        ohs_fm[:, :], 0.0, ALU.add, ALU.max))
    pr.vd()
    pr.v(lambda e: e.tensor_tensor(pos_fm[:, :], pos_fm[:, :], ohs_fm[:, :],
                                   ALU.mult).then_inc(dv.h, 1)); dv.c += 1
    pr.vd()
    pr.wait("t", dv)
    for t8 in range(8):
        if t8 > 0:
            pr.wait("t", dv)
        pr.t(lambda e, t8=t8: e.matmul(
            psum[0][0:1, 0:512], sel8[:, 0:1],
            pos_fm[:, t8 * 512:(t8 + 1) * 512], start=True, stop=False))
        pr.t(lambda e, t8=t8: e.matmul(
            psum[0][0:1, 0:512], sel8[:, 1:2],
            ohs_fm[:, t8 * 512:(t8 + 1) * 512], start=False, stop=True)
            .then_inc(pe.h, 1)); pe.c += 1
        pr.wait("v", pe)
        if t8 > 0:
            pr.wait("v", lds)
        pr.v(lambda e: e.tensor_scalar_add(otile[0:1, :], psum[0][0:1, 0:512],
                                           -1.0).then_inc(dv.h, 1)); dv.c += 1
        pr.vd()
        pr.v(lambda e: e.tensor_copy(ot_int[0:1, :], otile[0:1, :])
             .then_inc(dv.h, 1)); dv.c += 1
        pr.vd()
        pr.wait("s", dv)
        pr.s(lambda e, t8=t8: e.dma_start(
            d["oint_d"][t8 * 512:(t8 + 1) * 512].unsqueeze(0),
            ot_int[0:1, :]).then_inc(lds.h, 16)); lds.c += 16
        pr.wait("s", lds)
    pr.run_block("compact_pos")
    cps.close()
    ohss.close()

    if phases < 1:
        pr.wait("q", st); pr.wait("q", ld)
        pr.wait("p", ldp); pr.wait("s", lds)
        pr.run_block("final")
        es.close()
        return nc

    mms = ExitStack()
    wib = [sbuf(mms, f"wib{i}", (P, 8 * 512), BF16) for i in range(2)]
    u_all = sbuf(mms, "u_all", (P, NIC * TC))
    ub = sbuf(mms, "ub", (P, NIC * TC), BF16)
    sres = sbuf(mms, "sres", (P, NIC * TC))
    dtb = sbuf(mms, "dtb", (DTR, TC), BF16)
    BCbc = sbuf(mms, "BCbc", (P, 2 * NST * TC))
    Bbc = BCbc[:, 0:NST * TC]
    Cbc = BCbc[:, NST * TC:2 * NST * TC]
    h_all = sbuf(mms, "h_all", (P, NST * TC))
    mc = sbuf(mms, "mc", (P, NST * TC))
    NSLOT = 3
    a_sl = [sbuf(mms, f"a_sl{i}", (P, TC)) for i in range(NSLOT)]
    b_all = sbuf(mms, "b_all", (P, NST * TC))
    u_raw = sbuf(mms, "u_raw", (P, TC + 4))
    c_t = sbuf(mms, "c_t", (P, TC))
    e_t = [sbuf(mms, f"e_t{i}", (P, TC)) for i in range(2)]
    r_t = [sbuf(mms, f"r_t{i}", (P, TC)) for i in range(2)]
    xb_t = [sbuf(mms, f"xb_t{i}", (P, TC)) for i in range(2)]
    y_t = c_t
    du_t = [sbuf(mms, f"du_t{i}", (P, TC)) for i in range(2)]
    delta_t = [sbuf(mms, f"delta_t{i}", (P, TC)) for i in range(2)]
    rl_t = e_t
    g_all = sbuf(mms, "g_all", (P, NIC * TC), BF16)
    mout = sbuf(mms, "mout", (P, 4 * TC))
    wout_sb = sbuf(mms, "wout_sb", (P, NIC * DM), BF16)

    pr.wait("p", st); pr.wait("p", ld); pr.wait("p", lds)
    xn2t = mout[:, :].bitcast(BF16)
    bc_sb = mc[0:NST, 0:2 * TC]
    pdma(wout_sb[:, :].rearrange("p (c f) -> p c f", c=NIC),
         d["W_out"][:, :].rearrange("(c p) f -> p c f", p=P))
    wout_ldp = ldp.c

    bank_free = {0: ("sc", 0), 1: ("sc", 0)}
    wib_ldp = [0, 0]
    wout_evac_dv = 0
    prev_mamba_add_ldp = 0
    sems_by_name = {"sc": sc, "dv": dv, "pe": pe, "pl": pl}
    wout_mm_pe = [0]
    xn2b_store_st = st.c

    for tch in range(NTCH):
        t0 = tch * TC

        if tch == 0:
            pr.wait("q", st, xn2b_store_st)
        else:
            pr.wait("q", ldp, prev_mamba_add_ldp)
        qdma(xn2t.rearrange("p (c f) -> p c f", c=NDC),
             d["xn2b_d"][:, t0:t0 + TC].rearrange("(c p) f -> p c f", p=P),
             pin=True)
        xn2t_ld = ld.c

        for fb in range(8):
            buf = fb % 2
            pr.wait("p", pe)
            pdma(wib[buf][:, :].rearrange("p (c f) -> p c f", c=NDC),
                 d["W_in"][:, fb * 512:(fb + 1) * 512]
                 .rearrange("(c p) f -> p c f", p=P), pin=True)
            wib_ldp[buf] = ldp.c
            for fc in range(4):
                j = fb * 4 + fc
                bank = j % 2
                sname, sval = bank_free[bank]
                pr.wait("t", sems_by_name[sname], sval)
                pr.wait("t", ldp, wib_ldp[buf])
                if fb == 0 and fc == 0:
                    pr.wait("t", ld, xn2t_ld)
                for kc in range(NDC):
                    if kc < NDC - 1:
                        pr.t(lambda e, bank=bank, buf=buf, fc=fc, kc=kc: e.matmul(
                            psum[bank][:, 0:TC],
                            wib[buf][:, kc * 512 + fc * 128:kc * 512 + (fc + 1) * 128],
                            xn2t[:, kc * TC:(kc + 1) * TC],
                            start=(kc == 0), stop=False))
                    else:
                        pr.t(lambda e, bank=bank, buf=buf, fc=fc, kc=kc: e.matmul(
                            psum[bank][:, 0:TC],
                            wib[buf][:, kc * 512 + fc * 128:kc * 512 + (fc + 1) * 128],
                            xn2t[:, kc * TC:(kc + 1) * TC],
                            start=False, stop=True).then_inc(pe.h, 1))
                pe.c += 1
                if j < 16:
                    pr.wait("s", dv)
                    pr.wait("s", pe)
                    pr.s(lambda e, bank=bank: e.copy(
                        u_raw[:, 4:4 + TC], psum[bank][:, 0:TC])
                        .then_inc(sc.h, 1)); sc.c += 1
                    bank_free[bank] = ("sc", sc.c)
                    pr.wait("v", sc)
                    pr.v(lambda e, j=j: e.tensor_copy(
                        u_raw[:, 1:4], u_tails[:, j * 4:j * 4 + 3]))
                    pr.vd()
                    pr.v(lambda e, j=j: e.tensor_scalar(
                        c_t[:, :], u_raw[:, 1:1 + TC],
                        convw[:, j * 4 + 0:j * 4 + 1], bsum[:, j:j + 1],
                        ALU.mult, ALU.add))
                    pr.vd()
                    for tap in range(1, 4):
                        pr.v(lambda e, j=j, tap=tap: e.scalar_tensor_tensor(
                            c_t[:, :], u_raw[:, 1 + tap:1 + tap + TC],
                            convw[:, j * 4 + tap:j * 4 + tap + 1], c_t[:, :],
                            ALU.mult, ALU.add))
                        pr.vd()
                    pr.v(lambda e, j=j: e.tensor_copy(
                        u_tails[:, j * 4:j * 4 + 3],
                        u_raw[:, 1 + TC - 3:1 + TC]).then_inc(dv.h, 1))
                    dv.c += 1
                    pr.vd()
                    pr.wait("s", dv)
                    pr.s(lambda e, j=j: e.activation(
                        u_all[:, j * TC:(j + 1) * TC], c_t[:, :], AF.Silu))
                    pr.sd()
                    pr.s(lambda e, j=j: e.copy(
                        ub[:, j * TC:(j + 1) * TC],
                        u_all[:, j * TC:(j + 1) * TC]).then_inc(sc.h, 1))
                    sc.c += 1
                    pr.sd()
                else:
                    jr = j - 16
                    pr.wait("s", pe)
                    if fb == 4 and fc == 0:
                        pr.wait("s", dv)
                    pr.s(lambda e, bank=bank, j=j, jr=jr: e.activation(
                        sres[:, jr * TC:(jr + 1) * TC], psum[bank][:, 0:TC],
                        AF.Silu, bias=b_in[:, j:j + 1])
                        .then_inc(sc.h, 1)); sc.c += 1
                    pr.sd()
                    bank_free[bank] = ("sc", sc.c)
        s1_done_sc = sc.c

        pr.wait("t", sc, s1_done_sc)
        for kc in range(NIC):
            pr.t(lambda e, kc=kc: e.matmul(
                psum[2][0:DTR, 0:TC], wx_sb[:, kc * 96:kc * 96 + 64],
                ub[:, kc * TC:(kc + 1) * TC],
                start=(kc == 0), stop=(kc == NIC - 1)))
        for kc in range(NIC):
            pr.t(lambda e, kc=kc: e.matmul(
                psum[3][0:NST, 0:TC], wx_sb[:, kc * 96 + 64:kc * 96 + 80],
                ub[:, kc * TC:(kc + 1) * TC],
                start=(kc == 0), stop=(kc == NIC - 1)))
        for kc in range(NIC):
            if kc < NIC - 1:
                pr.t(lambda e, kc=kc: e.matmul(
                    psum[3][0:NST, 256:256 + TC],
                    wx_sb[:, kc * 96 + 80:kc * 96 + 96],
                    ub[:, kc * TC:(kc + 1) * TC],
                    start=(kc == 0), stop=False))
            else:
                pr.t(lambda e, kc=kc: e.matmul(
                    psum[3][0:NST, 256:256 + TC],
                    wx_sb[:, kc * 96 + 80:kc * 96 + 96],
                    ub[:, kc * TC:(kc + 1) * TC],
                    start=False, stop=True).then_inc(pe.h, 1))
        pe.c += 1
        pr.wait("s", pe)
        pr.wait("s", lds)
        pr.wait("s", dv)
        pr.s(lambda e: e.copy(dtb[:, :], psum[2][0:DTR, 0:TC]))
        pr.sd()
        pr.s(lambda e: e.copy(bc_sb[:, 0:TC], psum[3][0:NST, 0:TC]))
        pr.sd()
        pr.s(lambda e: e.copy(bc_sb[:, TC:2 * TC],
                              psum[3][0:NST, 256:256 + TC])
             .then_inc(sc.h, 1)); sc.c += 1
        pr.sd()
        dtb_sc = sc.c
        pr.wait("s", sc, dtb_sc)
        pr.s(lambda e: e.dma_start(
            d["bc_d"][:, :].rearrange("(two n) t -> n two t", two=2),
            bc_sb.rearrange("n (two t) -> n two t", two=2))
            .then_inc(lds.h, 16)); lds.c += 16
        pr.wait("s", lds)
        pr.wait("p", lds)
        pr.wait("p", dv)
        pr.wait("p", pl)
        pdma(BCbc[:, :].rearrange("p (n t) -> p n t", n=2 * NST),
             d["bc_d"][:, :].unsqueeze(0).broadcast_to([P, 2 * NST, TC]),
             pin=True)
        bc_ldp = ldp.c

        scan_dv_hist = []
        relu_sc = [0] * (NIC + 1)
        aprep_sc = [0] * (NIC + 1)
        du_pl_l = [0] * (NIC + 1)
        bprep_pl = [0] * (NIC + 1)
        nab_dv = [0] * (NIC + 1)
        mm_pe = [0] * (NIC + 1)

        def emit_A_pe(j):
            bank = j % 2
            sname, sval = bank_free[bank]
            pr.wait("t", sems_by_name[sname], sval)
            if j == 0:
                pr.wait("t", sc, dtb_sc)
            pr.t(lambda e, bank=bank, j=j: e.matmul(
                psum[bank][:, 0:TC], wdt_sb[:, j * 128:(j + 1) * 128],
                dtb[:, :], start=True, stop=True)
                .then_inc(pe.h, 1)); pe.c += 1
            mm_pe[j] = pe.c

        def emit_A_dve(j):
            par = j % 2
            pr.wait("v", pe, mm_pe[j])
            if j >= 2:
                pr.wait("v", sc, relu_sc[j - 2])
                pr.wait("v", pl, du_pl_l[j - 2])
            pr.v(lambda e, j=j, par=par: e.tensor_single_scalar(
                xb_t[par][:, :], psum[j % 2][:, 0:TC], b_dt[:, j:j + 1],
                ALU.add))
            pr.vd()
            pr.v(lambda e, par=par: e.scalar_tensor_tensor(
                e_t[par][:, :], xb_t[par][:, :], -1.0, xb_t[par][:, :],
                ALU.mult, ALU.min).then_inc(dv.h, 1)); dv.c += 1
            pr.vd()
            nab_dv[j] = dv.c
            bank_free[j % 2] = ("dv", dv.c)

        def emit_A_sc(j):
            par = j % 2
            pr.wait("s", dv, nab_dv[j])
            if j >= 2:
                pr.wait("s", pl, du_pl_l[j - 2])
            pr.s(lambda e, par=par: e.activation(r_t[par][:, :], e_t[par][:, :],
                                                 AF.Exp))
            pr.sd()
            pr.s(lambda e, par=par: e.activation(r_t[par][:, :], r_t[par][:, :],
                                                 AF.Ln, bias=1.0))
            pr.sd()
            pr.s(lambda e, par=par: e.activation(rl_t[par][:, :],
                                                 xb_t[par][:, :], AF.Relu)
                 .then_inc(sc.h, 1)); sc.c += 1
            relu_sc[j] = sc.c

        def emit_A_pl(j):
            par = j % 2
            pr.wait("p", sc, relu_sc[j])
            if j >= 2:
                pr.wait("p", sc, aprep_sc[j - 2])
            pr.p(lambda e, par=par: e.tensor_tensor(
                delta_t[par][:, :], r_t[par][:, :], rl_t[par][:, :], ALU.add))
            pr.pd()
            pr.p(lambda e, j=j, par=par: e.tensor_tensor(
                du_t[par][:, :], delta_t[par][:, :],
                u_all[:, j * TC:(j + 1) * TC], ALU.mult)
                .then_inc(pl.h, 1)); pl.c += 1
            du_pl_l[j] = pl.c

        def emit_B_pl(j):
            par = j % 2
            if j == 0:
                pr.wait("p", ldp, bc_ldp)
            else:
                pr.wait("p", dv, last_scan_dv[0])
            for n in range(NST):
                pr.p(lambda e, n=n, par=par: e.tensor_tensor(
                    b_all[:, n * TC:(n + 1) * TC], du_t[par][:, :],
                    Bbc[:, n * TC:(n + 1) * TC], ALU.mult)
                    .then_inc(pl.h, 1)); pl.c += 1
            bprep_pl[j] = pl.c

        def emit_B_sc(j):
            par = j % 2
            base = dv.c
            pr.wait("s", pl, du_pl_l[j])
            for n in range(NST):
                slot = n % NSLOT
                gi = j * NST + n
                if n >= NSLOT:
                    free_dv = base + (n - NSLOT + 1)
                elif gi >= NSLOT:
                    free_dv = scan_dv_hist[gi - NSLOT]
                else:
                    free_dv = 0
                if free_dv:
                    pr.wait("s", dv, free_dv)
                pr.s(lambda e, j=j, n=n, slot=slot, par=par: e.activation(
                    a_sl[slot][:, :], delta_t[par][:, :], AF.Exp,
                    scale=A_col[:, j * NST + n:j * NST + n + 1])
                    .then_inc(sc.h, 1)); sc.c += 1
            aprep_sc[j] = sc.c

        def emit_B_dve(j):
            for n in range(NST):
                slot = n % NSLOT
                pr.wait("v", sc, aprep_sc[j] - (NST - 1 - n))
                pr.wait("v", pl, bprep_pl[j] - (NST - 1 - n))
                pr.v(lambda e, j=j, n=n, slot=slot: e.tensor_tensor_scan(
                    h_all[:, n * TC:(n + 1) * TC], a_sl[slot][:, :],
                    b_all[:, n * TC:(n + 1) * TC],
                    hstate[:, j * NST + n:j * NST + n + 1],
                    ALU.mult, ALU.add)
                    .then_inc(dv.h, 1)); dv.c += 1
                scan_dv_hist.append(dv.c)
            last_scan_dv[0] = dv.c
            pr.vd()
            pr.v(lambda e, j=j: e.tensor_copy(
                hstate[:, j * NST:(j + 1) * NST],
                h_all[:, :].rearrange("p (n t) -> p n t", n=NST)[:, :, TC - 1:TC]))
            pr.vd()
            pr.v(lambda e: e.tensor_tensor(
                mc[:, :].rearrange("p (t n) -> p n t", n=NST),
                h_all[:, :].rearrange("p (n t) -> p n t", n=NST),
                Cbc.rearrange("p (n t) -> p n t", n=NST), ALU.mult))
            pr.vd()
            pr.v(lambda e: e.reduce_sum(
                y_t[:, :], mc[:, :].rearrange("p (t n) -> p t n", n=NST),
                axis=AX.X))
            pr.vd()
            pr.v(lambda e, j=j: e.scalar_tensor_tensor(
                y_t[:, :], u_all[:, j * TC:(j + 1) * TC], Dvec[:, j:j + 1],
                y_t[:, :], ALU.mult, ALU.add))
            pr.vd()
            pr.v(lambda e, j=j: e.tensor_tensor(
                y_t[:, :], y_t[:, :], sres[:, j * TC:(j + 1) * TC], ALU.mult)
                .then_inc(dv.h, 1)); dv.c += 1
            pr.wait("s", dv)
            if j == 0:
                pr.wait("s", pe, wout_mm_pe[0])
            pr.s(lambda e, j=j: e.copy(
                g_all[:, j * TC:(j + 1) * TC], y_t[:, :])
                 .then_inc(sc.h, 1)); sc.c += 1

        last_scan_dv = [0]
        emit_A_pe(0); emit_A_dve(0); emit_A_sc(0); emit_A_pl(0)
        for j in range(NIC):
            if j + 1 < NIC:
                emit_A_pe(j + 1)
                emit_A_dve(j + 1)
                emit_A_sc(j + 1)
                emit_A_pl(j + 1)
            emit_B_pl(j)
            emit_B_sc(j)
            emit_B_dve(j)
        g_done_sc = sc.c

        pr.wait("t", sc, g_done_sc)
        for half in range(2):
            if half == 1 or tch > 0:
                pr.wait("t", dv, wout_evac_dv)
            for dm4 in range(4):
                dm = half * 4 + dm4
                for j in range(NIC):
                    last = (dm4 == 3 and j == NIC - 1)
                    if last:
                        pr.t(lambda e, j=j, dm=dm, dm4=dm4: e.matmul(
                            psum[4 + dm4][:, 0:TC],
                            wout_sb[:, j * DM + dm * 128:j * DM + (dm + 1) * 128],
                            g_all[:, j * TC:(j + 1) * TC],
                            start=(j == 0), stop=(j == NIC - 1))
                            .then_inc(pe.h, 1))
                    else:
                        pr.t(lambda e, j=j, dm=dm, dm4=dm4: e.matmul(
                            psum[4 + dm4][:, 0:TC],
                            wout_sb[:, j * DM + dm * 128:j * DM + (dm + 1) * 128],
                            g_all[:, j * TC:(j + 1) * TC],
                            start=(j == 0), stop=(j == NIC - 1)))
            pe.c += 1
            wout_mm_pe[0] = pe.c
            pr.wait("v", pe)
            pr.wait("v", ldp, prev_mamba_add_ldp)
            for dm4 in range(4):
                dm = half * 4 + dm4
                if dm4 < 3:
                    pr.v(lambda e, dm=dm, dm4=dm4: e.tensor_scalar_add(
                        mout[:, dm4 * TC:(dm4 + 1) * TC], psum[4 + dm4][:, 0:TC],
                        b_out[:, dm:dm + 1]))
                else:
                    pr.v(lambda e, dm=dm, dm4=dm4: e.tensor_scalar_add(
                        mout[:, dm4 * TC:(dm4 + 1) * TC], psum[4 + dm4][:, 0:TC],
                        b_out[:, dm:dm + 1]).then_inc(dv.h, 1))
                pr.vd()
            dv.c += 1
            wout_evac_dv = dv.c
            pr.wait("p", dv, wout_evac_dv)
            if tch == 0 and half == 0:
                pr.wait("p", st, xn2b_store_st)
            pdma(out_fm[half * 512:(half + 1) * 512, t0:t0 + TC]
                 .rearrange("(c p) t -> p c t", p=P),
                 mout[:, :].rearrange("p (c t) -> p c t", c=4),
                 accum=ALU.add, pin=True)
            prev_mamba_add_ldp = ldp.c

    pr.run_block("mamba")
    mamba_adds_done = ldp.c
    mms.close()

    if phases < 2:
        pr.wait("q", st); pr.wait("q", ld)
        pr.wait("p", ldp); pr.wait("s", lds)
        pr.run_block("final")
        es.close()
        return nc

    from concourse.bass import IndirectOffsetOnAxis as IOOA
    mos = ExitStack()
    wbuf = sbuf(mos, "wbuf", (P, 32 * DM), BF16)
    h_e = sbuf(mos, "h_e", (P, 32 * CAP), BF16)
    Xg_tm = sbuf(mos, "Xg_tm", (P, 6 * DM), BF16)
    Xg_fm = [sbuf(mos, f"Xg_fm{i}", (P, 8 * CAP), BF16) for i in range(2)]
    oe_fm = sbuf(mos, "oe_fm", (P, NDC * CRT))
    oe_tm = [sbuf(mos, f"oe_tm{i}", (P, DM)) for i in range(2)]
    v_sb = sbuf(mos, "v_sb", (P, CRT))
    z_t2 = sbuf(mos, "z_t2", (P, CRT))
    t1_t = sbuf(mos, "t1_t", (P, CRT))
    sv_sb = sbuf(mos, "sv_sb", (P, FF // P))
    s_sb = sbuf(mos, "s_sb", (P, FF // P))
    s1_sb = sbuf(mos, "s1_sb", (P, DM // P))
    oidx_sb = sbuf(mos, "oidx_sb", (P, 2 * T // P), mybir.dt.int32)
    sl_sb = [sbuf(mos, f"sl_sb{i}", (P, CAP // P), mybir.dt.int32)
             for i in range(2)]
    row_sb = [sbuf(mos, f"row_sb{i}", (P, CAP // P), mybir.dt.int32)
              for i in range(2)]

    pr.wait("q", lds)
    pr.wait("q", st); pr.wait("q", ld); pr.wait("q", ldp)
    qdma(oidx_sb[:, :], d["oint_d"][:].rearrange("(p c) -> p c", p=P),
         pin=True)
    oidx_ld = ld.c
    pr.wait("p", ld, oidx_ld)
    pr.v(lambda e: e.memset(Xg_tm[:, :], 0.0).then_inc(dv.h, 1)); dv.c += 1
    for c32 in range(2 * T // P):
        pr.p(lambda e, c32=c32: e.indirect_dma_start(
            d["slotlist_d"][:].unsqueeze(1),
            IOOA(ap=oidx_sb[:, c32:c32 + 1], axis=0),
            slotids[:, c32:c32 + 1], None,
            bounds_check=bnd_list, oob_is_err=False)
            .then_inc(ldp.h, 16)); ldp.c += 16
    pr.wait("p", ldp)
    pr.run_block("compact")
    scatters_ldp = ldp.c

    m1_banks_free = {b: 0 for b in range(8)}
    xg_banks = {4: 0, 5: 0}
    m2_evac_dv = 0
    tr_evac_dv = 0
    moe_scat_ldp = 0
    oe_scat_of = [0, 0]
    m2_done_pe = 0

    xg_tp_pe = [0]
    prep_scat_guard = [0]

    def emit_prep(e_i):
        par = e_i % 2
        pr.wait("q", ldp)
        qdma(sl_sb[par][:, :],
             d["slotlist_d"][e_i * CAP:(e_i + 1) * CAP]
             .rearrange("(p rc) -> p rc", p=P), pin=True)
        sl_ld = ld.c
        pr.wait("v", ld, sl_ld)
        pr.wait("v", ldp)
        pr.v(lambda e, par=par: e.tensor_single_scalar(
            row_sb[par][:, :], sl_sb[par][:, :], 1, ALU.arith_shift_right)
            .then_inc(dv.h, 1)); dv.c += 1
        pr.vd()
        pr.wait("p", dv)
        pr.wait("p", pe, xg_tp_pe[0])
        for rc in range(6):
            pr.p(lambda e, rc=rc, par=par: e.indirect_dma_start(
                Xg_tm[:, rc * DM:(rc + 1) * DM], None,
                d["xnb_tm_d"][:, :], IOOA(ap=row_sb[par][:, rc:rc + 1], axis=0),
                bounds_check=bnd_tok, oob_is_err=False)
                .then_inc(ldp.h, 16)); ldp.c += 16
        pr.wait("p", ldp)
        gath_ldp = ldp.c
        for rc in range(6):
            bank = 4 + (rc % 2)
            pr.wait("t", dv, xg_banks[bank])
            if rc == 0:
                pr.wait("t", ldp, gath_ldp)
                pr.wait("t", dv)
            for kc in range(NDC):
                if kc < NDC - 1:
                    pr.t(lambda e, rc=rc, kc=kc, bank=bank: e.transpose(
                        psum[bank][:, :].bitcast(BF16)[:, kc * P:(kc + 1) * P],
                        Xg_tm[:, rc * DM + kc * P:rc * DM + (kc + 1) * P],
                        ident_bf[:, :]))
                else:
                    pr.t(lambda e, rc=rc, kc=kc, bank=bank: e.transpose(
                        psum[bank][:, :].bitcast(BF16)[:, kc * P:(kc + 1) * P],
                        Xg_tm[:, rc * DM + kc * P:rc * DM + (kc + 1) * P],
                        ident_bf[:, :]).then_inc(pe.h, 1))
            pe.c += 1
            pr.wait("v", pe)
            pr.v(lambda e, rc=rc, bank=bank, par=par: e.tensor_copy(
                Xg_fm[par][:, :].rearrange("p (kc r) -> p kc r", kc=NDC)
                [:, :, rc * P:(rc + 1) * P],
                psum[bank][:, :].bitcast(BF16)
                .rearrange("p (kc r) -> p kc r", kc=NDC))
                .then_inc(dv.h, 1)); dv.c += 1
            pr.vd()
            xg_banks[bank] = dv.c
        xg_tp_pe[0] = pe.c
        return dv.c

    xg_ready_dv = [0, 0]
    xg_ready_dv[0] = emit_prep(0)
    pr.run_block("prep0")

    for e_i in range(NE):
        par = e_i % 2
        xg_done_dv = xg_ready_dv[par]
        pr.wait("q", dv)
        qdma(sv_sb[:, :], d["svT"][e_i, :, :])
        qdma(s_sb[:, :], d["sT"][e_i, :, :])
        qdma(s1_sb[:, :], d["s1T"][e_i, :, :], pin=True)
        sv_ld = ld.c
        pr.wait("v", ld, sv_ld)
        half_mm_pe = [m2_done_pe, m2_done_pe]
        wv_ldp_half = [0, 0]
        for fb in range(4):
            hid = fb % 2
            half = hid * 16 * DM
            pr.wait("p", pe, half_mm_pe[hid])
            pdma(wbuf[:, half:half + 8 * DM]
                 .rearrange("p (c f) -> p c f", c=NDC),
                 d["wv"][e_i].rearrange("(c p) f -> p c f", p=P)
                 [:, :, fb * DM:(fb + 1) * DM], sem=lw[hid])
            pdma(wbuf[:, half + 8 * DM:half + 16 * DM]
                 .rearrange("p (c f) -> p c f", c=NDC),
                 d["wmat"][e_i].rearrange("(c p) f -> p c f", p=P)
                 [:, :, fb * DM:(fb + 1) * DM], sem=lw[hid])
            wv_ldp_half[hid] = lw[hid].c
            for fc in range(8):
                fcG = fb * 8 + fc
                for rt in range(2):
                    bv = (2 * fc + rt) % 4
                    by = 4 + (2 * fc + rt) % 4
                    pr.wait("t", lw[hid], wv_ldp_half[hid])
                    pr.wait("t", dv, max(m1_banks_free[bv], m1_banks_free[by],
                                         xg_done_dv))
                    for kc in range(NDC):
                        pr.t(lambda e, bv=bv, fc=fc, kc=kc, rt=rt, half=half: e.matmul(
                            psum[bv][:, 0:CRT],
                            wbuf[:, half + kc * DM + fc * 128:
                                 half + kc * DM + (fc + 1) * 128],
                            Xg_fm[par][:, kc * CAP + rt * CRT:
                                       kc * CAP + (rt + 1) * CRT],
                            start=(kc == 0), stop=(kc == NDC - 1)))
                    for kc in range(NDC):
                        if kc < NDC - 1:
                            pr.t(lambda e, by=by, fc=fc, kc=kc, rt=rt, half=half: e.matmul(
                                psum[by][:, 0:CRT],
                                wbuf[:, half + 8 * DM + kc * DM + fc * 128:
                                     half + 8 * DM + kc * DM + (fc + 1) * 128],
                                Xg_fm[par][:, kc * CAP + rt * CRT:
                                           kc * CAP + (rt + 1) * CRT],
                                start=(kc == 0), stop=False))
                        else:
                            pr.t(lambda e, by=by, fc=fc, kc=kc, rt=rt, half=half: e.matmul(
                                psum[by][:, 0:CRT],
                                wbuf[:, half + 8 * DM + kc * DM + fc * 128:
                                     half + 8 * DM + kc * DM + (fc + 1) * 128],
                                Xg_fm[par][:, kc * CAP + rt * CRT:
                                           kc * CAP + (rt + 1) * CRT],
                                start=False, stop=True).then_inc(pe.h, 1))
                    pe.c += 1
                    pr.wait("v", pe)
                    pr.v(lambda e, bv=bv, fcG=fcG: e.tensor_single_scalar(
                        v_sb[:, :], psum[bv][:, 0:CRT], sv_sb[:, fcG:fcG + 1],
                        ALU.mult))
                    pr.vd()
                    pr.v(lambda e, by=by, fcG=fcG: e.tensor_single_scalar(
                        z_t2[:, :], psum[by][:, 0:CRT], s_sb[:, fcG:fcG + 1],
                        ALU.mult).then_inc(dv.h, 1)); dv.c += 1
                    pr.vd()
                    m1_banks_free[bv] = dv.c
                    m1_banks_free[by] = dv.c
                    pr.wait("s", dv)
                    pr.s(lambda e: e.activation(t1_t[:, :], z_t2[:, :],
                                                AF.Gelu_apprx_tanh)
                         .then_inc(sc.h, 1)); sc.c += 1
                    pr.wait("v", sc)
                    pr.v(lambda e, fcG=fcG, rt=rt: e.tensor_tensor(
                        h_e[:, fcG * CAP + rt * CRT:fcG * CAP + (rt + 1) * CRT],
                        v_sb[:, :], t1_t[:, :], ALU.mult)
                        .then_inc(dv.h, 1)); dv.c += 1
                    pr.vd()
            half_mm_pe[hid] = pe.c
            if fb == 2:
                pr.wait("p", pe, half_mm_pe[0])
                pdma(wbuf[:, 0:16 * DM].rearrange("p (c f) -> p c f", c=16),
                     d["w1"][e_i][0:2048, :].rearrange("(c p) f -> p c f", p=P),
                     sem=lwp)
            if fb == 3:
                pr.wait("p", pe, half_mm_pe[1])
                pdma(wbuf[:, 16 * DM:32 * DM]
                     .rearrange("p (c f) -> p c f", c=16),
                     d["w1"][e_i][2048:4096, :].rearrange("(c p) f -> p c f", p=P),
                     sem=lwp)
        h_done_dv = dv.c
        w1_ldp = lwp.c

        if e_i + 1 < NE:
            xg_ready_dv[(e_i + 1) % 2] = emit_prep(e_i + 1)

        for rt in range(2):
            pr.wait("t", lwp, w1_ldp)
            pr.wait("t", dv, max(h_done_dv, m2_evac_dv, tr_evac_dv,
                                 xg_banks[4], xg_banks[5]))
            for dm in range(NDC):
                for kc in range(32):
                    last = (dm == NDC - 1 and kc == 31)
                    if last:
                        pr.t(lambda e, dm=dm, kc=kc, rt=rt: e.matmul(
                            psum[dm][:, 0:CRT],
                            wbuf[:, kc * DM + dm * 128:kc * DM + (dm + 1) * 128],
                            h_e[:, kc * CAP + rt * CRT:kc * CAP + (rt + 1) * CRT],
                            start=(kc == 0), stop=(kc == 31)).then_inc(pe.h, 1))
                    else:
                        pr.t(lambda e, dm=dm, kc=kc, rt=rt: e.matmul(
                            psum[dm][:, 0:CRT],
                            wbuf[:, kc * DM + dm * 128:kc * DM + (dm + 1) * 128],
                            h_e[:, kc * CAP + rt * CRT:kc * CAP + (rt + 1) * CRT],
                            start=(kc == 0), stop=(kc == 31)))
            pe.c += 1
            pr.wait("v", pe)
            for dm in range(NDC):
                if dm < NDC - 1:
                    pr.v(lambda e, dm=dm: e.tensor_single_scalar(
                        oe_fm[:, dm * CRT:(dm + 1) * CRT], psum[dm][:, 0:CRT],
                        s1_sb[:, dm:dm + 1], ALU.mult))
                else:
                    pr.v(lambda e, dm=dm: e.tensor_single_scalar(
                        oe_fm[:, dm * CRT:(dm + 1) * CRT], psum[dm][:, 0:CRT],
                        s1_sb[:, dm:dm + 1], ALU.mult).then_inc(dv.h, 1))
                pr.vd()
            dv.c += 1
            m2_evac_dv = dv.c
            for rsub in range(3):
                rc = rt * 3 + rsub
                opar = rc % 2
                pr.wait("t", dv, max(m2_evac_dv, tr_evac_dv))
                for dm in range(NDC):
                    bk = dm // 4
                    if dm < NDC - 1:
                        pr.t(lambda e, dm=dm, rsub=rsub, bk=bk: e.transpose(
                            psum[bk][:, (dm % 4) * 128:(dm % 4 + 1) * 128],
                            oe_fm[:, dm * CRT + rsub * 128:dm * CRT + (rsub + 1) * 128],
                            ident[:, :]))
                    else:
                        pr.t(lambda e, dm=dm, rsub=rsub, bk=bk: e.transpose(
                            psum[bk][:, (dm % 4) * 128:(dm % 4 + 1) * 128],
                            oe_fm[:, dm * CRT + rsub * 128:dm * CRT + (rsub + 1) * 128],
                            ident[:, :]).then_inc(pe.h, 1))
                pe.c += 1
                pr.wait("v", pe)
                pr.wait("v", ldp, oe_scat_of[opar])
                pr.v(lambda e, opar=opar: e.tensor_copy(
                    oe_tm[opar][:, 0:512], psum[0][:, :]))
                pr.vd()
                pr.v(lambda e, opar=opar: e.tensor_copy(
                    oe_tm[opar][:, 512:1024], psum[1][:, :])
                     .then_inc(dv.h, 1)); dv.c += 1
                pr.vd()
                tr_evac_dv = dv.c
                pr.wait("p", dv, tr_evac_dv)
                pr.p(lambda e, rc=rc, par=par, opar=opar: e.indirect_dma_start(
                    d["pairs_d"][:, :],
                    IOOA(ap=sl_sb[par][:, rc:rc + 1], axis=0),
                    oe_tm[opar][:, :], None,
                    bounds_check=bnd_slot, oob_is_err=False)
                    .then_inc(ldp.h, 16)); ldp.c += 16
                moe_scat_ldp = ldp.c
                oe_scat_of[opar] = ldp.c
        pr.wait("p", ldp)
        for b in range(8):
            m1_banks_free[b] = tr_evac_dv
        xg_banks[4] = tr_evac_dv
        xg_banks[5] = tr_evac_dv
        m2_done_pe = pe.c

        pr.run_block(f"moe{e_i}")
    moe_scats_done = ldp.c
    mos.close()

    fin = ExitStack()
    pr_tile = [sbuf(fin, f"pr_tile{i}", (P, 2 * DM)) for i in range(2)]
    res_t = [sbuf(fin, f"res_t{i}", (P, DM)) for i in range(2)]
    fstage = [sbuf(fin, f"fstage{i}", (P, DM)) for i in range(2)]
    fin_banks = {2: 0, 3: 0}
    tp_pe_of = [0, 0]
    fadd_of = [0, 0]
    pr.wait("q", ldp, moe_scats_done)
    pr.wait("q", st); pr.wait("q", ld)
    for ti in range(16):
        fp_ = ti % 2
        pr.wait("q", pe, tp_pe_of[fp_])
        qdma(pr_tile[fp_][:, :],
             d["pairs_d"][2 * ti * P:2 * (ti + 1) * P, :]
             .rearrange("(p two) m -> p two m", p=P), pin=True)
        pt_ld = ld.c
        pr.wait("v", ld, pt_ld)
        pr.v(lambda e, ti=ti, fp_=fp_: e.tensor_single_scalar(
            res_t[fp_][:, :], pr_tile[fp_][:, DM:2 * DM],
            g01_all[:, 2 * ti + 1:2 * ti + 2], ALU.mult))
        pr.vd()
        pr.v(lambda e, ti=ti, fp_=fp_: e.scalar_tensor_tensor(
            res_t[fp_][:, :], pr_tile[fp_][:, 0:DM],
            g01_all[:, 2 * ti:2 * ti + 1],
            res_t[fp_][:, :], ALU.mult, ALU.add).then_inc(dv.h, 1)); dv.c += 1
        pr.vd()
        bank = 2 + (ti % 2)
        pr.wait("t", dv)
        pr.wait("t", dv, fin_banks[bank])
        for c in range(NDC):
            bk2 = bank
            col = (c % 4) * 128
            if c == 3 or c == NDC - 1:
                pr.t(lambda e, c=c, col=col, bk2=bk2, fp_=fp_: e.transpose(
                    psum[bk2][:, col:col + 128] if (c // 4) == 0 else
                    psum[bk2 + 2][:, col:col + 128],
                    res_t[fp_][:, c * P:(c + 1) * P], ident[:, :])
                    .then_inc(pe.h, 1))
                pe.c += 1
            else:
                pr.t(lambda e, c=c, col=col, bk2=bk2, fp_=fp_: e.transpose(
                    psum[bk2][:, col:col + 128] if (c // 4) == 0 else
                    psum[bk2 + 2][:, col:col + 128],
                    res_t[fp_][:, c * P:(c + 1) * P], ident[:, :]))
        tp_pe_of[fp_] = pe.c
        pr.wait("v", pe)
        pr.wait("v", ldp, fadd_of[fp_])
        pr.v(lambda e, bank=bank, fp_=fp_: e.tensor_copy(
            fstage[fp_][:, 0:512], psum[bank][:, :]))
        pr.vd()
        pr.v(lambda e, bank=bank, fp_=fp_: e.tensor_copy(
            fstage[fp_][:, 512:1024], psum[bank + 2][:, :]).then_inc(dv.h, 1))
        dv.c += 1
        pr.vd()
        fin_banks[bank] = dv.c
        pr.wait("p", dv)
        if ti == 0:
            pr.wait("p", ldp, moe_scats_done)
        pdma(out_fm[:, ti * P:(ti + 1) * P].rearrange("(c p) t -> p c t", p=P),
             fstage[fp_][:, :].rearrange("p (c t) -> p c t", c=NDC),
             accum=ALU.add)
        fadd_of[fp_] = ldp.c
    pr.wait("p", ldp)
    pr.run_block("fincomb")
    fin.close()

    pr.wait("q", st)
    pr.wait("q", ld)
    pr.wait("p", ldp)
    pr.wait("s", lds)
    pr.run_block("final")

    mos.close()
    es.close()
    return nc


def kernel(**inputs):
    x = np.asarray(inputs["L0_attn_x"], np.float32)
    W_in = np.ascontiguousarray(inputs["L0_attn_W_in"], np.float32)
    W_x = np.ascontiguousarray(inputs["L0_attn_W_x"], np.float32)
    W_dt = np.ascontiguousarray(inputs["L0_attn_W_dt"], np.float32)
    W_out = np.ascontiguousarray(inputs["L0_attn_W_out"], np.float32)
    router_w = np.ascontiguousarray(inputs["L0_ffn_router_w"], np.float32)
    wv = np.ascontiguousarray(inputs["L0_ffn_wv"], np.float32)
    wmat = np.ascontiguousarray(inputs["L0_ffn_w"], np.float32)
    w1 = np.ascontiguousarray(inputs["L0_ffn_w1"], np.float32)

    def colmajor(vec, nch):
        return np.ascontiguousarray(
            np.asarray(vec, np.float32).reshape(nch, P).T)

    svT = np.ascontiguousarray(
        np.asarray(inputs["L0_ffn_sv"], np.float32).reshape(NE, FF // P, P)
        .transpose(0, 2, 1))
    sT = np.ascontiguousarray(
        np.asarray(inputs["L0_ffn_s"], np.float32).reshape(NE, FF // P, P)
        .transpose(0, 2, 1))
    s1T = np.ascontiguousarray(
        np.asarray(inputs["L0_ffn_s1"], np.float32).reshape(NE, DM // P, P)
        .transpose(0, 2, 1))

    common = {
        "W_in": W_in, "W_x": W_x, "W_dt": W_dt, "W_out": W_out,
        "router_w": router_w, "wv": wv, "wmat": wmat, "w1": w1,
        "svT": svT, "sT": sT, "s1T": s1T,
        "gamma_c": colmajor(inputs["L0_attn_ln_gamma"], NDC),
        "beta_c": colmajor(inputs["L0_attn_ln_beta"], NDC),
        "b_in_c": colmajor(inputs["L0_attn_b_in"], 2 * DI // P),
        "b_dt_c": colmajor(inputs["L0_attn_b_dt"], NIC),
        "conv_b_c": colmajor(inputs["L0_attn_conv_b"], NIC),
        "D_c": colmajor(inputs["L0_attn_D"], NIC),
        "b_out_c": colmajor(inputs["L0_attn_b_out"], NDC),
        "convw_c": np.ascontiguousarray(
            np.asarray(inputs["L0_attn_conv_w"], np.float32).T
            .reshape(NIC, P, 4)),
        "A_log_c": np.ascontiguousarray(
            np.asarray(inputs["L0_attn_A_log"], np.float32)
            .reshape(NIC, P, NST)),
        "ones_sc": np.full((P, P), 1.0 / DM, np.float32),
        "ident": np.eye(P, dtype=np.float32),
        "ident_bf": np.eye(P, dtype=ml_dtypes.bfloat16),
        "sel8": np.stack([np.ones(NE, np.float32),
                          np.arange(NE, dtype=np.float32) * CAP],
                         axis=1),
        "slot_ids": np.arange(2 * T, dtype=np.int32).reshape(P, 2 * T // P),
        "oob_init": np.full(NE * CAP, 1 << 20, np.int32),
    }

    in_maps = []
    for c in range(NCORES):
        m = dict(common)
        m["xT"] = np.ascontiguousarray(x[c].T)
        in_maps.append(m)

    nc = build(debug=False)
    res = run_bass_kernel_spmd(nc, in_maps, core_ids=list(range(NCORES)))
    out = np.stack([np.asarray(r["out_fm"]).T for r in res.results])
    return np.ascontiguousarray(out.astype(np.float32))


# version 15
# speedup vs baseline: 1.0784x; 1.0784x over previous
import numpy as np
import ml_dtypes
from contextlib import ExitStack

import concourse.bass as bass
import concourse.mybir as mybir
from concourse.bass_utils import run_bass_kernel_spmd

F32 = mybir.dt.float32
BF16 = mybir.dt.bfloat16
AF = mybir.ActivationFunctionType
ALU = mybir.AluOpType
AX = mybir.AxisListType

P = 128
DM = 1024
DI = 2048
NST = 16
DTR = 64
T = 2048
NE = 8
FF = 4096
NDC = DM // P
NIC = DI // P
TC = 256
NTCH = T // TC
TT = 512
NTT = T // TT
CAP = 768
CRT = 384
NCORES = 8

bfdt = ml_dtypes.bfloat16


class Sem:
    def __init__(self, es, nc, name):
        self.h = es.enter_context(nc.semaphore(name))
        self.c = 0


class Prog:
    def __init__(self, nc):
        self.nc = nc
        self.reset()

    def reset(self):
        self.ops = {k: [] for k in "tvspq"}

    def _add(self, k, fn):
        self.ops[k].append(fn)

    def t(self, fn): self._add("t", fn)
    def v(self, fn): self._add("v", fn)
    def s(self, fn): self._add("s", fn)
    def p(self, fn): self._add("p", fn)
    def q(self, fn): self._add("q", fn)
    def vd(self): self._add("v", lambda e: e.drain())
    def sd(self): self._add("s", lambda e: e.drain())
    def pd(self): self._add("p", lambda e: e.drain())

    def wait(self, k, sem, val=None):
        v = sem.c if val is None else val
        if v > 0:
            self._add(k, lambda e, _h=sem.h, _v=v: e.wait_ge(_h, _v))

    def run_block(self, name=None):
        ops = self.ops
        with self.nc.Block(name) as blk:
            if ops["q"]:
                blk.sync(lambda e, L=list(ops["q"]): [f(e) for f in L])
            if ops["t"]:
                blk.tensor(lambda e, L=list(ops["t"]): [f(e) for f in L])
            if ops["v"]:
                blk.vector(lambda e, L=list(ops["v"]): [f(e) for f in L])
            if ops["s"]:
                blk.scalar(lambda e, L=list(ops["s"]): [f(e) for f in L])
            if ops["p"]:
                blk.gpsimd(lambda e, L=list(ops["p"]): [f(e) for f in L])
        self.reset()


def build(debug=False, phases=3):
    nc = bass.Bass("TRN2", target_bir_lowering=False)
    es = ExitStack()

    EI, EO = "ExternalInput", "ExternalOutput"
    d = {}
    def din(name, shape, dt=F32):
        d[name] = nc.dram_tensor(name, list(shape), dt, kind=EI)
    def dint(name, shape, dt=F32):
        d[name] = nc.dram_tensor(name, list(shape), dt,
                                 kind=(EO if debug else "Internal"))

    din("xT", (DM, T))
    din("W_in", (DM, 2 * DI))
    din("W_x", (DI, DTR + 2 * NST))
    din("W_dt", (DTR, DI))
    din("W_out", (DI, DM))
    din("router_w", (DM, NE))
    din("wv", (NE, DM, FF)); din("wmat", (NE, DM, FF)); din("w1", (NE, FF, DM))
    din("svT", (NE, P, FF // P)); din("sT", (NE, P, FF // P))
    din("s1T", (NE, P, DM // P))
    din("gamma_c", (P, NDC)); din("beta_c", (P, NDC))
    din("b_in_c", (P, 2 * DI // P)); din("b_dt_c", (P, NIC))
    din("conv_b_c", (P, NIC)); din("D_c", (P, NIC)); din("b_out_c", (P, NDC))
    din("convw_c", (NIC, P, 4)); din("A_log_c", (NIC, P, NST))
    din("ones_sc", (P, P)); din("ident", (P, P))

    out_fm = nc.dram_tensor("out_fm", [DM, T], F32, kind=EO)

    dint("xnb_d", (DM, T), BF16)
    dint("xnb_tm_d", (T, DM), BF16)
    dint("xn2b_d", (DM, T), BF16)
    dint("gate_d", (NE, T))
    dint("bc_d", (2 * NST, TC))
    dint("oint_d", (2 * T,), mybir.dt.int32)
    dint("slotlist_d", (NE * CAP,), mybir.dt.int32)
    dint("pairs_d", (2 * T, DM))
    din("ident_bf", (P, P), BF16)
    din("sel8", (NE, 2))
    din("slot_ids", (P, 2 * T // P), mybir.dt.int32)
    din("oob_init", (NE * CAP,), mybir.dt.int32)

    def sbuf(stack, name, shape, dt=F32):
        return stack.enter_context(nc.sbuf_tensor(name, list(shape), dt))

    ones_sc = sbuf(es, "c_ones", (P, P))
    ident = sbuf(es, "c_id", (P, P))
    gamma = sbuf(es, "c_gamma", (P, NDC)); beta = sbuf(es, "c_beta", (P, NDC))
    b_in = sbuf(es, "c_bin", (P, 2 * DI // P))
    b_dt = sbuf(es, "c_bdt", (P, NIC)); conv_b = sbuf(es, "c_convb", (P, NIC))
    Dvec = sbuf(es, "c_D", (P, NIC)); b_out = sbuf(es, "c_bout", (P, NDC))
    convw = sbuf(es, "c_convw", (P, NIC * 4))
    A_col = sbuf(es, "c_Acol", (P, NIC * NST))
    wx_sb = sbuf(es, "c_wx", (P, NIC * 96), BF16)
    wdt_sb = sbuf(es, "c_wdt", (DTR, DI), BF16)
    rw_sb = sbuf(es, "c_rw", (P, NDC * NE))
    u_tails = sbuf(es, "c_utails", (P, NIC * 4))
    hstate = sbuf(es, "c_hstate", (P, NIC * NST))
    ccol = sbuf(es, "c_ccol", (P, 2))
    bsum = sbuf(es, "c_bsum", (P, NIC))
    g01_all = sbuf(es, "c_g01", (P, 32))
    ident_bf = sbuf(es, "c_idbf", (P, P), BF16)
    sel8 = sbuf(es, "c_sel8", (NE, 2))
    slotids = sbuf(es, "c_slotids", (P, 2 * T // P), mybir.dt.int32)

    psum = [es.enter_context(nc.psum_tensor(f"ps{i}", [P, 512], F32))
            for i in range(8)]

    ld = Sem(es, nc, "ld")
    ldp = Sem(es, nc, "ldp")
    lds = Sem(es, nc, "lds")
    st = Sem(es, nc, "st")
    lw = [Sem(es, nc, "lw0"), Sem(es, nc, "lw1")]
    lwp = Sem(es, nc, "lwp")
    pe = Sem(es, nc, "pe")
    dv = Sem(es, nc, "dv")
    sc = Sem(es, nc, "sc")
    pl = Sem(es, nc, "pl")

    pr = Prog(nc)
    bnd_tok = nc.gpsimd.alloc_register("bnd_tok")
    bnd_slot = nc.gpsimd.alloc_register("bnd_slot")
    bnd_list = nc.gpsimd.alloc_register("bnd_list")

    def qdma(dst, src, sem=ld, pin=False):
        pr.q(lambda e, a=dst, b=src, s=sem: e.dma_start(a, b).then_inc(s.h, 16))
        sem.c += 16
        if pin:
            pr.wait("q", sem)

    def pdma(dst, src, sem=ldp, accum=None, pin=False):
        if accum is None:
            pr.p(lambda e, a=dst, b=src, s=sem: e.dma_start(a, b)
                 .then_inc(s.h, 16))
        else:
            pr.p(lambda e, a=dst, b=src, s=sem, op=accum: e.dma_start(
                a, b, accum_op=op).then_inc(s.h, 16))
        sem.c += 16
        if pin:
            pr.wait("p", sem)

    for dst, src in [(ones_sc, d["ones_sc"]), (ident, d["ident"]),
                     (gamma, d["gamma_c"]), (beta, d["beta_c"]),
                     (b_in, d["b_in_c"]), (b_dt, d["b_dt_c"]),
                     (conv_b, d["conv_b_c"]), (Dvec, d["D_c"]),
                     (b_out, d["b_out_c"])]:
        qdma(dst[:, :], src[:, :])
    qdma(convw[:, :].rearrange("p (c f) -> p c f", c=NIC),
         d["convw_c"][:, :, :].transpose([1, 0, 2]))
    qdma(A_col[:, :].rearrange("p (c f) -> p c f", c=NIC),
         d["A_log_c"][:, :, :].transpose([1, 0, 2]))
    pdma(wx_sb[:, :].rearrange("p (c f) -> p c f", c=NIC),
         d["W_x"][:, :].rearrange("(c p) f -> p c f", p=P))
    pdma(wdt_sb[:, :], d["W_dt"][:, :])
    qdma(ident_bf[:, :], d["ident_bf"][:, :])
    qdma(sel8[:, :], d["sel8"][:, :])
    qdma(slotids[:, :], d["slot_ids"][:, :])
    qdma(rw_sb[:, :].rearrange("p (c f) -> p c f", c=NDC),
         d["router_w"][:, :].rearrange("(c p) f -> p c f", p=P), pin=True)
    qdma(d["slotlist_d"][:], d["oob_init"][:])
    pr.p(lambda e: e.reg_mov(bnd_tok, T - 1))
    pr.p(lambda e: e.reg_mov(bnd_slot, 2 * T - 1))
    pr.p(lambda e: e.reg_mov(bnd_list, NE * CAP - 1))
    pr.wait("s", ld)
    pr.s(lambda e: e.activation(A_col[:, :], A_col[:, :], AF.Exp)
         .then_inc(sc.h, 1)); sc.c += 1
    pr.wait("v", sc)
    pr.v(lambda e: e.tensor_scalar_mul(A_col[:, :], A_col[:, :], -1.0))
    pr.vd()
    pr.v(lambda e: e.memset(u_tails[:, :], 0.0))
    pr.v(lambda e: e.memset(hstate[:, :], 0.0))
    pr.v(lambda e: e.memset(ccol[:, 0:1], 1e-5))
    pr.v(lambda e: e.memset(ccol[:, 1:2], -0.5))
    pr.vd()
    for j in range(NIC):
        pr.v(lambda e, j=j: e.tensor_single_scalar(
            u_tails[:, j * 4:j * 4 + 3], u_tails[:, j * 4:j * 4 + 3],
            b_in[:, j:j + 1], ALU.subtract))
        pr.v(lambda e, j=j: e.reduce_sum(
            bsum[:, j:j + 1], convw[:, j * 4:(j + 1) * 4], axis=AX.X))
    pr.vd()
    for j in range(NIC):
        pr.v(lambda e, j=j: e.scalar_tensor_tensor(
            bsum[:, j:j + 1], bsum[:, j:j + 1], b_in[:, j:j + 1],
            conv_b[:, j:j + 1], ALU.mult, ALU.add))
    pr.vd()
    pr.run_block("consts")

    ohss = ExitStack()
    ohs_fm = sbuf(ohss, "c_ohs", (NE, 2 * T))
    lns = ExitStack()
    x_all = sbuf(lns, "x_all", (P, NDC * T))
    xn_all = sbuf(lns, "xn_all", (P, NDC * T))
    xsq = [sbuf(lns, f"xsq{i}", (P, T)) for i in range(2)]
    mean_bc = sbuf(lns, "mean_bc", (P, T))
    s2_bc = sbuf(lns, "s2_bc", (P, T))
    rstd_bc = s2_bc
    xtile_b = sbuf(lns, "xtile_b", (P, T), BF16)
    probs_t = sbuf(lns, "probs_t", (P, NE))
    row_t = sbuf(lns, "row_t", (P, 4))
    top_t = sbuf(lns, "top_t", (P, NE))
    mask_t = sbuf(lns, "mask_t", (P, NE))
    gate_t = sbuf(lns, "gate_t", (P, NE))
    gate_fm = sbuf(lns, "gate_fm", (NE, T))
    oh0_t = sbuf(lns, "oh0_t", (P, NE))
    oh1_t = sbuf(lns, "oh1_t", (P, NE))
    xtm_stage = sbuf(lns, "xtm_stage", (P, DM), BF16)
    xcast = sbuf(lns, "xcast", (P, DM), BF16)

    def xc(c): return x_all[:, c * T:(c + 1) * T]
    def xnc(c): return xn_all[:, c * T:(c + 1) * T]

    for c in range(NDC):
        qdma(xc(c), d["xT"][c * P:(c + 1) * P, :], pin=(c == NDC - 1))

    def ln_stats(src_of):
        sq_pe = [0, 0]
        for c in range(NDC):
            cp = c % 2
            if c >= 2:
                pr.wait("s", pe, sq_pe[cp])
            if c == 0:
                pr.wait("s", ld)
            pr.s(lambda e, c=c, cp=cp: e.activation(xsq[cp][:, :], src_of(c),
                                                    AF.Square)
                 .then_inc(sc.h, 1)); sc.c += 1
            pr.wait("t", sc)
            for tt4 in range(4):
                st_, sp_ = (c == 0), (c == NDC - 1)
                pr.t(lambda e, c=c, tt4=tt4, st_=st_, sp_=sp_: e.matmul(
                    psum[tt4][:, :], ones_sc[:, :],
                    src_of(c)[:, tt4 * 512:(tt4 + 1) * 512],
                    start=st_, stop=sp_))
                if tt4 < 3:
                    pr.t(lambda e, c=c, tt4=tt4, st_=st_, sp_=sp_, cp=cp: e.matmul(
                        psum[4 + tt4][:, :], ones_sc[:, :],
                        xsq[cp][:, tt4 * 512:(tt4 + 1) * 512],
                        start=st_, stop=sp_))
                else:
                    pr.t(lambda e, c=c, tt4=tt4, st_=st_, sp_=sp_, cp=cp: e.matmul(
                        psum[4 + tt4][:, :], ones_sc[:, :],
                        xsq[cp][:, tt4 * 512:(tt4 + 1) * 512],
                        start=st_, stop=sp_).then_inc(pe.h, 1))
            pe.c += 1
            sq_pe[cp] = pe.c
        pr.wait("v", pe)
        for tt4 in range(4):
            pr.v(lambda e, tt4=tt4: e.tensor_copy(
                mean_bc[:, tt4 * 512:(tt4 + 1) * 512], psum[tt4][:, :]))
            pr.v(lambda e, tt4=tt4: e.tensor_copy(
                s2_bc[:, tt4 * 512:(tt4 + 1) * 512], psum[4 + tt4][:, :]))
        pr.vd()
        pr.v(lambda e: e.tensor_tensor(xsq[0][:, :], mean_bc[:, :],
                                       mean_bc[:, :], ALU.mult))
        pr.vd()
        pr.v(lambda e: e.tensor_tensor(s2_bc[:, :], s2_bc[:, :], xsq[0][:, :],
                                       ALU.subtract).then_inc(dv.h, 1))
        dv.c += 1
        pr.wait("s", dv)
        pr.s(lambda e: e.activation(rstd_bc[:, :], s2_bc[:, :], AF.Ln,
                                    bias=ccol[:, 0:1]))
        pr.sd()
        pr.s(lambda e: e.activation(rstd_bc[:, :], rstd_bc[:, :], AF.Exp,
                                    scale=ccol[:, 1:2]).then_inc(sc.h, 1)); sc.c += 1

    ln_stats(xc)
    pr.wait("v", sc)
    for c in range(NDC):
        pr.v(lambda e, c=c: e.tensor_tensor(xnc(c), xc(c), mean_bc[:, :],
                                            ALU.subtract))
        pr.vd()
        if c < NDC - 1:
            pr.v(lambda e, c=c: e.tensor_tensor(xnc(c), xnc(c), rstd_bc[:, :],
                                                ALU.mult))
        else:
            pr.v(lambda e, c=c: e.tensor_tensor(xnc(c), xnc(c), rstd_bc[:, :],
                                                ALU.mult).then_inc(dv.h, 1))
        pr.vd()
    dv.c += 1
    ln1_done_dv = dv.c

    for c in range(NDC):
        pr.wait("v", st)
        pr.v(lambda e, c=c: e.tensor_copy(xtile_b[:, :], xnc(c))
             .then_inc(dv.h, 1)); dv.c += 1
        pr.vd()
        pr.wait("q", dv)
        qdma(d["xnb_d"][c * P:(c + 1) * P, :], xtile_b[:, :], st)

    pr.wait("t", dv, ln1_done_dv)
    for ti in range(16):
        for c in range(NDC):
            if ti == 15 and c == NDC - 1:
                pr.t(lambda e, ti=ti, c=c: e.matmul(
                    psum[0][:, ti * 8:(ti + 1) * 8],
                    xnc(c)[:, ti * P:(ti + 1) * P],
                    rw_sb[:, c * NE:(c + 1) * NE],
                    start=(c == 0), stop=(c == NDC - 1)).then_inc(pe.h, 1))
            else:
                pr.t(lambda e, ti=ti, c=c: e.matmul(
                    psum[0][:, ti * 8:(ti + 1) * 8],
                    xnc(c)[:, ti * P:(ti + 1) * P],
                    rw_sb[:, c * NE:(c + 1) * NE],
                    start=(c == 0), stop=(c == NDC - 1)))
    pe.c += 1
    logits_pe = pe.c

    for c in range(NDC):
        pr.v(lambda e, c=c: e.tensor_tensor(xc(c), xc(c), xnc(c), ALU.add)
             .then_inc(dv.h, 1)); dv.c += 1
        pr.vd()
        pr.wait("q", dv)
        qdma(out_fm[c * P:(c + 1) * P, :], xc(c), st)
    r0_store_st = st.c

    pr.wait("v", pe, logits_pe)
    pr.wait("s", pe, logits_pe)
    evac_dv_of_bank = {1: 0, 2: 0}
    for ti in range(16):
        lg = psum[0][:, ti * 8:(ti + 1) * 8]
        pr.v(lambda e, lg=lg: e.reduce_max(row_t[:, 0:1], lg, axis=AX.X,
                                           negate=True).then_inc(dv.h, 1))
        dv.c += 1
        pr.vd()
        pr.wait("s", dv)
        pr.s(lambda e, lg=lg: e.activation(probs_t[:, :], lg, AF.Exp,
                                           bias=row_t[:, 0:1])
             .then_inc(sc.h, 1)); sc.c += 1
        pr.wait("v", sc)
        pr.v(lambda e: e.reduce_sum(row_t[:, 1:2], probs_t[:, :], axis=AX.X))
        pr.vd()
        pr.v(lambda e: e.reciprocal(row_t[:, 2:3], row_t[:, 1:2]))
        pr.vd()
        pr.v(lambda e: e.tensor_scalar_mul(probs_t[:, :], probs_t[:, :],
                                           row_t[:, 2:3]))
        pr.vd()
        pr.v(lambda e: e.max(top_t[:, :], probs_t[:, :]))
        pr.vd()
        pr.v(lambda e: e.tensor_single_scalar(mask_t[:, :], probs_t[:, :],
                                              top_t[:, 1:2], ALU.is_ge))
        pr.v(lambda e, ti=ti: e.tensor_copy(g01_all[:, ti * 2:ti * 2 + 2],
                                            top_t[:, 0:2]))
        pr.v(lambda e: e.tensor_single_scalar(oh0_t[:, :], probs_t[:, :],
                                              top_t[:, 0:1], ALU.is_equal))
        pr.v(lambda e: e.tensor_single_scalar(oh1_t[:, :], probs_t[:, :],
                                              top_t[:, 1:2], ALU.is_equal))
        pr.vd()
        pr.v(lambda e: e.tensor_tensor(gate_t[:, :], probs_t[:, :],
                                       mask_t[:, :], ALU.mult)
             .then_inc(dv.h, 1)); dv.c += 1
        pr.vd()
        bank = 1 + (ti % 2)
        pr.wait("t", dv, evac_dv_of_bank[bank])
        pr.wait("t", dv)
        pr.t(lambda e, bank=bank: e.transpose(
            psum[bank][0:NE, 0:P], gate_t[:, :], ident[:, :]))
        pr.t(lambda e, bank=bank: e.transpose(
            psum[bank][0:NE, 128:256], oh0_t[:, :], ident[:, :]))
        pr.t(lambda e, bank=bank: e.transpose(
            psum[bank][0:NE, 256:384], oh1_t[:, :], ident[:, :])
            .then_inc(pe.h, 1)); pe.c += 1
        pr.wait("v", pe)
        pr.v(lambda e, ti=ti, bank=bank: e.tensor_copy(
            gate_fm[:, ti * P:(ti + 1) * P], psum[bank][0:NE, 0:P]))
        pr.v(lambda e, ti=ti, bank=bank: e.tensor_copy(
            ohs_fm[:, 2 * ti * P:2 * (ti + 1) * P]
            .rearrange("e (t two) -> e two t", two=2)[:, 0, :],
            psum[bank][0:NE, 128:256]))
        pr.v(lambda e, ti=ti, bank=bank: e.tensor_copy(
            ohs_fm[:, 2 * ti * P:2 * (ti + 1) * P]
            .rearrange("e (t two) -> e two t", two=2)[:, 1, :],
            psum[bank][0:NE, 256:384])
            .then_inc(dv.h, 1)); dv.c += 1
        pr.vd()
        evac_dv_of_bank[bank] = dv.c
    pr.wait("q", dv)
    qdma(d["gate_d"][:, :], gate_fm[:, :], st)

    xtm_banks = {5: 0, 6: 0}
    for ti in range(16):
        bank = 5 + (ti % 2)
        pr.wait("v", pe)
        for c in range(NDC):
            if c < NDC - 1:
                pr.v(lambda e, ti=ti, c=c: e.tensor_copy(
                    xcast[:, c * P:(c + 1) * P],
                    xnc(c)[:, ti * P:(ti + 1) * P]))
            else:
                pr.v(lambda e, ti=ti, c=c: e.tensor_copy(
                    xcast[:, c * P:(c + 1) * P],
                    xnc(c)[:, ti * P:(ti + 1) * P]).then_inc(dv.h, 1))
        dv.c += 1
        pr.vd()
        pr.wait("t", dv)
        pr.wait("t", dv, xtm_banks[bank])
        for c in range(NDC):
            if c < NDC - 1:
                pr.t(lambda e, c=c, bank=bank: e.transpose(
                    psum[bank][:, :].bitcast(BF16)[:, c * P:(c + 1) * P],
                    xcast[:, c * P:(c + 1) * P], ident_bf[:, :]))
            else:
                pr.t(lambda e, c=c, bank=bank: e.transpose(
                    psum[bank][:, :].bitcast(BF16)[:, c * P:(c + 1) * P],
                    xcast[:, c * P:(c + 1) * P], ident_bf[:, :])
                    .then_inc(pe.h, 1))
        pe.c += 1
        pr.wait("v", pe)
        pr.wait("v", st)
        pr.v(lambda e, bank=bank: e.tensor_copy(
            xtm_stage[:, :], psum[bank][:, :].bitcast(BF16))
            .then_inc(dv.h, 1)); dv.c += 1
        pr.vd()
        xtm_banks[bank] = dv.c
        pr.wait("q", dv)
        qdma(d["xnb_tm_d"][ti * P:(ti + 1) * P, :], xtm_stage[:, :], st,
             pin=(ti == 15))

    routing_done_dv = dv.c

    pr.wait("t", dv, routing_done_dv)
    ln_stats(xnc)
    pr.wait("v", sc)
    for c in range(NDC):
        pr.v(lambda e, c=c: e.tensor_tensor(xnc(c), xnc(c), mean_bc[:, :],
                                            ALU.subtract))
        pr.vd()
        pr.v(lambda e, c=c: e.tensor_tensor(xnc(c), xnc(c), rstd_bc[:, :],
                                            ALU.mult))
        pr.vd()
        pr.wait("v", st)
        pr.v(lambda e, c=c: e.tensor_scalar(
            xtile_b[:, :], xnc(c), gamma[:, c:c + 1], beta[:, c:c + 1],
            ALU.mult, ALU.add).then_inc(dv.h, 1)); dv.c += 1
        pr.vd()
        pr.wait("q", dv)
        qdma(d["xn2b_d"][c * P:(c + 1) * P, :], xtile_b[:, :], st)

    pr.wait("q", st)
    pr.run_block("ln_routing")
    lns.close()

    cps = ExitStack()
    pos_fm = sbuf(cps, "pos_fm", (NE, 2 * T))
    otile = sbuf(cps, "otile", (1, 512))
    ot_int = sbuf(cps, "ot_int", (1, 512), mybir.dt.int32)
    pr.wait("q", st); pr.wait("q", ld); pr.wait("q", ldp); pr.wait("q", lds)
    pr.v(lambda e: e.tensor_tensor_scan(pos_fm[:, :], ohs_fm[:, :],
                                        ohs_fm[:, :], 0.0, ALU.add, ALU.max))
    pr.vd()
    pr.v(lambda e: e.tensor_tensor(pos_fm[:, :], pos_fm[:, :], ohs_fm[:, :],
                                   ALU.mult).then_inc(dv.h, 1)); dv.c += 1
    pr.vd()
    pr.wait("t", dv)
    for t8 in range(8):
        if t8 > 0:
            pr.wait("t", dv)
        pr.t(lambda e, t8=t8: e.matmul(
            psum[0][0:1, 0:512], sel8[:, 0:1],
            pos_fm[:, t8 * 512:(t8 + 1) * 512], start=True, stop=False))
        pr.t(lambda e, t8=t8: e.matmul(
            psum[0][0:1, 0:512], sel8[:, 1:2],
            ohs_fm[:, t8 * 512:(t8 + 1) * 512], start=False, stop=True)
            .then_inc(pe.h, 1)); pe.c += 1
        pr.wait("v", pe)
        if t8 > 0:
            pr.wait("v", lds)
        pr.v(lambda e: e.tensor_scalar_add(otile[0:1, :], psum[0][0:1, 0:512],
                                           -1.0).then_inc(dv.h, 1)); dv.c += 1
        pr.vd()
        pr.v(lambda e: e.tensor_copy(ot_int[0:1, :], otile[0:1, :])
             .then_inc(dv.h, 1)); dv.c += 1
        pr.vd()
        pr.wait("s", dv)
        pr.s(lambda e, t8=t8: e.dma_start(
            d["oint_d"][t8 * 512:(t8 + 1) * 512].unsqueeze(0),
            ot_int[0:1, :]).then_inc(lds.h, 16)); lds.c += 16
        pr.wait("s", lds)
    pr.run_block("compact_pos")
    cps.close()
    ohss.close()

    if phases < 1:
        pr.wait("q", st); pr.wait("q", ld)
        pr.wait("p", ldp); pr.wait("s", lds)
        pr.run_block("final")
        es.close()
        return nc

    mms = ExitStack()
    wib = [sbuf(mms, f"wib{i}", (P, 8 * 512), BF16) for i in range(2)]
    u_all = sbuf(mms, "u_all", (P, NIC * TC))
    ub = sbuf(mms, "ub", (P, NIC * TC), BF16)
    sres = sbuf(mms, "sres", (P, NIC * TC))
    dtb = sbuf(mms, "dtb", (DTR, TC), BF16)
    BCbc = sbuf(mms, "BCbc", (P, 2 * NST * TC))
    Bbc = BCbc[:, 0:NST * TC]
    Cbc = BCbc[:, NST * TC:2 * NST * TC]
    h_all = sbuf(mms, "h_all", (P, NST * TC))
    mc = sbuf(mms, "mc", (P, NST * TC))
    NSLOT = 3
    a_sl = [sbuf(mms, f"a_sl{i}", (P, TC)) for i in range(NSLOT)]
    b_all = sbuf(mms, "b_all", (P, NST * TC))
    u_raw = sbuf(mms, "u_raw", (P, TC + 4))
    c_t = sbuf(mms, "c_t", (P, TC))
    e_t = [sbuf(mms, f"e_t{i}", (P, TC)) for i in range(2)]
    r_t = [sbuf(mms, f"r_t{i}", (P, TC)) for i in range(2)]
    xb_t = [sbuf(mms, f"xb_t{i}", (P, TC)) for i in range(2)]
    y_t = c_t
    du_t = [sbuf(mms, f"du_t{i}", (P, TC)) for i in range(2)]
    delta_t = [sbuf(mms, f"delta_t{i}", (P, TC)) for i in range(2)]
    rl_t = e_t
    g_all = sbuf(mms, "g_all", (P, NIC * TC), BF16)
    mout = sbuf(mms, "mout", (P, 4 * TC))
    wout_sb = sbuf(mms, "wout_sb", (P, NIC * DM), BF16)

    pr.wait("p", st); pr.wait("p", ld); pr.wait("p", lds)
    xn2t = mout[:, :].bitcast(BF16)
    bc_sb = mc[0:NST, 0:2 * TC]
    pdma(wout_sb[:, :].rearrange("p (c f) -> p c f", c=NIC),
         d["W_out"][:, :].rearrange("(c p) f -> p c f", p=P))
    wout_ldp = ldp.c

    bank_free = {0: ("sc", 0), 1: ("sc", 0)}
    wib_ldp = [0, 0]
    wout_evac_dv = 0
    prev_mamba_add_ldp = 0
    sems_by_name = {"sc": sc, "dv": dv, "pe": pe, "pl": pl}
    wout_mm_pe = [0]
    xn2b_store_st = st.c

    for tch in range(NTCH):
        t0 = tch * TC

        if tch == 0:
            pr.wait("q", st, xn2b_store_st)
        else:
            pr.wait("q", ldp, prev_mamba_add_ldp)
        qdma(xn2t.rearrange("p (c f) -> p c f", c=NDC),
             d["xn2b_d"][:, t0:t0 + TC].rearrange("(c p) f -> p c f", p=P),
             pin=True)
        xn2t_ld = ld.c

        for fb in range(8):
            buf = fb % 2
            pr.wait("p", pe)
            pdma(wib[buf][:, :].rearrange("p (c f) -> p c f", c=NDC),
                 d["W_in"][:, fb * 512:(fb + 1) * 512]
                 .rearrange("(c p) f -> p c f", p=P), pin=True)
            wib_ldp[buf] = ldp.c
            for fc in range(4):
                j = fb * 4 + fc
                bank = j % 2
                sname, sval = bank_free[bank]
                pr.wait("t", sems_by_name[sname], sval)
                pr.wait("t", ldp, wib_ldp[buf])
                if fb == 0 and fc == 0:
                    pr.wait("t", ld, xn2t_ld)
                for kc in range(NDC):
                    if kc < NDC - 1:
                        pr.t(lambda e, bank=bank, buf=buf, fc=fc, kc=kc: e.matmul(
                            psum[bank][:, 0:TC],
                            wib[buf][:, kc * 512 + fc * 128:kc * 512 + (fc + 1) * 128],
                            xn2t[:, kc * TC:(kc + 1) * TC],
                            start=(kc == 0), stop=False))
                    else:
                        pr.t(lambda e, bank=bank, buf=buf, fc=fc, kc=kc: e.matmul(
                            psum[bank][:, 0:TC],
                            wib[buf][:, kc * 512 + fc * 128:kc * 512 + (fc + 1) * 128],
                            xn2t[:, kc * TC:(kc + 1) * TC],
                            start=False, stop=True).then_inc(pe.h, 1))
                pe.c += 1
                if j < 16:
                    pr.wait("s", dv)
                    pr.wait("s", pe)
                    pr.s(lambda e, bank=bank: e.copy(
                        u_raw[:, 4:4 + TC], psum[bank][:, 0:TC])
                        .then_inc(sc.h, 1)); sc.c += 1
                    bank_free[bank] = ("sc", sc.c)
                    pr.wait("v", sc)
                    pr.v(lambda e, j=j: e.tensor_copy(
                        u_raw[:, 1:4], u_tails[:, j * 4:j * 4 + 3]))
                    pr.vd()
                    pr.v(lambda e, j=j: e.tensor_scalar(
                        c_t[:, :], u_raw[:, 1:1 + TC],
                        convw[:, j * 4 + 0:j * 4 + 1], bsum[:, j:j + 1],
                        ALU.mult, ALU.add))
                    pr.vd()
                    for tap in range(1, 4):
                        pr.v(lambda e, j=j, tap=tap: e.scalar_tensor_tensor(
                            c_t[:, :], u_raw[:, 1 + tap:1 + tap + TC],
                            convw[:, j * 4 + tap:j * 4 + tap + 1], c_t[:, :],
                            ALU.mult, ALU.add))
                        pr.vd()
                    pr.v(lambda e, j=j: e.tensor_copy(
                        u_tails[:, j * 4:j * 4 + 3],
                        u_raw[:, 1 + TC - 3:1 + TC]).then_inc(dv.h, 1))
                    dv.c += 1
                    pr.vd()
                    pr.wait("s", dv)
                    pr.s(lambda e, j=j: e.activation(
                        u_all[:, j * TC:(j + 1) * TC], c_t[:, :], AF.Silu))
                    pr.sd()
                    pr.s(lambda e, j=j: e.copy(
                        ub[:, j * TC:(j + 1) * TC],
                        u_all[:, j * TC:(j + 1) * TC]).then_inc(sc.h, 1))
                    sc.c += 1
                    pr.sd()
                else:
                    jr = j - 16
                    pr.wait("s", pe)
                    if fb == 4 and fc == 0:
                        pr.wait("s", dv)
                    pr.s(lambda e, bank=bank, j=j, jr=jr: e.activation(
                        sres[:, jr * TC:(jr + 1) * TC], psum[bank][:, 0:TC],
                        AF.Silu, bias=b_in[:, j:j + 1])
                        .then_inc(sc.h, 1)); sc.c += 1
                    pr.sd()
                    bank_free[bank] = ("sc", sc.c)
        s1_done_sc = sc.c

        pr.wait("t", sc, s1_done_sc)
        for kc in range(NIC):
            pr.t(lambda e, kc=kc: e.matmul(
                psum[2][0:DTR, 0:TC], wx_sb[:, kc * 96:kc * 96 + 64],
                ub[:, kc * TC:(kc + 1) * TC],
                start=(kc == 0), stop=(kc == NIC - 1)))
        for kc in range(NIC):
            pr.t(lambda e, kc=kc: e.matmul(
                psum[3][0:NST, 0:TC], wx_sb[:, kc * 96 + 64:kc * 96 + 80],
                ub[:, kc * TC:(kc + 1) * TC],
                start=(kc == 0), stop=(kc == NIC - 1)))
        for kc in range(NIC):
            if kc < NIC - 1:
                pr.t(lambda e, kc=kc: e.matmul(
                    psum[3][0:NST, 256:256 + TC],
                    wx_sb[:, kc * 96 + 80:kc * 96 + 96],
                    ub[:, kc * TC:(kc + 1) * TC],
                    start=(kc == 0), stop=False))
            else:
                pr.t(lambda e, kc=kc: e.matmul(
                    psum[3][0:NST, 256:256 + TC],
                    wx_sb[:, kc * 96 + 80:kc * 96 + 96],
                    ub[:, kc * TC:(kc + 1) * TC],
                    start=False, stop=True).then_inc(pe.h, 1))
        pe.c += 1
        pr.wait("s", pe)
        pr.wait("s", lds)
        pr.wait("s", dv)
        pr.s(lambda e: e.copy(dtb[:, :], psum[2][0:DTR, 0:TC]))
        pr.sd()
        pr.s(lambda e: e.copy(bc_sb[:, 0:TC], psum[3][0:NST, 0:TC]))
        pr.sd()
        pr.s(lambda e: e.copy(bc_sb[:, TC:2 * TC],
                              psum[3][0:NST, 256:256 + TC])
             .then_inc(sc.h, 1)); sc.c += 1
        pr.sd()
        dtb_sc = sc.c
        pr.wait("s", sc, dtb_sc)
        pr.s(lambda e: e.dma_start(
            d["bc_d"][:, :].rearrange("(two n) t -> n two t", two=2),
            bc_sb.rearrange("n (two t) -> n two t", two=2))
            .then_inc(lds.h, 16)); lds.c += 16
        pr.wait("s", lds)
        pr.wait("p", lds)
        pr.wait("p", dv)
        pr.wait("p", pl)
        pdma(BCbc[:, :].rearrange("p (n t) -> p n t", n=2 * NST),
             d["bc_d"][:, :].unsqueeze(0).broadcast_to([P, 2 * NST, TC]),
             pin=True)
        bc_ldp = ldp.c

        scan_dv_hist = []
        relu_sc = [0] * (NIC + 1)
        aprep_sc = [0] * (NIC + 1)
        du_pl_l = [0] * (NIC + 1)
        bprep_pl = [0] * (NIC + 1)
        nab_dv = [0] * (NIC + 1)
        mm_pe = [0] * (NIC + 1)

        def emit_A_pe(j):
            bank = j % 2
            sname, sval = bank_free[bank]
            pr.wait("t", sems_by_name[sname], sval)
            if j == 0:
                pr.wait("t", sc, dtb_sc)
            pr.t(lambda e, bank=bank, j=j: e.matmul(
                psum[bank][:, 0:TC], wdt_sb[:, j * 128:(j + 1) * 128],
                dtb[:, :], start=True, stop=True)
                .then_inc(pe.h, 1)); pe.c += 1
            mm_pe[j] = pe.c

        def emit_A_dve(j):
            par = j % 2
            pr.wait("v", pe, mm_pe[j])
            if j >= 2:
                pr.wait("v", sc, relu_sc[j - 2])
                pr.wait("v", pl, du_pl_l[j - 2])
            pr.v(lambda e, j=j, par=par: e.tensor_single_scalar(
                xb_t[par][:, :], psum[j % 2][:, 0:TC], b_dt[:, j:j + 1],
                ALU.add))
            pr.vd()
            pr.v(lambda e, par=par: e.scalar_tensor_tensor(
                e_t[par][:, :], xb_t[par][:, :], -1.0, xb_t[par][:, :],
                ALU.mult, ALU.min).then_inc(dv.h, 1)); dv.c += 1
            pr.vd()
            nab_dv[j] = dv.c
            bank_free[j % 2] = ("dv", dv.c)

        def emit_A_sc(j):
            par = j % 2
            pr.wait("s", dv, nab_dv[j])
            if j >= 2:
                pr.wait("s", pl, du_pl_l[j - 2])
            pr.s(lambda e, par=par: e.activation(r_t[par][:, :], e_t[par][:, :],
                                                 AF.Exp))
            pr.sd()
            pr.s(lambda e, par=par: e.activation(r_t[par][:, :], r_t[par][:, :],
                                                 AF.Ln, bias=1.0))
            pr.sd()
            pr.s(lambda e, par=par: e.activation(rl_t[par][:, :],
                                                 xb_t[par][:, :], AF.Relu)
                 .then_inc(sc.h, 1)); sc.c += 1
            relu_sc[j] = sc.c

        def emit_A_pl(j):
            par = j % 2
            pr.wait("p", sc, relu_sc[j])
            if j >= 2:
                pr.wait("p", sc, aprep_sc[j - 2])
            pr.p(lambda e, par=par: e.tensor_tensor(
                delta_t[par][:, :], r_t[par][:, :], rl_t[par][:, :], ALU.add))
            pr.pd()
            pr.p(lambda e, j=j, par=par: e.tensor_tensor(
                du_t[par][:, :], delta_t[par][:, :],
                u_all[:, j * TC:(j + 1) * TC], ALU.mult)
                .then_inc(pl.h, 1)); pl.c += 1
            du_pl_l[j] = pl.c

        def emit_B_pl(j):
            par = j % 2
            if j == 0:
                pr.wait("p", ldp, bc_ldp)
            else:
                pr.wait("p", dv, last_scan_dv[0])
            for n in range(NST):
                pr.p(lambda e, n=n, par=par: e.tensor_tensor(
                    b_all[:, n * TC:(n + 1) * TC], du_t[par][:, :],
                    Bbc[:, n * TC:(n + 1) * TC], ALU.mult)
                    .then_inc(pl.h, 1)); pl.c += 1
            bprep_pl[j] = pl.c

        def emit_B_sc(j):
            par = j % 2
            base = dv.c
            pr.wait("s", pl, du_pl_l[j])
            for n in range(NST):
                slot = n % NSLOT
                gi = j * NST + n
                if n >= NSLOT:
                    free_dv = base + (n - NSLOT + 1)
                elif gi >= NSLOT:
                    free_dv = scan_dv_hist[gi - NSLOT]
                else:
                    free_dv = 0
                if free_dv:
                    pr.wait("s", dv, free_dv)
                pr.s(lambda e, j=j, n=n, slot=slot, par=par: e.activation(
                    a_sl[slot][:, :], delta_t[par][:, :], AF.Exp,
                    scale=A_col[:, j * NST + n:j * NST + n + 1])
                    .then_inc(sc.h, 1)); sc.c += 1
            aprep_sc[j] = sc.c

        def emit_B_dve(j):
            for n in range(NST):
                slot = n % NSLOT
                pr.wait("v", sc, aprep_sc[j] - (NST - 1 - n))
                pr.wait("v", pl, bprep_pl[j] - (NST - 1 - n))
                pr.v(lambda e, j=j, n=n, slot=slot: e.tensor_tensor_scan(
                    h_all[:, n * TC:(n + 1) * TC], a_sl[slot][:, :],
                    b_all[:, n * TC:(n + 1) * TC],
                    hstate[:, j * NST + n:j * NST + n + 1],
                    ALU.mult, ALU.add)
                    .then_inc(dv.h, 1)); dv.c += 1
                scan_dv_hist.append(dv.c)
            last_scan_dv[0] = dv.c
            pr.vd()
            pr.v(lambda e, j=j: e.tensor_copy(
                hstate[:, j * NST:(j + 1) * NST],
                h_all[:, :].rearrange("p (n t) -> p n t", n=NST)[:, :, TC - 1:TC]))
            pr.vd()
            pr.v(lambda e: e.tensor_tensor(
                mc[:, :].rearrange("p (t n) -> p n t", n=NST),
                h_all[:, :].rearrange("p (n t) -> p n t", n=NST),
                Cbc.rearrange("p (n t) -> p n t", n=NST), ALU.mult))
            pr.vd()
            pr.v(lambda e: e.reduce_sum(
                y_t[:, :], mc[:, :].rearrange("p (t n) -> p t n", n=NST),
                axis=AX.X))
            pr.vd()
            pr.v(lambda e, j=j: e.scalar_tensor_tensor(
                y_t[:, :], u_all[:, j * TC:(j + 1) * TC], Dvec[:, j:j + 1],
                y_t[:, :], ALU.mult, ALU.add))
            pr.vd()
            pr.v(lambda e, j=j: e.tensor_tensor(
                y_t[:, :], y_t[:, :], sres[:, j * TC:(j + 1) * TC], ALU.mult)
                .then_inc(dv.h, 1)); dv.c += 1
            pr.wait("s", dv)
            if j == 0:
                pr.wait("s", pe, wout_mm_pe[0])
            pr.s(lambda e, j=j: e.copy(
                g_all[:, j * TC:(j + 1) * TC], y_t[:, :])
                 .then_inc(sc.h, 1)); sc.c += 1

        last_scan_dv = [0]
        emit_A_pe(0); emit_A_dve(0); emit_A_sc(0); emit_A_pl(0)
        for j in range(NIC):
            if j + 1 < NIC:
                emit_A_pe(j + 1)
                emit_A_dve(j + 1)
            emit_B_pl(j)
            emit_B_sc(j)
            if j + 1 < NIC:
                emit_A_sc(j + 1)
                emit_A_pl(j + 1)
            emit_B_dve(j)
        g_done_sc = sc.c

        pr.wait("t", sc, g_done_sc)
        for half in range(2):
            if half == 1 or tch > 0:
                pr.wait("t", dv, wout_evac_dv)
            for dm4 in range(4):
                dm = half * 4 + dm4
                for j in range(NIC):
                    last = (dm4 == 3 and j == NIC - 1)
                    if last:
                        pr.t(lambda e, j=j, dm=dm, dm4=dm4: e.matmul(
                            psum[4 + dm4][:, 0:TC],
                            wout_sb[:, j * DM + dm * 128:j * DM + (dm + 1) * 128],
                            g_all[:, j * TC:(j + 1) * TC],
                            start=(j == 0), stop=(j == NIC - 1))
                            .then_inc(pe.h, 1))
                    else:
                        pr.t(lambda e, j=j, dm=dm, dm4=dm4: e.matmul(
                            psum[4 + dm4][:, 0:TC],
                            wout_sb[:, j * DM + dm * 128:j * DM + (dm + 1) * 128],
                            g_all[:, j * TC:(j + 1) * TC],
                            start=(j == 0), stop=(j == NIC - 1)))
            pe.c += 1
            wout_mm_pe[0] = pe.c
            pr.wait("v", pe)
            pr.wait("v", ldp, prev_mamba_add_ldp)
            for dm4 in range(4):
                dm = half * 4 + dm4
                if dm4 < 3:
                    pr.v(lambda e, dm=dm, dm4=dm4: e.tensor_scalar_add(
                        mout[:, dm4 * TC:(dm4 + 1) * TC], psum[4 + dm4][:, 0:TC],
                        b_out[:, dm:dm + 1]))
                else:
                    pr.v(lambda e, dm=dm, dm4=dm4: e.tensor_scalar_add(
                        mout[:, dm4 * TC:(dm4 + 1) * TC], psum[4 + dm4][:, 0:TC],
                        b_out[:, dm:dm + 1]).then_inc(dv.h, 1))
                pr.vd()
            dv.c += 1
            wout_evac_dv = dv.c
            pr.wait("p", dv, wout_evac_dv)
            if tch == 0 and half == 0:
                pr.wait("p", st, xn2b_store_st)
            pdma(out_fm[half * 512:(half + 1) * 512, t0:t0 + TC]
                 .rearrange("(c p) t -> p c t", p=P),
                 mout[:, :].rearrange("p (c t) -> p c t", c=4),
                 accum=ALU.add, pin=True)
            prev_mamba_add_ldp = ldp.c

    pr.run_block("mamba")
    mamba_adds_done = ldp.c
    mms.close()

    if phases < 2:
        pr.wait("q", st); pr.wait("q", ld)
        pr.wait("p", ldp); pr.wait("s", lds)
        pr.run_block("final")
        es.close()
        return nc

    from concourse.bass import IndirectOffsetOnAxis as IOOA
    mos = ExitStack()
    wbuf = sbuf(mos, "wbuf", (P, 32 * DM), BF16)
    h_e = sbuf(mos, "h_e", (P, 32 * CAP), BF16)
    Xg_tm = sbuf(mos, "Xg_tm", (P, 6 * DM), BF16)
    Xg_fm = [sbuf(mos, f"Xg_fm{i}", (P, 8 * CAP), BF16) for i in range(2)]
    oe_fm = sbuf(mos, "oe_fm", (P, NDC * CRT))
    oe_tm = [sbuf(mos, f"oe_tm{i}", (P, DM)) for i in range(2)]
    v_sb = sbuf(mos, "v_sb", (P, CRT))
    z_t2 = sbuf(mos, "z_t2", (P, CRT))
    t1_t = sbuf(mos, "t1_t", (P, CRT))
    sv_sb = sbuf(mos, "sv_sb", (P, FF // P))
    s_sb = sbuf(mos, "s_sb", (P, FF // P))
    s1_sb = sbuf(mos, "s1_sb", (P, DM // P))
    oidx_sb = sbuf(mos, "oidx_sb", (P, 2 * T // P), mybir.dt.int32)
    sl_sb = [sbuf(mos, f"sl_sb{i}", (P, CAP // P), mybir.dt.int32)
             for i in range(2)]
    row_sb = [sbuf(mos, f"row_sb{i}", (P, CAP // P), mybir.dt.int32)
              for i in range(2)]

    pr.wait("q", lds)
    pr.wait("q", st); pr.wait("q", ld); pr.wait("q", ldp)
    qdma(oidx_sb[:, :], d["oint_d"][:].rearrange("(p c) -> p c", p=P),
         pin=True)
    oidx_ld = ld.c
    pr.wait("p", ld, oidx_ld)
    pr.v(lambda e: e.memset(Xg_tm[:, :], 0.0).then_inc(dv.h, 1)); dv.c += 1
    for c32 in range(2 * T // P):
        pr.p(lambda e, c32=c32: e.indirect_dma_start(
            d["slotlist_d"][:].unsqueeze(1),
            IOOA(ap=oidx_sb[:, c32:c32 + 1], axis=0),
            slotids[:, c32:c32 + 1], None,
            bounds_check=bnd_list, oob_is_err=False)
            .then_inc(ldp.h, 16)); ldp.c += 16
    pr.wait("p", ldp)
    pr.run_block("compact")
    scatters_ldp = ldp.c

    m1_banks_free = {b: 0 for b in range(8)}
    xg_banks = {4: 0, 5: 0}
    m2_evac_dv = 0
    tr_evac_dv = 0
    moe_scat_ldp = 0
    oe_scat_of = [0, 0]
    m2_done_pe = 0

    xg_tp_pe = [0]
    prep_scat_guard = [0]

    def emit_prep(e_i):
        par = e_i % 2
        pr.wait("q", ldp)
        qdma(sl_sb[par][:, :],
             d["slotlist_d"][e_i * CAP:(e_i + 1) * CAP]
             .rearrange("(p rc) -> p rc", p=P), pin=True)
        sl_ld = ld.c
        pr.wait("v", ld, sl_ld)
        pr.wait("v", ldp)
        pr.v(lambda e, par=par: e.tensor_single_scalar(
            row_sb[par][:, :], sl_sb[par][:, :], 1, ALU.arith_shift_right)
            .then_inc(dv.h, 1)); dv.c += 1
        pr.vd()
        pr.wait("p", dv)
        pr.wait("p", pe, xg_tp_pe[0])
        for rc in range(6):
            pr.p(lambda e, rc=rc, par=par: e.indirect_dma_start(
                Xg_tm[:, rc * DM:(rc + 1) * DM], None,
                d["xnb_tm_d"][:, :], IOOA(ap=row_sb[par][:, rc:rc + 1], axis=0),
                bounds_check=bnd_tok, oob_is_err=False)
                .then_inc(ldp.h, 16)); ldp.c += 16
        pr.wait("p", ldp)
        gath_ldp = ldp.c
        for rc in range(6):
            bank = 4 + (rc % 2)
            pr.wait("t", dv, xg_banks[bank])
            if rc == 0:
                pr.wait("t", ldp, gath_ldp)
                pr.wait("t", dv)
            for kc in range(NDC):
                if kc < NDC - 1:
                    pr.t(lambda e, rc=rc, kc=kc, bank=bank: e.transpose(
                        psum[bank][:, :].bitcast(BF16)[:, kc * P:(kc + 1) * P],
                        Xg_tm[:, rc * DM + kc * P:rc * DM + (kc + 1) * P],
                        ident_bf[:, :]))
                else:
                    pr.t(lambda e, rc=rc, kc=kc, bank=bank: e.transpose(
                        psum[bank][:, :].bitcast(BF16)[:, kc * P:(kc + 1) * P],
                        Xg_tm[:, rc * DM + kc * P:rc * DM + (kc + 1) * P],
                        ident_bf[:, :]).then_inc(pe.h, 1))
            pe.c += 1
            pr.wait("v", pe)
            pr.v(lambda e, rc=rc, bank=bank, par=par: e.tensor_copy(
                Xg_fm[par][:, :].rearrange("p (kc r) -> p kc r", kc=NDC)
                [:, :, rc * P:(rc + 1) * P],
                psum[bank][:, :].bitcast(BF16)
                .rearrange("p (kc r) -> p kc r", kc=NDC))
                .then_inc(dv.h, 1)); dv.c += 1
            pr.vd()
            xg_banks[bank] = dv.c
        xg_tp_pe[0] = pe.c
        return dv.c

    xg_ready_dv = [0, 0]
    xg_ready_dv[0] = emit_prep(0)
    pr.run_block("prep0")

    for e_i in range(NE):
        par = e_i % 2
        xg_done_dv = xg_ready_dv[par]
        pr.wait("q", dv)
        qdma(sv_sb[:, :], d["svT"][e_i, :, :])
        qdma(s_sb[:, :], d["sT"][e_i, :, :])
        qdma(s1_sb[:, :], d["s1T"][e_i, :, :], pin=True)
        sv_ld = ld.c
        pr.wait("v", ld, sv_ld)
        half_mm_pe = [m2_done_pe, m2_done_pe]
        wv_ldp_half = [0, 0]
        for fb in range(4):
            hid = fb % 2
            half = hid * 16 * DM
            pr.wait("p", pe, half_mm_pe[hid])
            pdma(wbuf[:, half:half + 8 * DM]
                 .rearrange("p (c f) -> p c f", c=NDC),
                 d["wv"][e_i].rearrange("(c p) f -> p c f", p=P)
                 [:, :, fb * DM:(fb + 1) * DM], sem=lw[hid])
            pdma(wbuf[:, half + 8 * DM:half + 16 * DM]
                 .rearrange("p (c f) -> p c f", c=NDC),
                 d["wmat"][e_i].rearrange("(c p) f -> p c f", p=P)
                 [:, :, fb * DM:(fb + 1) * DM], sem=lw[hid])
            wv_ldp_half[hid] = lw[hid].c
            for fc in range(8):
                fcG = fb * 8 + fc
                for rt in range(2):
                    bv = (2 * fc + rt) % 4
                    by = 4 + (2 * fc + rt) % 4
                    pr.wait("t", lw[hid], wv_ldp_half[hid])
                    pr.wait("t", dv, max(m1_banks_free[bv], m1_banks_free[by],
                                         xg_done_dv))
                    for kc in range(NDC):
                        pr.t(lambda e, bv=bv, fc=fc, kc=kc, rt=rt, half=half: e.matmul(
                            psum[bv][:, 0:CRT],
                            wbuf[:, half + kc * DM + fc * 128:
                                 half + kc * DM + (fc + 1) * 128],
                            Xg_fm[par][:, kc * CAP + rt * CRT:
                                       kc * CAP + (rt + 1) * CRT],
                            start=(kc == 0), stop=(kc == NDC - 1)))
                    for kc in range(NDC):
                        if kc < NDC - 1:
                            pr.t(lambda e, by=by, fc=fc, kc=kc, rt=rt, half=half: e.matmul(
                                psum[by][:, 0:CRT],
                                wbuf[:, half + 8 * DM + kc * DM + fc * 128:
                                     half + 8 * DM + kc * DM + (fc + 1) * 128],
                                Xg_fm[par][:, kc * CAP + rt * CRT:
                                           kc * CAP + (rt + 1) * CRT],
                                start=(kc == 0), stop=False))
                        else:
                            pr.t(lambda e, by=by, fc=fc, kc=kc, rt=rt, half=half: e.matmul(
                                psum[by][:, 0:CRT],
                                wbuf[:, half + 8 * DM + kc * DM + fc * 128:
                                     half + 8 * DM + kc * DM + (fc + 1) * 128],
                                Xg_fm[par][:, kc * CAP + rt * CRT:
                                           kc * CAP + (rt + 1) * CRT],
                                start=False, stop=True).then_inc(pe.h, 1))
                    pe.c += 1
                    pr.wait("v", pe)
                    pr.v(lambda e, bv=bv, fcG=fcG: e.tensor_single_scalar(
                        v_sb[:, :], psum[bv][:, 0:CRT], sv_sb[:, fcG:fcG + 1],
                        ALU.mult))
                    pr.vd()
                    pr.v(lambda e, by=by, fcG=fcG: e.tensor_single_scalar(
                        z_t2[:, :], psum[by][:, 0:CRT], s_sb[:, fcG:fcG + 1],
                        ALU.mult).then_inc(dv.h, 1)); dv.c += 1
                    pr.vd()
                    m1_banks_free[bv] = dv.c
                    m1_banks_free[by] = dv.c
                    pr.wait("s", dv)
                    pr.s(lambda e: e.activation(t1_t[:, :], z_t2[:, :],
                                                AF.Gelu_apprx_tanh)
                         .then_inc(sc.h, 1)); sc.c += 1
                    pr.wait("v", sc)
                    pr.v(lambda e, fcG=fcG, rt=rt: e.tensor_tensor(
                        h_e[:, fcG * CAP + rt * CRT:fcG * CAP + (rt + 1) * CRT],
                        v_sb[:, :], t1_t[:, :], ALU.mult)
                        .then_inc(dv.h, 1)); dv.c += 1
                    pr.vd()
            half_mm_pe[hid] = pe.c
            if fb == 2:
                pr.wait("p", pe, half_mm_pe[0])
                pdma(wbuf[:, 0:16 * DM].rearrange("p (c f) -> p c f", c=16),
                     d["w1"][e_i][0:2048, :].rearrange("(c p) f -> p c f", p=P),
                     sem=lwp)
            if fb == 3:
                pr.wait("p", pe, half_mm_pe[1])
                pdma(wbuf[:, 16 * DM:32 * DM]
                     .rearrange("p (c f) -> p c f", c=16),
                     d["w1"][e_i][2048:4096, :].rearrange("(c p) f -> p c f", p=P),
                     sem=lwp)
        h_done_dv = dv.c
        w1_ldp = lwp.c

        if e_i + 1 < NE:
            xg_ready_dv[(e_i + 1) % 2] = emit_prep(e_i + 1)

        for rt in range(2):
            pr.wait("t", lwp, w1_ldp)
            pr.wait("t", dv, max(h_done_dv, m2_evac_dv, tr_evac_dv,
                                 xg_banks[4], xg_banks[5]))
            for dm in range(NDC):
                for kc in range(32):
                    last = (dm == NDC - 1 and kc == 31)
                    if last:
                        pr.t(lambda e, dm=dm, kc=kc, rt=rt: e.matmul(
                            psum[dm][:, 0:CRT],
                            wbuf[:, kc * DM + dm * 128:kc * DM + (dm + 1) * 128],
                            h_e[:, kc * CAP + rt * CRT:kc * CAP + (rt + 1) * CRT],
                            start=(kc == 0), stop=(kc == 31)).then_inc(pe.h, 1))
                    else:
                        pr.t(lambda e, dm=dm, kc=kc, rt=rt: e.matmul(
                            psum[dm][:, 0:CRT],
                            wbuf[:, kc * DM + dm * 128:kc * DM + (dm + 1) * 128],
                            h_e[:, kc * CAP + rt * CRT:kc * CAP + (rt + 1) * CRT],
                            start=(kc == 0), stop=(kc == 31)))
            pe.c += 1
            pr.wait("v", pe)
            for dm in range(NDC):
                if dm < NDC - 1:
                    pr.v(lambda e, dm=dm: e.tensor_single_scalar(
                        oe_fm[:, dm * CRT:(dm + 1) * CRT], psum[dm][:, 0:CRT],
                        s1_sb[:, dm:dm + 1], ALU.mult))
                else:
                    pr.v(lambda e, dm=dm: e.tensor_single_scalar(
                        oe_fm[:, dm * CRT:(dm + 1) * CRT], psum[dm][:, 0:CRT],
                        s1_sb[:, dm:dm + 1], ALU.mult).then_inc(dv.h, 1))
                pr.vd()
            dv.c += 1
            m2_evac_dv = dv.c
            for rsub in range(3):
                rc = rt * 3 + rsub
                opar = rc % 2
                pr.wait("t", dv, max(m2_evac_dv, tr_evac_dv))
                for dm in range(NDC):
                    bk = dm // 4
                    if dm < NDC - 1:
                        pr.t(lambda e, dm=dm, rsub=rsub, bk=bk: e.transpose(
                            psum[bk][:, (dm % 4) * 128:(dm % 4 + 1) * 128],
                            oe_fm[:, dm * CRT + rsub * 128:dm * CRT + (rsub + 1) * 128],
                            ident[:, :]))
                    else:
                        pr.t(lambda e, dm=dm, rsub=rsub, bk=bk: e.transpose(
                            psum[bk][:, (dm % 4) * 128:(dm % 4 + 1) * 128],
                            oe_fm[:, dm * CRT + rsub * 128:dm * CRT + (rsub + 1) * 128],
                            ident[:, :]).then_inc(pe.h, 1))
                pe.c += 1
                pr.wait("v", pe)
                pr.wait("v", ldp, oe_scat_of[opar])
                pr.v(lambda e, opar=opar: e.tensor_copy(
                    oe_tm[opar][:, 0:512], psum[0][:, :]))
                pr.vd()
                pr.v(lambda e, opar=opar: e.tensor_copy(
                    oe_tm[opar][:, 512:1024], psum[1][:, :])
                     .then_inc(dv.h, 1)); dv.c += 1
                pr.vd()
                tr_evac_dv = dv.c
                pr.wait("p", dv, tr_evac_dv)
                pr.p(lambda e, rc=rc, par=par, opar=opar: e.indirect_dma_start(
                    d["pairs_d"][:, :],
                    IOOA(ap=sl_sb[par][:, rc:rc + 1], axis=0),
                    oe_tm[opar][:, :], None,
                    bounds_check=bnd_slot, oob_is_err=False)
                    .then_inc(ldp.h, 16)); ldp.c += 16
                moe_scat_ldp = ldp.c
                oe_scat_of[opar] = ldp.c
        pr.wait("p", ldp)
        for b in range(8):
            m1_banks_free[b] = tr_evac_dv
        xg_banks[4] = tr_evac_dv
        xg_banks[5] = tr_evac_dv
        m2_done_pe = pe.c

        pr.run_block(f"moe{e_i}")
    moe_scats_done = ldp.c
    mos.close()

    fin = ExitStack()
    pr_tile = [sbuf(fin, f"pr_tile{i}", (P, 2 * DM)) for i in range(2)]
    res_t = [sbuf(fin, f"res_t{i}", (P, DM)) for i in range(2)]
    fstage = [sbuf(fin, f"fstage{i}", (P, DM)) for i in range(2)]
    fin_banks = {2: 0, 3: 0}
    tp_pe_of = [0, 0]
    fadd_of = [0, 0]
    pr.wait("q", ldp, moe_scats_done)
    pr.wait("q", st); pr.wait("q", ld)
    for ti in range(16):
        fp_ = ti % 2
        pr.wait("q", pe, tp_pe_of[fp_])
        qdma(pr_tile[fp_][:, :],
             d["pairs_d"][2 * ti * P:2 * (ti + 1) * P, :]
             .rearrange("(p two) m -> p two m", p=P), pin=True)
        pt_ld = ld.c
        pr.wait("v", ld, pt_ld)
        pr.v(lambda e, ti=ti, fp_=fp_: e.tensor_single_scalar(
            res_t[fp_][:, :], pr_tile[fp_][:, DM:2 * DM],
            g01_all[:, 2 * ti + 1:2 * ti + 2], ALU.mult))
        pr.vd()
        pr.v(lambda e, ti=ti, fp_=fp_: e.scalar_tensor_tensor(
            res_t[fp_][:, :], pr_tile[fp_][:, 0:DM],
            g01_all[:, 2 * ti:2 * ti + 1],
            res_t[fp_][:, :], ALU.mult, ALU.add).then_inc(dv.h, 1)); dv.c += 1
        pr.vd()
        bank = 2 + (ti % 2)
        pr.wait("t", dv)
        pr.wait("t", dv, fin_banks[bank])
        for c in range(NDC):
            bk2 = bank
            col = (c % 4) * 128
            if c == 3 or c == NDC - 1:
                pr.t(lambda e, c=c, col=col, bk2=bk2, fp_=fp_: e.transpose(
                    psum[bk2][:, col:col + 128] if (c // 4) == 0 else
                    psum[bk2 + 2][:, col:col + 128],
                    res_t[fp_][:, c * P:(c + 1) * P], ident[:, :])
                    .then_inc(pe.h, 1))
                pe.c += 1
            else:
                pr.t(lambda e, c=c, col=col, bk2=bk2, fp_=fp_: e.transpose(
                    psum[bk2][:, col:col + 128] if (c // 4) == 0 else
                    psum[bk2 + 2][:, col:col + 128],
                    res_t[fp_][:, c * P:(c + 1) * P], ident[:, :]))
        tp_pe_of[fp_] = pe.c
        pr.wait("v", pe)
        pr.wait("v", ldp, fadd_of[fp_])
        pr.v(lambda e, bank=bank, fp_=fp_: e.tensor_copy(
            fstage[fp_][:, 0:512], psum[bank][:, :]))
        pr.vd()
        pr.v(lambda e, bank=bank, fp_=fp_: e.tensor_copy(
            fstage[fp_][:, 512:1024], psum[bank + 2][:, :]).then_inc(dv.h, 1))
        dv.c += 1
        pr.vd()
        fin_banks[bank] = dv.c
        pr.wait("p", dv)
        if ti == 0:
            pr.wait("p", ldp, moe_scats_done)
        pdma(out_fm[:, ti * P:(ti + 1) * P].rearrange("(c p) t -> p c t", p=P),
             fstage[fp_][:, :].rearrange("p (c t) -> p c t", c=NDC),
             accum=ALU.add)
        fadd_of[fp_] = ldp.c
    pr.wait("p", ldp)
    pr.run_block("fincomb")
    fin.close()

    pr.wait("q", st)
    pr.wait("q", ld)
    pr.wait("p", ldp)
    pr.wait("s", lds)
    pr.run_block("final")

    mos.close()
    es.close()
    return nc


def kernel(**inputs):
    x = np.asarray(inputs["L0_attn_x"], np.float32)
    W_in = np.ascontiguousarray(inputs["L0_attn_W_in"], np.float32)
    W_x = np.ascontiguousarray(inputs["L0_attn_W_x"], np.float32)
    W_dt = np.ascontiguousarray(inputs["L0_attn_W_dt"], np.float32)
    W_out = np.ascontiguousarray(inputs["L0_attn_W_out"], np.float32)
    router_w = np.ascontiguousarray(inputs["L0_ffn_router_w"], np.float32)
    wv = np.ascontiguousarray(inputs["L0_ffn_wv"], np.float32)
    wmat = np.ascontiguousarray(inputs["L0_ffn_w"], np.float32)
    w1 = np.ascontiguousarray(inputs["L0_ffn_w1"], np.float32)

    def colmajor(vec, nch):
        return np.ascontiguousarray(
            np.asarray(vec, np.float32).reshape(nch, P).T)

    svT = np.ascontiguousarray(
        np.asarray(inputs["L0_ffn_sv"], np.float32).reshape(NE, FF // P, P)
        .transpose(0, 2, 1))
    sT = np.ascontiguousarray(
        np.asarray(inputs["L0_ffn_s"], np.float32).reshape(NE, FF // P, P)
        .transpose(0, 2, 1))
    s1T = np.ascontiguousarray(
        np.asarray(inputs["L0_ffn_s1"], np.float32).reshape(NE, DM // P, P)
        .transpose(0, 2, 1))

    common = {
        "W_in": W_in, "W_x": W_x, "W_dt": W_dt, "W_out": W_out,
        "router_w": router_w, "wv": wv, "wmat": wmat, "w1": w1,
        "svT": svT, "sT": sT, "s1T": s1T,
        "gamma_c": colmajor(inputs["L0_attn_ln_gamma"], NDC),
        "beta_c": colmajor(inputs["L0_attn_ln_beta"], NDC),
        "b_in_c": colmajor(inputs["L0_attn_b_in"], 2 * DI // P),
        "b_dt_c": colmajor(inputs["L0_attn_b_dt"], NIC),
        "conv_b_c": colmajor(inputs["L0_attn_conv_b"], NIC),
        "D_c": colmajor(inputs["L0_attn_D"], NIC),
        "b_out_c": colmajor(inputs["L0_attn_b_out"], NDC),
        "convw_c": np.ascontiguousarray(
            np.asarray(inputs["L0_attn_conv_w"], np.float32).T
            .reshape(NIC, P, 4)),
        "A_log_c": np.ascontiguousarray(
            np.asarray(inputs["L0_attn_A_log"], np.float32)
            .reshape(NIC, P, NST)),
        "ones_sc": np.full((P, P), 1.0 / DM, np.float32),
        "ident": np.eye(P, dtype=np.float32),
        "ident_bf": np.eye(P, dtype=ml_dtypes.bfloat16),
        "sel8": np.stack([np.ones(NE, np.float32),
                          np.arange(NE, dtype=np.float32) * CAP],
                         axis=1),
        "slot_ids": np.arange(2 * T, dtype=np.int32).reshape(P, 2 * T // P),
        "oob_init": np.full(NE * CAP, 1 << 20, np.int32),
    }

    in_maps = []
    for c in range(NCORES):
        m = dict(common)
        m["xT"] = np.ascontiguousarray(x[c].T)
        in_maps.append(m)

    nc = build(debug=False)
    res = run_bass_kernel_spmd(nc, in_maps, core_ids=list(range(NCORES)))
    out = np.stack([np.asarray(r["out_fm"]).T for r in res.results])
    return np.ascontiguousarray(out.astype(np.float32))


# version 16
# speedup vs baseline: 1.0839x; 1.0050x over previous
import numpy as np
import ml_dtypes
from contextlib import ExitStack

import concourse.bass as bass
import concourse.mybir as mybir
from concourse.bass_utils import run_bass_kernel_spmd

F32 = mybir.dt.float32
BF16 = mybir.dt.bfloat16
AF = mybir.ActivationFunctionType
ALU = mybir.AluOpType
AX = mybir.AxisListType

P = 128
DM = 1024
DI = 2048
NST = 16
DTR = 64
T = 2048
NE = 8
FF = 4096
NDC = DM // P
NIC = DI // P
TC = 256
NTCH = T // TC
TT = 512
NTT = T // TT
CAP = 768
CRT = 384
NCORES = 8

bfdt = ml_dtypes.bfloat16


class Sem:
    def __init__(self, es, nc, name):
        self.h = es.enter_context(nc.semaphore(name))
        self.c = 0


class Prog:
    def __init__(self, nc):
        self.nc = nc
        self.reset()

    def reset(self):
        self.ops = {k: [] for k in "tvspq"}

    def _add(self, k, fn):
        self.ops[k].append(fn)

    def t(self, fn): self._add("t", fn)
    def v(self, fn): self._add("v", fn)
    def s(self, fn): self._add("s", fn)
    def p(self, fn): self._add("p", fn)
    def q(self, fn): self._add("q", fn)
    def vd(self): self._add("v", lambda e: e.drain())
    def sd(self): self._add("s", lambda e: e.drain())
    def pd(self): self._add("p", lambda e: e.drain())

    def wait(self, k, sem, val=None):
        v = sem.c if val is None else val
        if v > 0:
            self._add(k, lambda e, _h=sem.h, _v=v: e.wait_ge(_h, _v))

    def run_block(self, name=None):
        ops = self.ops
        with self.nc.Block(name) as blk:
            if ops["q"]:
                blk.sync(lambda e, L=list(ops["q"]): [f(e) for f in L])
            if ops["t"]:
                blk.tensor(lambda e, L=list(ops["t"]): [f(e) for f in L])
            if ops["v"]:
                blk.vector(lambda e, L=list(ops["v"]): [f(e) for f in L])
            if ops["s"]:
                blk.scalar(lambda e, L=list(ops["s"]): [f(e) for f in L])
            if ops["p"]:
                blk.gpsimd(lambda e, L=list(ops["p"]): [f(e) for f in L])
        self.reset()


def build(debug=False, phases=3):
    nc = bass.Bass("TRN2", target_bir_lowering=False)
    es = ExitStack()

    EI, EO = "ExternalInput", "ExternalOutput"
    d = {}
    def din(name, shape, dt=F32):
        d[name] = nc.dram_tensor(name, list(shape), dt, kind=EI)
    def dint(name, shape, dt=F32):
        d[name] = nc.dram_tensor(name, list(shape), dt,
                                 kind=(EO if debug else "Internal"))

    din("xT", (DM, T))
    din("W_in", (DM, 2 * DI))
    din("W_x", (DI, DTR + 2 * NST))
    din("W_dt", (DTR, DI))
    din("W_out", (DI, DM))
    din("router_w", (DM, NE))
    din("wv", (NE, DM, FF)); din("wmat", (NE, DM, FF)); din("w1", (NE, FF, DM))
    din("svT", (NE, P, FF // P)); din("sT", (NE, P, FF // P))
    din("s1T", (NE, P, DM // P))
    din("gamma_c", (P, NDC)); din("beta_c", (P, NDC))
    din("b_in_c", (P, 2 * DI // P)); din("b_dt_c", (P, NIC))
    din("conv_b_c", (P, NIC)); din("D_c", (P, NIC)); din("b_out_c", (P, NDC))
    din("convw_c", (NIC, P, 4)); din("A_log_c", (NIC, P, NST))
    din("ones_sc", (P, P)); din("ident", (P, P))

    out_fm = nc.dram_tensor("out_fm", [DM, T], F32, kind=EO)

    dint("xnb_d", (DM, T), BF16)
    dint("xnb_tm_d", (T, DM), BF16)
    dint("xn2b_d", (DM, T), BF16)
    dint("gate_d", (NE, T))
    dint("bc_d", (2 * NST, TC))
    dint("oint_d", (2 * T,), mybir.dt.int32)
    dint("slotlist_d", (NE * CAP,), mybir.dt.int32)
    dint("pairs_d", (2 * T, DM))
    din("ident_bf", (P, P), BF16)
    din("sel8", (NE, 2))
    din("slot_ids", (P, 2 * T // P), mybir.dt.int32)
    din("oob_init", (NE * CAP,), mybir.dt.int32)

    def sbuf(stack, name, shape, dt=F32):
        return stack.enter_context(nc.sbuf_tensor(name, list(shape), dt))

    ones_sc = sbuf(es, "c_ones", (P, P))
    ident = sbuf(es, "c_id", (P, P))
    gamma = sbuf(es, "c_gamma", (P, NDC)); beta = sbuf(es, "c_beta", (P, NDC))
    b_in = sbuf(es, "c_bin", (P, 2 * DI // P))
    b_dt = sbuf(es, "c_bdt", (P, NIC)); conv_b = sbuf(es, "c_convb", (P, NIC))
    Dvec = sbuf(es, "c_D", (P, NIC)); b_out = sbuf(es, "c_bout", (P, NDC))
    convw = sbuf(es, "c_convw", (P, NIC * 4))
    A_col = sbuf(es, "c_Acol", (P, NIC * NST))
    wx_sb = sbuf(es, "c_wx", (P, NIC * 96), BF16)
    wdt_sb = sbuf(es, "c_wdt", (DTR, DI), BF16)
    rw_sb = sbuf(es, "c_rw", (P, NDC * NE))
    u_tails = sbuf(es, "c_utails", (P, NIC * 4))
    hstate = sbuf(es, "c_hstate", (P, NIC * NST))
    ccol = sbuf(es, "c_ccol", (P, 2))
    bsum = sbuf(es, "c_bsum", (P, NIC))
    g01_all = sbuf(es, "c_g01", (P, 32))
    ident_bf = sbuf(es, "c_idbf", (P, P), BF16)
    sel8 = sbuf(es, "c_sel8", (NE, 2))
    slotids = sbuf(es, "c_slotids", (P, 2 * T // P), mybir.dt.int32)

    psum = [es.enter_context(nc.psum_tensor(f"ps{i}", [P, 512], F32))
            for i in range(8)]

    ld = Sem(es, nc, "ld")
    ldp = Sem(es, nc, "ldp")
    lds = Sem(es, nc, "lds")
    st = Sem(es, nc, "st")
    lw = [Sem(es, nc, "lw0"), Sem(es, nc, "lw1")]
    lwp = Sem(es, nc, "lwp")
    pe = Sem(es, nc, "pe")
    dv = Sem(es, nc, "dv")
    sc = Sem(es, nc, "sc")
    pl = Sem(es, nc, "pl")

    pr = Prog(nc)
    bnd_tok = nc.gpsimd.alloc_register("bnd_tok")
    bnd_slot = nc.gpsimd.alloc_register("bnd_slot")
    bnd_list = nc.gpsimd.alloc_register("bnd_list")

    def qdma(dst, src, sem=ld, pin=False):
        pr.q(lambda e, a=dst, b=src, s=sem: e.dma_start(a, b).then_inc(s.h, 16))
        sem.c += 16
        if pin:
            pr.wait("q", sem)

    def pdma(dst, src, sem=ldp, accum=None, pin=False):
        if accum is None:
            pr.p(lambda e, a=dst, b=src, s=sem: e.dma_start(a, b)
                 .then_inc(s.h, 16))
        else:
            pr.p(lambda e, a=dst, b=src, s=sem, op=accum: e.dma_start(
                a, b, accum_op=op).then_inc(s.h, 16))
        sem.c += 16
        if pin:
            pr.wait("p", sem)

    for dst, src in [(ones_sc, d["ones_sc"]), (ident, d["ident"]),
                     (gamma, d["gamma_c"]), (beta, d["beta_c"]),
                     (b_in, d["b_in_c"]), (b_dt, d["b_dt_c"]),
                     (conv_b, d["conv_b_c"]), (Dvec, d["D_c"]),
                     (b_out, d["b_out_c"])]:
        qdma(dst[:, :], src[:, :])
    qdma(convw[:, :].rearrange("p (c f) -> p c f", c=NIC),
         d["convw_c"][:, :, :].transpose([1, 0, 2]))
    qdma(A_col[:, :].rearrange("p (c f) -> p c f", c=NIC),
         d["A_log_c"][:, :, :].transpose([1, 0, 2]))
    pdma(wx_sb[:, :].rearrange("p (c f) -> p c f", c=NIC),
         d["W_x"][:, :].rearrange("(c p) f -> p c f", p=P))
    pdma(wdt_sb[:, :], d["W_dt"][:, :])
    qdma(ident_bf[:, :], d["ident_bf"][:, :])
    qdma(sel8[:, :], d["sel8"][:, :])
    qdma(slotids[:, :], d["slot_ids"][:, :])
    qdma(rw_sb[:, :].rearrange("p (c f) -> p c f", c=NDC),
         d["router_w"][:, :].rearrange("(c p) f -> p c f", p=P), pin=True)
    qdma(d["slotlist_d"][:], d["oob_init"][:])
    pr.p(lambda e: e.reg_mov(bnd_tok, T - 1))
    pr.p(lambda e: e.reg_mov(bnd_slot, 2 * T - 1))
    pr.p(lambda e: e.reg_mov(bnd_list, NE * CAP - 1))
    pr.wait("s", ld)
    pr.s(lambda e: e.activation(A_col[:, :], A_col[:, :], AF.Exp)
         .then_inc(sc.h, 1)); sc.c += 1
    pr.wait("v", sc)
    pr.v(lambda e: e.tensor_scalar_mul(A_col[:, :], A_col[:, :], -1.0))
    pr.vd()
    pr.v(lambda e: e.memset(u_tails[:, :], 0.0))
    pr.v(lambda e: e.memset(hstate[:, :], 0.0))
    pr.v(lambda e: e.memset(ccol[:, 0:1], 1e-5))
    pr.v(lambda e: e.memset(ccol[:, 1:2], -0.5))
    pr.vd()
    for j in range(NIC):
        pr.v(lambda e, j=j: e.tensor_single_scalar(
            u_tails[:, j * 4:j * 4 + 3], u_tails[:, j * 4:j * 4 + 3],
            b_in[:, j:j + 1], ALU.subtract))
        pr.v(lambda e, j=j: e.reduce_sum(
            bsum[:, j:j + 1], convw[:, j * 4:(j + 1) * 4], axis=AX.X))
    pr.vd()
    for j in range(NIC):
        pr.v(lambda e, j=j: e.scalar_tensor_tensor(
            bsum[:, j:j + 1], bsum[:, j:j + 1], b_in[:, j:j + 1],
            conv_b[:, j:j + 1], ALU.mult, ALU.add))
    pr.vd()
    pr.run_block("consts")

    ohss = ExitStack()
    ohs_fm = sbuf(ohss, "c_ohs", (NE, 2 * T))
    lns = ExitStack()
    x_all = sbuf(lns, "x_all", (P, NDC * T))
    xn_all = sbuf(lns, "xn_all", (P, NDC * T))
    xsq = [sbuf(lns, f"xsq{i}", (P, T)) for i in range(2)]
    mean_bc = sbuf(lns, "mean_bc", (P, T))
    s2_bc = sbuf(lns, "s2_bc", (P, T))
    rstd_bc = s2_bc
    xtile_b = sbuf(lns, "xtile_b", (P, T), BF16)
    probs_t = sbuf(lns, "probs_t", (P, NE))
    row_t = sbuf(lns, "row_t", (P, 4))
    top_t = sbuf(lns, "top_t", (P, NE))
    mask_t = sbuf(lns, "mask_t", (P, NE))
    gate_t = sbuf(lns, "gate_t", (P, NE))
    gate_fm = sbuf(lns, "gate_fm", (NE, T))
    oh0_t = sbuf(lns, "oh0_t", (P, NE))
    oh1_t = sbuf(lns, "oh1_t", (P, NE))
    xtm_stage = sbuf(lns, "xtm_stage", (P, DM), BF16)
    xcast = sbuf(lns, "xcast", (P, DM), BF16)

    def xc(c): return x_all[:, c * T:(c + 1) * T]
    def xnc(c): return xn_all[:, c * T:(c + 1) * T]

    for c in range(NDC):
        qdma(xc(c), d["xT"][c * P:(c + 1) * P, :], pin=(c == NDC - 1))

    def ln_stats(src_of):
        sq_pe = [0, 0]
        for c in range(NDC):
            cp = c % 2
            if c >= 2:
                pr.wait("s", pe, sq_pe[cp])
            if c == 0:
                pr.wait("s", ld)
            pr.s(lambda e, c=c, cp=cp: e.activation(xsq[cp][:, :], src_of(c),
                                                    AF.Square)
                 .then_inc(sc.h, 1)); sc.c += 1
            pr.wait("t", sc)
            for tt4 in range(4):
                st_, sp_ = (c == 0), (c == NDC - 1)
                pr.t(lambda e, c=c, tt4=tt4, st_=st_, sp_=sp_: e.matmul(
                    psum[tt4][:, :], ones_sc[:, :],
                    src_of(c)[:, tt4 * 512:(tt4 + 1) * 512],
                    start=st_, stop=sp_))
                if tt4 < 3:
                    pr.t(lambda e, c=c, tt4=tt4, st_=st_, sp_=sp_, cp=cp: e.matmul(
                        psum[4 + tt4][:, :], ones_sc[:, :],
                        xsq[cp][:, tt4 * 512:(tt4 + 1) * 512],
                        start=st_, stop=sp_))
                else:
                    pr.t(lambda e, c=c, tt4=tt4, st_=st_, sp_=sp_, cp=cp: e.matmul(
                        psum[4 + tt4][:, :], ones_sc[:, :],
                        xsq[cp][:, tt4 * 512:(tt4 + 1) * 512],
                        start=st_, stop=sp_).then_inc(pe.h, 1))
            pe.c += 1
            sq_pe[cp] = pe.c
        pr.wait("v", pe)
        for tt4 in range(4):
            pr.v(lambda e, tt4=tt4: e.tensor_copy(
                mean_bc[:, tt4 * 512:(tt4 + 1) * 512], psum[tt4][:, :]))
            pr.v(lambda e, tt4=tt4: e.tensor_copy(
                s2_bc[:, tt4 * 512:(tt4 + 1) * 512], psum[4 + tt4][:, :]))
        pr.vd()
        pr.v(lambda e: e.tensor_tensor(xsq[0][:, :], mean_bc[:, :],
                                       mean_bc[:, :], ALU.mult))
        pr.vd()
        pr.v(lambda e: e.tensor_tensor(s2_bc[:, :], s2_bc[:, :], xsq[0][:, :],
                                       ALU.subtract).then_inc(dv.h, 1))
        dv.c += 1
        pr.wait("s", dv)
        pr.s(lambda e: e.activation(rstd_bc[:, :], s2_bc[:, :], AF.Ln,
                                    bias=ccol[:, 0:1]))
        pr.sd()
        pr.s(lambda e: e.activation(rstd_bc[:, :], rstd_bc[:, :], AF.Exp,
                                    scale=ccol[:, 1:2]).then_inc(sc.h, 1)); sc.c += 1

    ln_stats(xc)
    pr.wait("v", sc)
    for c in range(NDC):
        pr.v(lambda e, c=c: e.tensor_tensor(xnc(c), xc(c), mean_bc[:, :],
                                            ALU.subtract))
        pr.vd()
        if c < NDC - 1:
            pr.v(lambda e, c=c: e.tensor_tensor(xnc(c), xnc(c), rstd_bc[:, :],
                                                ALU.mult))
        else:
            pr.v(lambda e, c=c: e.tensor_tensor(xnc(c), xnc(c), rstd_bc[:, :],
                                                ALU.mult).then_inc(dv.h, 1))
        pr.vd()
    dv.c += 1
    ln1_done_dv = dv.c

    for c in range(NDC):
        pr.wait("v", st)
        pr.v(lambda e, c=c: e.tensor_copy(xtile_b[:, :], xnc(c))
             .then_inc(dv.h, 1)); dv.c += 1
        pr.vd()
        pr.wait("q", dv)
        qdma(d["xnb_d"][c * P:(c + 1) * P, :], xtile_b[:, :], st)

    pr.wait("t", dv, ln1_done_dv)
    for ti in range(16):
        for c in range(NDC):
            if ti == 15 and c == NDC - 1:
                pr.t(lambda e, ti=ti, c=c: e.matmul(
                    psum[0][:, ti * 8:(ti + 1) * 8],
                    xnc(c)[:, ti * P:(ti + 1) * P],
                    rw_sb[:, c * NE:(c + 1) * NE],
                    start=(c == 0), stop=(c == NDC - 1)).then_inc(pe.h, 1))
            else:
                pr.t(lambda e, ti=ti, c=c: e.matmul(
                    psum[0][:, ti * 8:(ti + 1) * 8],
                    xnc(c)[:, ti * P:(ti + 1) * P],
                    rw_sb[:, c * NE:(c + 1) * NE],
                    start=(c == 0), stop=(c == NDC - 1)))
    pe.c += 1
    logits_pe = pe.c

    for c in range(NDC):
        pr.v(lambda e, c=c: e.tensor_tensor(xc(c), xc(c), xnc(c), ALU.add)
             .then_inc(dv.h, 1)); dv.c += 1
        pr.vd()
        pr.wait("q", dv)
        qdma(out_fm[c * P:(c + 1) * P, :], xc(c), st)
    r0_store_st = st.c

    pr.wait("v", pe, logits_pe)
    pr.wait("s", pe, logits_pe)
    evac_dv_of_bank = {1: 0, 2: 0}
    for ti in range(16):
        lg = psum[0][:, ti * 8:(ti + 1) * 8]
        pr.v(lambda e, lg=lg: e.reduce_max(row_t[:, 0:1], lg, axis=AX.X,
                                           negate=True).then_inc(dv.h, 1))
        dv.c += 1
        pr.vd()
        pr.wait("s", dv)
        pr.s(lambda e, lg=lg: e.activation(probs_t[:, :], lg, AF.Exp,
                                           bias=row_t[:, 0:1])
             .then_inc(sc.h, 1)); sc.c += 1
        pr.wait("v", sc)
        pr.v(lambda e: e.reduce_sum(row_t[:, 1:2], probs_t[:, :], axis=AX.X))
        pr.vd()
        pr.v(lambda e: e.reciprocal(row_t[:, 2:3], row_t[:, 1:2]))
        pr.vd()
        pr.v(lambda e: e.tensor_scalar_mul(probs_t[:, :], probs_t[:, :],
                                           row_t[:, 2:3]))
        pr.vd()
        pr.v(lambda e: e.max(top_t[:, :], probs_t[:, :]))
        pr.vd()
        pr.v(lambda e: e.tensor_single_scalar(mask_t[:, :], probs_t[:, :],
                                              top_t[:, 1:2], ALU.is_ge))
        pr.v(lambda e, ti=ti: e.tensor_copy(g01_all[:, ti * 2:ti * 2 + 2],
                                            top_t[:, 0:2]))
        pr.v(lambda e: e.tensor_single_scalar(oh0_t[:, :], probs_t[:, :],
                                              top_t[:, 0:1], ALU.is_equal))
        pr.v(lambda e: e.tensor_single_scalar(oh1_t[:, :], probs_t[:, :],
                                              top_t[:, 1:2], ALU.is_equal))
        pr.vd()
        pr.v(lambda e: e.tensor_tensor(gate_t[:, :], probs_t[:, :],
                                       mask_t[:, :], ALU.mult)
             .then_inc(dv.h, 1)); dv.c += 1
        pr.vd()
        bank = 1 + (ti % 2)
        pr.wait("t", dv, evac_dv_of_bank[bank])
        pr.wait("t", dv)
        pr.t(lambda e, bank=bank: e.transpose(
            psum[bank][0:NE, 0:P], gate_t[:, :], ident[:, :]))
        pr.t(lambda e, bank=bank: e.transpose(
            psum[bank][0:NE, 128:256], oh0_t[:, :], ident[:, :]))
        pr.t(lambda e, bank=bank: e.transpose(
            psum[bank][0:NE, 256:384], oh1_t[:, :], ident[:, :])
            .then_inc(pe.h, 1)); pe.c += 1
        pr.wait("v", pe)
        pr.v(lambda e, ti=ti, bank=bank: e.tensor_copy(
            gate_fm[:, ti * P:(ti + 1) * P], psum[bank][0:NE, 0:P]))
        pr.v(lambda e, ti=ti, bank=bank: e.tensor_copy(
            ohs_fm[:, 2 * ti * P:2 * (ti + 1) * P]
            .rearrange("e (t two) -> e two t", two=2)[:, 0, :],
            psum[bank][0:NE, 128:256]))
        pr.v(lambda e, ti=ti, bank=bank: e.tensor_copy(
            ohs_fm[:, 2 * ti * P:2 * (ti + 1) * P]
            .rearrange("e (t two) -> e two t", two=2)[:, 1, :],
            psum[bank][0:NE, 256:384])
            .then_inc(dv.h, 1)); dv.c += 1
        pr.vd()
        evac_dv_of_bank[bank] = dv.c
    pr.wait("q", dv)
    qdma(d["gate_d"][:, :], gate_fm[:, :], st)

    xtm_banks = {5: 0, 6: 0}
    for ti in range(16):
        bank = 5 + (ti % 2)
        pr.wait("v", pe)
        for c in range(NDC):
            if c < NDC - 1:
                pr.v(lambda e, ti=ti, c=c: e.tensor_copy(
                    xcast[:, c * P:(c + 1) * P],
                    xnc(c)[:, ti * P:(ti + 1) * P]))
            else:
                pr.v(lambda e, ti=ti, c=c: e.tensor_copy(
                    xcast[:, c * P:(c + 1) * P],
                    xnc(c)[:, ti * P:(ti + 1) * P]).then_inc(dv.h, 1))
        dv.c += 1
        pr.vd()
        pr.wait("t", dv)
        pr.wait("t", dv, xtm_banks[bank])
        for c in range(NDC):
            if c < NDC - 1:
                pr.t(lambda e, c=c, bank=bank: e.transpose(
                    psum[bank][:, :].bitcast(BF16)[:, c * P:(c + 1) * P],
                    xcast[:, c * P:(c + 1) * P], ident_bf[:, :]))
            else:
                pr.t(lambda e, c=c, bank=bank: e.transpose(
                    psum[bank][:, :].bitcast(BF16)[:, c * P:(c + 1) * P],
                    xcast[:, c * P:(c + 1) * P], ident_bf[:, :])
                    .then_inc(pe.h, 1))
        pe.c += 1
        pr.wait("v", pe)
        pr.wait("v", st)
        pr.v(lambda e, bank=bank: e.tensor_copy(
            xtm_stage[:, :], psum[bank][:, :].bitcast(BF16))
            .then_inc(dv.h, 1)); dv.c += 1
        pr.vd()
        xtm_banks[bank] = dv.c
        pr.wait("q", dv)
        qdma(d["xnb_tm_d"][ti * P:(ti + 1) * P, :], xtm_stage[:, :], st,
             pin=(ti == 15))

    routing_done_dv = dv.c

    pr.wait("t", dv, routing_done_dv)
    ln_stats(xnc)
    pr.wait("v", sc)
    for c in range(NDC):
        pr.v(lambda e, c=c: e.tensor_tensor(xnc(c), xnc(c), mean_bc[:, :],
                                            ALU.subtract))
        pr.vd()
        pr.v(lambda e, c=c: e.tensor_tensor(xnc(c), xnc(c), rstd_bc[:, :],
                                            ALU.mult))
        pr.vd()
        pr.wait("v", st)
        pr.v(lambda e, c=c: e.tensor_scalar(
            xtile_b[:, :], xnc(c), gamma[:, c:c + 1], beta[:, c:c + 1],
            ALU.mult, ALU.add).then_inc(dv.h, 1)); dv.c += 1
        pr.vd()
        pr.wait("q", dv)
        qdma(d["xn2b_d"][c * P:(c + 1) * P, :], xtile_b[:, :], st)

    pr.wait("q", st)
    pr.run_block("ln_routing")
    lns.close()

    cps = ExitStack()
    pos_fm = sbuf(cps, "pos_fm", (NE, 2 * T))
    otile = sbuf(cps, "otile", (1, 512))
    ot_int = sbuf(cps, "ot_int", (1, 512), mybir.dt.int32)
    pr.wait("q", st); pr.wait("q", ld); pr.wait("q", ldp); pr.wait("q", lds)
    pr.v(lambda e: e.tensor_tensor_scan(pos_fm[:, :], ohs_fm[:, :],
                                        ohs_fm[:, :], 0.0, ALU.add, ALU.max))
    pr.vd()
    pr.v(lambda e: e.tensor_tensor(pos_fm[:, :], pos_fm[:, :], ohs_fm[:, :],
                                   ALU.mult).then_inc(dv.h, 1)); dv.c += 1
    pr.vd()
    pr.wait("t", dv)
    for t8 in range(8):
        if t8 > 0:
            pr.wait("t", dv)
        pr.t(lambda e, t8=t8: e.matmul(
            psum[0][0:1, 0:512], sel8[:, 0:1],
            pos_fm[:, t8 * 512:(t8 + 1) * 512], start=True, stop=False))
        pr.t(lambda e, t8=t8: e.matmul(
            psum[0][0:1, 0:512], sel8[:, 1:2],
            ohs_fm[:, t8 * 512:(t8 + 1) * 512], start=False, stop=True)
            .then_inc(pe.h, 1)); pe.c += 1
        pr.wait("v", pe)
        if t8 > 0:
            pr.wait("v", lds)
        pr.v(lambda e: e.tensor_scalar_add(otile[0:1, :], psum[0][0:1, 0:512],
                                           -1.0).then_inc(dv.h, 1)); dv.c += 1
        pr.vd()
        pr.v(lambda e: e.tensor_copy(ot_int[0:1, :], otile[0:1, :])
             .then_inc(dv.h, 1)); dv.c += 1
        pr.vd()
        pr.wait("s", dv)
        pr.s(lambda e, t8=t8: e.dma_start(
            d["oint_d"][t8 * 512:(t8 + 1) * 512].unsqueeze(0),
            ot_int[0:1, :]).then_inc(lds.h, 16)); lds.c += 16
        pr.wait("s", lds)
    pr.run_block("compact_pos")
    cps.close()
    ohss.close()

    if phases < 1:
        pr.wait("q", st); pr.wait("q", ld)
        pr.wait("p", ldp); pr.wait("s", lds)
        pr.run_block("final")
        es.close()
        return nc

    mms = ExitStack()
    wib = [sbuf(mms, f"wib{i}", (P, 8 * 512), BF16) for i in range(2)]
    u_all = sbuf(mms, "u_all", (P, NIC * TC))
    ub = sbuf(mms, "ub", (P, NIC * TC), BF16)
    sres = sbuf(mms, "sres", (P, NIC * TC))
    dtb = sbuf(mms, "dtb", (DTR, TC), BF16)
    BCbc = sbuf(mms, "BCbc", (P, 2 * NST * TC))
    Bbc = BCbc[:, 0:NST * TC]
    Cbc = BCbc[:, NST * TC:2 * NST * TC]
    h_all = sbuf(mms, "h_all", (P, NST * TC))
    mc = sbuf(mms, "mc", (P, NST * TC))
    NSLOT = 3
    a_sl = [sbuf(mms, f"a_sl{i}", (P, TC)) for i in range(NSLOT)]
    b_all = sbuf(mms, "b_all", (P, NST * TC))
    u_raw = sbuf(mms, "u_raw", (P, TC + 4))
    c_t = sbuf(mms, "c_t", (P, TC))
    e_t = [sbuf(mms, f"e_t{i}", (P, TC)) for i in range(2)]
    r_t = [sbuf(mms, f"r_t{i}", (P, TC)) for i in range(2)]
    xb_t = [sbuf(mms, f"xb_t{i}", (P, TC)) for i in range(2)]
    y_t = c_t
    du_t = [sbuf(mms, f"du_t{i}", (P, TC)) for i in range(2)]
    delta_t = [sbuf(mms, f"delta_t{i}", (P, TC)) for i in range(2)]
    rl_t = e_t
    g_all = sbuf(mms, "g_all", (P, NIC * TC), BF16)
    mout = sbuf(mms, "mout", (P, 4 * TC))
    wout_sb = sbuf(mms, "wout_sb", (P, NIC * DM), BF16)

    pr.wait("p", st); pr.wait("p", ld); pr.wait("p", lds)
    xn2t = mout[:, :].bitcast(BF16)
    bc_sb = mc[0:NST, 0:2 * TC]
    pdma(wout_sb[:, :].rearrange("p (c f) -> p c f", c=NIC),
         d["W_out"][:, :].rearrange("(c p) f -> p c f", p=P))
    wout_ldp = ldp.c

    bank_free = {0: ("sc", 0), 1: ("sc", 0)}
    wib_ldp = [0, 0]
    wout_evac_dv = 0
    prev_mamba_add_ldp = 0
    sems_by_name = {"sc": sc, "dv": dv, "pe": pe, "pl": pl}
    wout_mm_pe = [0]
    xn2b_store_st = st.c

    for tch in range(NTCH):
        t0 = tch * TC

        if tch == 0:
            pr.wait("q", st, xn2b_store_st)
        else:
            pr.wait("q", ldp, prev_mamba_add_ldp)
        qdma(xn2t.rearrange("p (c f) -> p c f", c=NDC),
             d["xn2b_d"][:, t0:t0 + TC].rearrange("(c p) f -> p c f", p=P),
             pin=True)
        xn2t_ld = ld.c

        for fb in range(8):
            buf = fb % 2
            pr.wait("p", pe)
            pdma(wib[buf][:, :].rearrange("p (c f) -> p c f", c=NDC),
                 d["W_in"][:, fb * 512:(fb + 1) * 512]
                 .rearrange("(c p) f -> p c f", p=P), pin=True)
            wib_ldp[buf] = ldp.c
            for fc in range(4):
                j = fb * 4 + fc
                bank = j % 2
                sname, sval = bank_free[bank]
                pr.wait("t", sems_by_name[sname], sval)
                pr.wait("t", ldp, wib_ldp[buf])
                if fb == 0 and fc == 0:
                    pr.wait("t", ld, xn2t_ld)
                for kc in range(NDC):
                    if kc < NDC - 1:
                        pr.t(lambda e, bank=bank, buf=buf, fc=fc, kc=kc: e.matmul(
                            psum[bank][:, 0:TC],
                            wib[buf][:, kc * 512 + fc * 128:kc * 512 + (fc + 1) * 128],
                            xn2t[:, kc * TC:(kc + 1) * TC],
                            start=(kc == 0), stop=False))
                    else:
                        pr.t(lambda e, bank=bank, buf=buf, fc=fc, kc=kc: e.matmul(
                            psum[bank][:, 0:TC],
                            wib[buf][:, kc * 512 + fc * 128:kc * 512 + (fc + 1) * 128],
                            xn2t[:, kc * TC:(kc + 1) * TC],
                            start=False, stop=True).then_inc(pe.h, 1))
                pe.c += 1
                if j < 16:
                    pr.wait("s", dv)
                    pr.wait("s", pe)
                    pr.s(lambda e, bank=bank: e.copy(
                        u_raw[:, 4:4 + TC], psum[bank][:, 0:TC])
                        .then_inc(sc.h, 1)); sc.c += 1
                    bank_free[bank] = ("sc", sc.c)
                    pr.wait("v", sc)
                    pr.v(lambda e, j=j: e.tensor_copy(
                        u_raw[:, 1:4], u_tails[:, j * 4:j * 4 + 3]))
                    pr.vd()
                    pr.v(lambda e, j=j: e.tensor_scalar(
                        c_t[:, :], u_raw[:, 1:1 + TC],
                        convw[:, j * 4 + 0:j * 4 + 1], bsum[:, j:j + 1],
                        ALU.mult, ALU.add))
                    pr.vd()
                    for tap in range(1, 4):
                        pr.v(lambda e, j=j, tap=tap: e.scalar_tensor_tensor(
                            c_t[:, :], u_raw[:, 1 + tap:1 + tap + TC],
                            convw[:, j * 4 + tap:j * 4 + tap + 1], c_t[:, :],
                            ALU.mult, ALU.add))
                        pr.vd()
                    pr.v(lambda e, j=j: e.tensor_copy(
                        u_tails[:, j * 4:j * 4 + 3],
                        u_raw[:, 1 + TC - 3:1 + TC]).then_inc(dv.h, 1))
                    dv.c += 1
                    pr.vd()
                    pr.wait("s", dv)
                    pr.s(lambda e, j=j: e.activation(
                        u_all[:, j * TC:(j + 1) * TC], c_t[:, :], AF.Silu))
                    pr.sd()
                    pr.s(lambda e, j=j: e.copy(
                        ub[:, j * TC:(j + 1) * TC],
                        u_all[:, j * TC:(j + 1) * TC]).then_inc(sc.h, 1))
                    sc.c += 1
                    pr.sd()
                else:
                    jr = j - 16
                    pr.wait("s", pe)
                    if fb == 4 and fc == 0:
                        pr.wait("s", dv)
                    pr.s(lambda e, bank=bank, j=j, jr=jr: e.activation(
                        sres[:, jr * TC:(jr + 1) * TC], psum[bank][:, 0:TC],
                        AF.Silu, bias=b_in[:, j:j + 1])
                        .then_inc(sc.h, 1)); sc.c += 1
                    pr.sd()
                    bank_free[bank] = ("sc", sc.c)
        s1_done_sc = sc.c

        pr.wait("t", sc, s1_done_sc)
        for kc in range(NIC):
            pr.t(lambda e, kc=kc: e.matmul(
                psum[2][0:DTR, 0:TC], wx_sb[:, kc * 96:kc * 96 + 64],
                ub[:, kc * TC:(kc + 1) * TC],
                start=(kc == 0), stop=(kc == NIC - 1)))
        for kc in range(NIC):
            pr.t(lambda e, kc=kc: e.matmul(
                psum[3][0:NST, 0:TC], wx_sb[:, kc * 96 + 64:kc * 96 + 80],
                ub[:, kc * TC:(kc + 1) * TC],
                start=(kc == 0), stop=(kc == NIC - 1)))
        for kc in range(NIC):
            if kc < NIC - 1:
                pr.t(lambda e, kc=kc: e.matmul(
                    psum[3][0:NST, 256:256 + TC],
                    wx_sb[:, kc * 96 + 80:kc * 96 + 96],
                    ub[:, kc * TC:(kc + 1) * TC],
                    start=(kc == 0), stop=False))
            else:
                pr.t(lambda e, kc=kc: e.matmul(
                    psum[3][0:NST, 256:256 + TC],
                    wx_sb[:, kc * 96 + 80:kc * 96 + 96],
                    ub[:, kc * TC:(kc + 1) * TC],
                    start=False, stop=True).then_inc(pe.h, 1))
        pe.c += 1
        pr.wait("s", pe)
        pr.wait("s", lds)
        pr.wait("s", dv)
        pr.s(lambda e: e.copy(dtb[:, :], psum[2][0:DTR, 0:TC]))
        pr.sd()
        pr.s(lambda e: e.copy(bc_sb[:, 0:TC], psum[3][0:NST, 0:TC]))
        pr.sd()
        pr.s(lambda e: e.copy(bc_sb[:, TC:2 * TC],
                              psum[3][0:NST, 256:256 + TC])
             .then_inc(sc.h, 1)); sc.c += 1
        pr.sd()
        dtb_sc = sc.c
        pr.wait("s", sc, dtb_sc)
        pr.s(lambda e: e.dma_start(
            d["bc_d"][:, :].rearrange("(two n) t -> n two t", two=2),
            bc_sb.rearrange("n (two t) -> n two t", two=2))
            .then_inc(lds.h, 16)); lds.c += 16
        pr.wait("s", lds)
        pr.wait("p", lds)
        pr.wait("p", dv)
        pr.wait("p", pl)
        pdma(BCbc[:, :].rearrange("p (n t) -> p n t", n=2 * NST),
             d["bc_d"][:, :].unsqueeze(0).broadcast_to([P, 2 * NST, TC]),
             pin=True)
        bc_ldp = ldp.c

        scan_dv_hist = []
        relu_sc = [0] * (NIC + 1)
        aprep_sc = [0] * (NIC + 1)
        du_pl_l = [0] * (NIC + 1)
        bprep_pl = [0] * (NIC + 1)
        nab_dv = [0] * (NIC + 1)
        mm_pe = [0] * (NIC + 1)

        def emit_A_pe(j):
            bank = j % 2
            sname, sval = bank_free[bank]
            pr.wait("t", sems_by_name[sname], sval)
            if j == 0:
                pr.wait("t", sc, dtb_sc)
            pr.t(lambda e, bank=bank, j=j: e.matmul(
                psum[bank][:, 0:TC], wdt_sb[:, j * 128:(j + 1) * 128],
                dtb[:, :], start=True, stop=True)
                .then_inc(pe.h, 1)); pe.c += 1
            mm_pe[j] = pe.c

        def emit_A_dve(j):
            par = j % 2
            pr.wait("v", pe, mm_pe[j])
            if j >= 2:
                pr.wait("v", sc, relu_sc[j - 2])
                pr.wait("v", pl, du_pl_l[j - 2])
            pr.v(lambda e, j=j, par=par: e.tensor_single_scalar(
                xb_t[par][:, :], psum[j % 2][:, 0:TC], b_dt[:, j:j + 1],
                ALU.add))
            pr.vd()
            pr.v(lambda e, par=par: e.scalar_tensor_tensor(
                e_t[par][:, :], xb_t[par][:, :], -1.0, xb_t[par][:, :],
                ALU.mult, ALU.min).then_inc(dv.h, 1)); dv.c += 1
            nab_dv[j] = dv.c
            bank_free[j % 2] = ("dv", dv.c)

        def emit_A_sc(j):
            par = j % 2
            pr.wait("s", dv, nab_dv[j])
            if j >= 2:
                pr.wait("s", pl, du_pl_l[j - 2])
            pr.s(lambda e, par=par: e.activation(r_t[par][:, :], e_t[par][:, :],
                                                 AF.Exp))
            pr.sd()
            pr.s(lambda e, par=par: e.activation(r_t[par][:, :], r_t[par][:, :],
                                                 AF.Ln, bias=1.0))
            pr.sd()
            pr.s(lambda e, par=par: e.activation(rl_t[par][:, :],
                                                 xb_t[par][:, :], AF.Relu)
                 .then_inc(sc.h, 1)); sc.c += 1
            relu_sc[j] = sc.c

        def emit_A_pl(j):
            par = j % 2
            pr.wait("p", sc, relu_sc[j])
            if j >= 2:
                pr.wait("p", sc, aprep_sc[j - 2])
            pr.p(lambda e, par=par: e.tensor_tensor(
                delta_t[par][:, :], r_t[par][:, :], rl_t[par][:, :], ALU.add))
            pr.pd()
            pr.p(lambda e, j=j, par=par: e.tensor_tensor(
                du_t[par][:, :], delta_t[par][:, :],
                u_all[:, j * TC:(j + 1) * TC], ALU.mult)
                .then_inc(pl.h, 1)); pl.c += 1
            du_pl_l[j] = pl.c

        def emit_B_pl(j):
            par = j % 2
            if j == 0:
                pr.wait("p", ldp, bc_ldp)
            else:
                pr.wait("p", dv, last_scan_dv[0])
            for n in range(NST):
                pr.p(lambda e, n=n, par=par: e.tensor_tensor(
                    b_all[:, n * TC:(n + 1) * TC], du_t[par][:, :],
                    Bbc[:, n * TC:(n + 1) * TC], ALU.mult)
                    .then_inc(pl.h, 1)); pl.c += 1
            bprep_pl[j] = pl.c

        def emit_B_sc(j):
            par = j % 2
            base = dv.c
            pr.wait("s", pl, du_pl_l[j])
            for n in range(NST):
                slot = n % NSLOT
                gi = j * NST + n
                if n >= NSLOT:
                    free_dv = base + (n - NSLOT + 1)
                elif gi >= NSLOT:
                    free_dv = scan_dv_hist[gi - NSLOT]
                else:
                    free_dv = 0
                if free_dv:
                    pr.wait("s", dv, free_dv)
                pr.s(lambda e, j=j, n=n, slot=slot, par=par: e.activation(
                    a_sl[slot][:, :], delta_t[par][:, :], AF.Exp,
                    scale=A_col[:, j * NST + n:j * NST + n + 1])
                    .then_inc(sc.h, 1)); sc.c += 1
            aprep_sc[j] = sc.c

        def emit_B_dve(j):
            for n in range(NST):
                slot = n % NSLOT
                pr.wait("v", sc, aprep_sc[j] - (NST - 1 - n))
                pr.wait("v", pl, bprep_pl[j] - (NST - 1 - n))
                pr.v(lambda e, j=j, n=n, slot=slot: e.tensor_tensor_scan(
                    h_all[:, n * TC:(n + 1) * TC], a_sl[slot][:, :],
                    b_all[:, n * TC:(n + 1) * TC],
                    hstate[:, j * NST + n:j * NST + n + 1],
                    ALU.mult, ALU.add)
                    .then_inc(dv.h, 1)); dv.c += 1
                scan_dv_hist.append(dv.c)
            last_scan_dv[0] = dv.c
            pr.vd()
            pr.v(lambda e, j=j: e.tensor_copy(
                hstate[:, j * NST:(j + 1) * NST],
                h_all[:, :].rearrange("p (n t) -> p n t", n=NST)[:, :, TC - 1:TC]))
            pr.vd()
            pr.v(lambda e: e.tensor_tensor(
                mc[:, :].rearrange("p (t n) -> p n t", n=NST),
                h_all[:, :].rearrange("p (n t) -> p n t", n=NST),
                Cbc.rearrange("p (n t) -> p n t", n=NST), ALU.mult))
            pr.vd()
            pr.v(lambda e: e.reduce_sum(
                y_t[:, :], mc[:, :].rearrange("p (t n) -> p t n", n=NST),
                axis=AX.X))
            pr.vd()
            pr.v(lambda e, j=j: e.scalar_tensor_tensor(
                y_t[:, :], u_all[:, j * TC:(j + 1) * TC], Dvec[:, j:j + 1],
                y_t[:, :], ALU.mult, ALU.add))
            pr.vd()
            pr.v(lambda e, j=j: e.tensor_tensor(
                y_t[:, :], y_t[:, :], sres[:, j * TC:(j + 1) * TC], ALU.mult)
                .then_inc(dv.h, 1)); dv.c += 1
            pr.wait("s", dv)
            if j == 0:
                pr.wait("s", pe, wout_mm_pe[0])
            pr.s(lambda e, j=j: e.copy(
                g_all[:, j * TC:(j + 1) * TC], y_t[:, :])
                 .then_inc(sc.h, 1)); sc.c += 1

        last_scan_dv = [0]
        emit_A_pe(0); emit_A_dve(0); emit_A_sc(0); emit_A_pl(0)
        for j in range(NIC):
            if j + 1 < NIC:
                emit_A_pe(j + 1)
                emit_A_dve(j + 1)
            emit_B_pl(j)
            emit_B_sc(j)
            if j + 1 < NIC:
                emit_A_sc(j + 1)
                emit_A_pl(j + 1)
            emit_B_dve(j)
        g_done_sc = sc.c

        pr.wait("t", sc, g_done_sc)
        for half in range(2):
            if half == 1 or tch > 0:
                pr.wait("t", dv, wout_evac_dv)
            for dm4 in range(4):
                dm = half * 4 + dm4
                for j in range(NIC):
                    last = (dm4 == 3 and j == NIC - 1)
                    if last:
                        pr.t(lambda e, j=j, dm=dm, dm4=dm4: e.matmul(
                            psum[4 + dm4][:, 0:TC],
                            wout_sb[:, j * DM + dm * 128:j * DM + (dm + 1) * 128],
                            g_all[:, j * TC:(j + 1) * TC],
                            start=(j == 0), stop=(j == NIC - 1))
                            .then_inc(pe.h, 1))
                    else:
                        pr.t(lambda e, j=j, dm=dm, dm4=dm4: e.matmul(
                            psum[4 + dm4][:, 0:TC],
                            wout_sb[:, j * DM + dm * 128:j * DM + (dm + 1) * 128],
                            g_all[:, j * TC:(j + 1) * TC],
                            start=(j == 0), stop=(j == NIC - 1)))
            pe.c += 1
            wout_mm_pe[0] = pe.c
            pr.wait("v", pe)
            pr.wait("v", ldp, prev_mamba_add_ldp)
            for dm4 in range(4):
                dm = half * 4 + dm4
                if dm4 < 3:
                    pr.v(lambda e, dm=dm, dm4=dm4: e.tensor_scalar_add(
                        mout[:, dm4 * TC:(dm4 + 1) * TC], psum[4 + dm4][:, 0:TC],
                        b_out[:, dm:dm + 1]))
                else:
                    pr.v(lambda e, dm=dm, dm4=dm4: e.tensor_scalar_add(
                        mout[:, dm4 * TC:(dm4 + 1) * TC], psum[4 + dm4][:, 0:TC],
                        b_out[:, dm:dm + 1]).then_inc(dv.h, 1))
                pr.vd()
            dv.c += 1
            wout_evac_dv = dv.c
            pr.wait("p", dv, wout_evac_dv)
            if tch == 0 and half == 0:
                pr.wait("p", st, xn2b_store_st)
            pdma(out_fm[half * 512:(half + 1) * 512, t0:t0 + TC]
                 .rearrange("(c p) t -> p c t", p=P),
                 mout[:, :].rearrange("p (c t) -> p c t", c=4),
                 accum=ALU.add, pin=True)
            prev_mamba_add_ldp = ldp.c

    pr.run_block("mamba")
    mamba_adds_done = ldp.c
    mms.close()

    if phases < 2:
        pr.wait("q", st); pr.wait("q", ld)
        pr.wait("p", ldp); pr.wait("s", lds)
        pr.run_block("final")
        es.close()
        return nc

    from concourse.bass import IndirectOffsetOnAxis as IOOA
    mos = ExitStack()
    wbuf = sbuf(mos, "wbuf", (P, 32 * DM), BF16)
    h_e = sbuf(mos, "h_e", (P, 32 * CAP), BF16)
    Xg_tm = sbuf(mos, "Xg_tm", (P, 6 * DM), BF16)
    Xg_fm = [sbuf(mos, f"Xg_fm{i}", (P, 8 * CAP), BF16) for i in range(2)]
    oe_fm = sbuf(mos, "oe_fm", (P, NDC * CRT))
    oe_tm = [sbuf(mos, f"oe_tm{i}", (P, DM)) for i in range(2)]
    v_sb = sbuf(mos, "v_sb", (P, CRT))
    z_t2 = sbuf(mos, "z_t2", (P, CRT))
    t1_t = sbuf(mos, "t1_t", (P, CRT))
    sv_sb = sbuf(mos, "sv_sb", (P, FF // P))
    s_sb = sbuf(mos, "s_sb", (P, FF // P))
    s1_sb = sbuf(mos, "s1_sb", (P, DM // P))
    oidx_sb = sbuf(mos, "oidx_sb", (P, 2 * T // P), mybir.dt.int32)
    sl_sb = [sbuf(mos, f"sl_sb{i}", (P, CAP // P), mybir.dt.int32)
             for i in range(2)]
    row_sb = [sbuf(mos, f"row_sb{i}", (P, CAP // P), mybir.dt.int32)
              for i in range(2)]

    pr.wait("q", lds)
    pr.wait("q", st); pr.wait("q", ld); pr.wait("q", ldp)
    qdma(oidx_sb[:, :], d["oint_d"][:].rearrange("(p c) -> p c", p=P),
         pin=True)
    oidx_ld = ld.c
    pr.wait("p", ld, oidx_ld)
    pr.v(lambda e: e.memset(Xg_tm[:, :], 0.0).then_inc(dv.h, 1)); dv.c += 1
    for c32 in range(2 * T // P):
        pr.p(lambda e, c32=c32: e.indirect_dma_start(
            d["slotlist_d"][:].unsqueeze(1),
            IOOA(ap=oidx_sb[:, c32:c32 + 1], axis=0),
            slotids[:, c32:c32 + 1], None,
            bounds_check=bnd_list, oob_is_err=False)
            .then_inc(ldp.h, 16)); ldp.c += 16
    pr.wait("p", ldp)
    pr.run_block("compact")
    scatters_ldp = ldp.c

    m1_banks_free = {b: 0 for b in range(8)}
    xg_banks = {4: 0, 5: 0}
    m2_evac_dv = 0
    tr_evac_dv = 0
    moe_scat_ldp = 0
    oe_scat_of = [0, 0]
    m2_done_pe = 0

    xg_tp_pe = [0]
    prep_scat_guard = [0]

    def emit_prep(e_i):
        par = e_i % 2
        pr.wait("q", ldp)
        qdma(sl_sb[par][:, :],
             d["slotlist_d"][e_i * CAP:(e_i + 1) * CAP]
             .rearrange("(p rc) -> p rc", p=P), pin=True)
        sl_ld = ld.c
        pr.wait("v", ld, sl_ld)
        pr.wait("v", ldp)
        pr.v(lambda e, par=par: e.tensor_single_scalar(
            row_sb[par][:, :], sl_sb[par][:, :], 1, ALU.arith_shift_right)
            .then_inc(dv.h, 1)); dv.c += 1
        pr.vd()
        pr.wait("p", dv)
        pr.wait("p", pe, xg_tp_pe[0])
        for rc in range(6):
            pr.p(lambda e, rc=rc, par=par: e.indirect_dma_start(
                Xg_tm[:, rc * DM:(rc + 1) * DM], None,
                d["xnb_tm_d"][:, :], IOOA(ap=row_sb[par][:, rc:rc + 1], axis=0),
                bounds_check=bnd_tok, oob_is_err=False)
                .then_inc(ldp.h, 16)); ldp.c += 16
        pr.wait("p", ldp)
        gath_ldp = ldp.c
        for rc in range(6):
            bank = 4 + (rc % 2)
            pr.wait("t", dv, xg_banks[bank])
            if rc == 0:
                pr.wait("t", ldp, gath_ldp)
                pr.wait("t", dv)
            for kc in range(NDC):
                if kc < NDC - 1:
                    pr.t(lambda e, rc=rc, kc=kc, bank=bank: e.transpose(
                        psum[bank][:, :].bitcast(BF16)[:, kc * P:(kc + 1) * P],
                        Xg_tm[:, rc * DM + kc * P:rc * DM + (kc + 1) * P],
                        ident_bf[:, :]))
                else:
                    pr.t(lambda e, rc=rc, kc=kc, bank=bank: e.transpose(
                        psum[bank][:, :].bitcast(BF16)[:, kc * P:(kc + 1) * P],
                        Xg_tm[:, rc * DM + kc * P:rc * DM + (kc + 1) * P],
                        ident_bf[:, :]).then_inc(pe.h, 1))
            pe.c += 1
            pr.wait("v", pe)
            pr.v(lambda e, rc=rc, bank=bank, par=par: e.tensor_copy(
                Xg_fm[par][:, :].rearrange("p (kc r) -> p kc r", kc=NDC)
                [:, :, rc * P:(rc + 1) * P],
                psum[bank][:, :].bitcast(BF16)
                .rearrange("p (kc r) -> p kc r", kc=NDC))
                .then_inc(dv.h, 1)); dv.c += 1
            pr.vd()
            xg_banks[bank] = dv.c
        xg_tp_pe[0] = pe.c
        return dv.c

    xg_ready_dv = [0, 0]
    xg_ready_dv[0] = emit_prep(0)
    pr.run_block("prep0")

    for e_i in range(NE):
        par = e_i % 2
        xg_done_dv = xg_ready_dv[par]
        pr.wait("q", dv)
        qdma(sv_sb[:, :], d["svT"][e_i, :, :])
        qdma(s_sb[:, :], d["sT"][e_i, :, :])
        qdma(s1_sb[:, :], d["s1T"][e_i, :, :], pin=True)
        sv_ld = ld.c
        pr.wait("v", ld, sv_ld)
        half_mm_pe = [m2_done_pe, m2_done_pe]
        wv_ldp_half = [0, 0]
        for fb in range(4):
            hid = fb % 2
            half = hid * 16 * DM
            pr.wait("p", pe, half_mm_pe[hid])
            pdma(wbuf[:, half:half + 8 * DM]
                 .rearrange("p (c f) -> p c f", c=NDC),
                 d["wv"][e_i].rearrange("(c p) f -> p c f", p=P)
                 [:, :, fb * DM:(fb + 1) * DM], sem=lw[hid])
            pdma(wbuf[:, half + 8 * DM:half + 16 * DM]
                 .rearrange("p (c f) -> p c f", c=NDC),
                 d["wmat"][e_i].rearrange("(c p) f -> p c f", p=P)
                 [:, :, fb * DM:(fb + 1) * DM], sem=lw[hid])
            wv_ldp_half[hid] = lw[hid].c
            for fc in range(8):
                fcG = fb * 8 + fc
                for rt in range(2):
                    bv = (2 * fc + rt) % 4
                    by = 4 + (2 * fc + rt) % 4
                    pr.wait("t", lw[hid], wv_ldp_half[hid])
                    pr.wait("t", dv, max(m1_banks_free[bv], m1_banks_free[by],
                                         xg_done_dv))
                    for kc in range(NDC):
                        pr.t(lambda e, bv=bv, fc=fc, kc=kc, rt=rt, half=half: e.matmul(
                            psum[bv][:, 0:CRT],
                            wbuf[:, half + kc * DM + fc * 128:
                                 half + kc * DM + (fc + 1) * 128],
                            Xg_fm[par][:, kc * CAP + rt * CRT:
                                       kc * CAP + (rt + 1) * CRT],
                            start=(kc == 0), stop=(kc == NDC - 1)))
                    for kc in range(NDC):
                        if kc < NDC - 1:
                            pr.t(lambda e, by=by, fc=fc, kc=kc, rt=rt, half=half: e.matmul(
                                psum[by][:, 0:CRT],
                                wbuf[:, half + 8 * DM + kc * DM + fc * 128:
                                     half + 8 * DM + kc * DM + (fc + 1) * 128],
                                Xg_fm[par][:, kc * CAP + rt * CRT:
                                           kc * CAP + (rt + 1) * CRT],
                                start=(kc == 0), stop=False))
                        else:
                            pr.t(lambda e, by=by, fc=fc, kc=kc, rt=rt, half=half: e.matmul(
                                psum[by][:, 0:CRT],
                                wbuf[:, half + 8 * DM + kc * DM + fc * 128:
                                     half + 8 * DM + kc * DM + (fc + 1) * 128],
                                Xg_fm[par][:, kc * CAP + rt * CRT:
                                           kc * CAP + (rt + 1) * CRT],
                                start=False, stop=True).then_inc(pe.h, 1))
                    pe.c += 1
                    pr.wait("v", pe)
                    pr.v(lambda e, bv=bv, fcG=fcG: e.tensor_single_scalar(
                        v_sb[:, :], psum[bv][:, 0:CRT], sv_sb[:, fcG:fcG + 1],
                        ALU.mult))
                    pr.vd()
                    pr.v(lambda e, by=by, fcG=fcG: e.tensor_single_scalar(
                        z_t2[:, :], psum[by][:, 0:CRT], s_sb[:, fcG:fcG + 1],
                        ALU.mult).then_inc(dv.h, 1)); dv.c += 1
                    pr.vd()
                    m1_banks_free[bv] = dv.c
                    m1_banks_free[by] = dv.c
                    pr.wait("s", dv)
                    pr.s(lambda e: e.activation(t1_t[:, :], z_t2[:, :],
                                                AF.Gelu_apprx_tanh)
                         .then_inc(sc.h, 1)); sc.c += 1
                    pr.wait("v", sc)
                    pr.v(lambda e, fcG=fcG, rt=rt: e.tensor_tensor(
                        h_e[:, fcG * CAP + rt * CRT:fcG * CAP + (rt + 1) * CRT],
                        v_sb[:, :], t1_t[:, :], ALU.mult)
                        .then_inc(dv.h, 1)); dv.c += 1
                    pr.vd()
            half_mm_pe[hid] = pe.c
            if fb == 2:
                pr.wait("p", pe, half_mm_pe[0])
                pdma(wbuf[:, 0:16 * DM].rearrange("p (c f) -> p c f", c=16),
                     d["w1"][e_i][0:2048, :].rearrange("(c p) f -> p c f", p=P),
                     sem=lwp)
            if fb == 3:
                pr.wait("p", pe, half_mm_pe[1])
                pdma(wbuf[:, 16 * DM:32 * DM]
                     .rearrange("p (c f) -> p c f", c=16),
                     d["w1"][e_i][2048:4096, :].rearrange("(c p) f -> p c f", p=P),
                     sem=lwp)
        h_done_dv = dv.c
        w1_ldp = lwp.c

        if e_i + 1 < NE:
            xg_ready_dv[(e_i + 1) % 2] = emit_prep(e_i + 1)

        for rt in range(2):
            pr.wait("t", lwp, w1_ldp)
            pr.wait("t", dv, max(h_done_dv, m2_evac_dv, tr_evac_dv,
                                 xg_banks[4], xg_banks[5]))
            for dm in range(NDC):
                for kc in range(32):
                    last = (dm == NDC - 1 and kc == 31)
                    if last:
                        pr.t(lambda e, dm=dm, kc=kc, rt=rt: e.matmul(
                            psum[dm][:, 0:CRT],
                            wbuf[:, kc * DM + dm * 128:kc * DM + (dm + 1) * 128],
                            h_e[:, kc * CAP + rt * CRT:kc * CAP + (rt + 1) * CRT],
                            start=(kc == 0), stop=(kc == 31)).then_inc(pe.h, 1))
                    else:
                        pr.t(lambda e, dm=dm, kc=kc, rt=rt: e.matmul(
                            psum[dm][:, 0:CRT],
                            wbuf[:, kc * DM + dm * 128:kc * DM + (dm + 1) * 128],
                            h_e[:, kc * CAP + rt * CRT:kc * CAP + (rt + 1) * CRT],
                            start=(kc == 0), stop=(kc == 31)))
            pe.c += 1
            pr.wait("v", pe)
            for dm in range(NDC):
                if dm < NDC - 1:
                    pr.v(lambda e, dm=dm: e.tensor_single_scalar(
                        oe_fm[:, dm * CRT:(dm + 1) * CRT], psum[dm][:, 0:CRT],
                        s1_sb[:, dm:dm + 1], ALU.mult))
                else:
                    pr.v(lambda e, dm=dm: e.tensor_single_scalar(
                        oe_fm[:, dm * CRT:(dm + 1) * CRT], psum[dm][:, 0:CRT],
                        s1_sb[:, dm:dm + 1], ALU.mult).then_inc(dv.h, 1))
                pr.vd()
            dv.c += 1
            m2_evac_dv = dv.c
            for rsub in range(3):
                rc = rt * 3 + rsub
                opar = rc % 2
                pr.wait("t", dv, max(m2_evac_dv, tr_evac_dv))
                for dm in range(NDC):
                    bk = dm // 4
                    if dm < NDC - 1:
                        pr.t(lambda e, dm=dm, rsub=rsub, bk=bk: e.transpose(
                            psum[bk][:, (dm % 4) * 128:(dm % 4 + 1) * 128],
                            oe_fm[:, dm * CRT + rsub * 128:dm * CRT + (rsub + 1) * 128],
                            ident[:, :]))
                    else:
                        pr.t(lambda e, dm=dm, rsub=rsub, bk=bk: e.transpose(
                            psum[bk][:, (dm % 4) * 128:(dm % 4 + 1) * 128],
                            oe_fm[:, dm * CRT + rsub * 128:dm * CRT + (rsub + 1) * 128],
                            ident[:, :]).then_inc(pe.h, 1))
                pe.c += 1
                pr.wait("v", pe)
                pr.wait("v", ldp, oe_scat_of[opar])
                pr.v(lambda e, opar=opar: e.tensor_copy(
                    oe_tm[opar][:, 0:512], psum[0][:, :]))
                pr.vd()
                pr.v(lambda e, opar=opar: e.tensor_copy(
                    oe_tm[opar][:, 512:1024], psum[1][:, :])
                     .then_inc(dv.h, 1)); dv.c += 1
                pr.vd()
                tr_evac_dv = dv.c
                pr.wait("p", dv, tr_evac_dv)
                pr.p(lambda e, rc=rc, par=par, opar=opar: e.indirect_dma_start(
                    d["pairs_d"][:, :],
                    IOOA(ap=sl_sb[par][:, rc:rc + 1], axis=0),
                    oe_tm[opar][:, :], None,
                    bounds_check=bnd_slot, oob_is_err=False)
                    .then_inc(ldp.h, 16)); ldp.c += 16
                moe_scat_ldp = ldp.c
                oe_scat_of[opar] = ldp.c
        pr.wait("p", ldp)
        for b in range(8):
            m1_banks_free[b] = tr_evac_dv
        xg_banks[4] = tr_evac_dv
        xg_banks[5] = tr_evac_dv
        m2_done_pe = pe.c

        pr.run_block(f"moe{e_i}")
    moe_scats_done = ldp.c
    mos.close()

    fin = ExitStack()
    pr_tile = [sbuf(fin, f"pr_tile{i}", (P, 2 * DM)) for i in range(2)]
    res_t = [sbuf(fin, f"res_t{i}", (P, DM)) for i in range(2)]
    fstage = [sbuf(fin, f"fstage{i}", (P, DM)) for i in range(2)]
    fin_banks = {2: 0, 3: 0}
    tp_pe_of = [0, 0]
    fadd_of = [0, 0]
    pr.wait("q", ldp, moe_scats_done)
    pr.wait("q", st); pr.wait("q", ld)
    for ti in range(16):
        fp_ = ti % 2
        pr.wait("q", pe, tp_pe_of[fp_])
        qdma(pr_tile[fp_][:, :],
             d["pairs_d"][2 * ti * P:2 * (ti + 1) * P, :]
             .rearrange("(p two) m -> p two m", p=P), pin=True)
        pt_ld = ld.c
        pr.wait("v", ld, pt_ld)
        pr.v(lambda e, ti=ti, fp_=fp_: e.tensor_single_scalar(
            res_t[fp_][:, :], pr_tile[fp_][:, DM:2 * DM],
            g01_all[:, 2 * ti + 1:2 * ti + 2], ALU.mult))
        pr.vd()
        pr.v(lambda e, ti=ti, fp_=fp_: e.scalar_tensor_tensor(
            res_t[fp_][:, :], pr_tile[fp_][:, 0:DM],
            g01_all[:, 2 * ti:2 * ti + 1],
            res_t[fp_][:, :], ALU.mult, ALU.add).then_inc(dv.h, 1)); dv.c += 1
        pr.vd()
        bank = 2 + (ti % 2)
        pr.wait("t", dv)
        pr.wait("t", dv, fin_banks[bank])
        for c in range(NDC):
            bk2 = bank
            col = (c % 4) * 128
            if c == 3 or c == NDC - 1:
                pr.t(lambda e, c=c, col=col, bk2=bk2, fp_=fp_: e.transpose(
                    psum[bk2][:, col:col + 128] if (c // 4) == 0 else
                    psum[bk2 + 2][:, col:col + 128],
                    res_t[fp_][:, c * P:(c + 1) * P], ident[:, :])
                    .then_inc(pe.h, 1))
                pe.c += 1
            else:
                pr.t(lambda e, c=c, col=col, bk2=bk2, fp_=fp_: e.transpose(
                    psum[bk2][:, col:col + 128] if (c // 4) == 0 else
                    psum[bk2 + 2][:, col:col + 128],
                    res_t[fp_][:, c * P:(c + 1) * P], ident[:, :]))
        tp_pe_of[fp_] = pe.c
        pr.wait("v", pe)
        pr.wait("v", ldp, fadd_of[fp_])
        pr.v(lambda e, bank=bank, fp_=fp_: e.tensor_copy(
            fstage[fp_][:, 0:512], psum[bank][:, :]))
        pr.vd()
        pr.v(lambda e, bank=bank, fp_=fp_: e.tensor_copy(
            fstage[fp_][:, 512:1024], psum[bank + 2][:, :]).then_inc(dv.h, 1))
        dv.c += 1
        pr.vd()
        fin_banks[bank] = dv.c
        pr.wait("p", dv)
        if ti == 0:
            pr.wait("p", ldp, moe_scats_done)
        pdma(out_fm[:, ti * P:(ti + 1) * P].rearrange("(c p) t -> p c t", p=P),
             fstage[fp_][:, :].rearrange("p (c t) -> p c t", c=NDC),
             accum=ALU.add)
        fadd_of[fp_] = ldp.c
    pr.wait("p", ldp)
    pr.run_block("fincomb")
    fin.close()

    pr.wait("q", st)
    pr.wait("q", ld)
    pr.wait("p", ldp)
    pr.wait("s", lds)
    pr.run_block("final")

    mos.close()
    es.close()
    return nc


def kernel(**inputs):
    x = np.asarray(inputs["L0_attn_x"], np.float32)
    W_in = np.ascontiguousarray(inputs["L0_attn_W_in"], np.float32)
    W_x = np.ascontiguousarray(inputs["L0_attn_W_x"], np.float32)
    W_dt = np.ascontiguousarray(inputs["L0_attn_W_dt"], np.float32)
    W_out = np.ascontiguousarray(inputs["L0_attn_W_out"], np.float32)
    router_w = np.ascontiguousarray(inputs["L0_ffn_router_w"], np.float32)
    wv = np.ascontiguousarray(inputs["L0_ffn_wv"], np.float32)
    wmat = np.ascontiguousarray(inputs["L0_ffn_w"], np.float32)
    w1 = np.ascontiguousarray(inputs["L0_ffn_w1"], np.float32)

    def colmajor(vec, nch):
        return np.ascontiguousarray(
            np.asarray(vec, np.float32).reshape(nch, P).T)

    svT = np.ascontiguousarray(
        np.asarray(inputs["L0_ffn_sv"], np.float32).reshape(NE, FF // P, P)
        .transpose(0, 2, 1))
    sT = np.ascontiguousarray(
        np.asarray(inputs["L0_ffn_s"], np.float32).reshape(NE, FF // P, P)
        .transpose(0, 2, 1))
    s1T = np.ascontiguousarray(
        np.asarray(inputs["L0_ffn_s1"], np.float32).reshape(NE, DM // P, P)
        .transpose(0, 2, 1))

    common = {
        "W_in": W_in, "W_x": W_x, "W_dt": W_dt, "W_out": W_out,
        "router_w": router_w, "wv": wv, "wmat": wmat, "w1": w1,
        "svT": svT, "sT": sT, "s1T": s1T,
        "gamma_c": colmajor(inputs["L0_attn_ln_gamma"], NDC),
        "beta_c": colmajor(inputs["L0_attn_ln_beta"], NDC),
        "b_in_c": colmajor(inputs["L0_attn_b_in"], 2 * DI // P),
        "b_dt_c": colmajor(inputs["L0_attn_b_dt"], NIC),
        "conv_b_c": colmajor(inputs["L0_attn_conv_b"], NIC),
        "D_c": colmajor(inputs["L0_attn_D"], NIC),
        "b_out_c": colmajor(inputs["L0_attn_b_out"], NDC),
        "convw_c": np.ascontiguousarray(
            np.asarray(inputs["L0_attn_conv_w"], np.float32).T
            .reshape(NIC, P, 4)),
        "A_log_c": np.ascontiguousarray(
            np.asarray(inputs["L0_attn_A_log"], np.float32)
            .reshape(NIC, P, NST)),
        "ones_sc": np.full((P, P), 1.0 / DM, np.float32),
        "ident": np.eye(P, dtype=np.float32),
        "ident_bf": np.eye(P, dtype=ml_dtypes.bfloat16),
        "sel8": np.stack([np.ones(NE, np.float32),
                          np.arange(NE, dtype=np.float32) * CAP],
                         axis=1),
        "slot_ids": np.arange(2 * T, dtype=np.int32).reshape(P, 2 * T // P),
        "oob_init": np.full(NE * CAP, 1 << 20, np.int32),
    }

    in_maps = []
    for c in range(NCORES):
        m = dict(common)
        m["xT"] = np.ascontiguousarray(x[c].T)
        in_maps.append(m)

    nc = build(debug=False)
    res = run_bass_kernel_spmd(nc, in_maps, core_ids=list(range(NCORES)))
    out = np.stack([np.asarray(r["out_fm"]).T for r in res.results])
    return np.ascontiguousarray(out.astype(np.float32))
